# version 21
# baseline (speedup 1.0000x reference)
import contextlib
import numpy as np
import concourse.bass as bass
import concourse.mybir as mybir
from concourse.bass_utils import run_bass_kernel_spmd

F32 = mybir.dt.float32
BF16 = mybir.dt.bfloat16
AF = mybir.ActivationFunctionType
ALU = mybir.AluOpType

T = 4096
D = 1024
W = 512
NCH = T // W
NT = D // 128
DI = 2048
NTI = DI // 128
NS = 16
GW = 1024
NG = T // GW
LN_EPS = 1e-5

_ENG = {"pe": "tensor", "dv": "vector", "ac": "scalar", "gp": "gpsimd",
        "sy": "sync"}


class Ctx:

    def __init__(self, nc):
        self.nc = nc
        self.block = None
        self.sems = {}

    def init_sems(self, stack):
        for name in ("pe", "dv", "ac", "gp", "sy"):
            h = stack.enter_context(self.nc.semaphore(f"sem_{name}"))
            self.sems[name] = [h, 0]
        self.lanes = {}
        self.lane_rr = {}
        for q, nl in (("sd", 8), ("gd", 6), ("ad", 4)):
            self.lanes[q] = []
            self.lane_rr[q] = 0
            for i in range(nl):
                h = stack.enter_context(self.nc.semaphore(f"lane_{q}{i}"))
                self.lanes[q].append([h, 0])

    def snap(self, *names):
        out = []
        for n in names:
            if n in self.sems:
                out.append((self.sems[n][0], self.sems[n][1]))
            else:
                out += [(h, v) for h, v in self.lanes[n]]
        return out

    def _waits(self, eng, deps):
        merged = {}
        for s, v in deps:
            if v > 0 and merged.get(s, 0) < v:
                merged[s] = v
        for s, v in merged.items():
            eng.wait_ge(s, v)

    def st(self, sel, deps, emit):
        sem, _ = self.sems[sel]

        def f(eng):
            self._waits(eng, deps)
            last = emit(eng)
            if last is not None:
                last.then_inc(sem, 1)
        getattr(self.block, _ENG[sel])(f)
        self.sems[sel][1] += 1
        return self.snap(sel)

    def dma(self, sel, deps, pairs):
        q = {"sy": "sd", "gp": "gd", "ac": "ad"}[sel]
        lanes = self.lanes[q]
        plan = []
        for _ in pairs:
            li = self.lane_rr[q] % len(lanes)
            self.lane_rr[q] += 1
            lane = lanes[li]
            plan.append((lane, lane[1], lane[1] + 16))
            lane[1] += 16
        out_snap = [(lane[0], after) for lane, _, after in plan]

        def f(eng):
            self._waits(eng, deps)
            for (lane, before, _), (out, in_) in zip(plan, pairs):
                if before > 0:
                    eng.wait_ge(lane[0], before)
                eng.dma_start(out=out, in_=in_).then_inc(lane[0], 16)
        getattr(self.block, _ENG[sel])(f)
        return out_snap


def emit_ln(C, src_sl, xsq, st_sum, st_sq, rep_a, rep_b, rows, tls, dep, *,
            rms=False, dst_bf_sl=None, dst_f32_sl=None, gb=None, n_k=NT):
    inv_n = 1.0 / (n_k * 128)
    ones1, onesr = C.ones1, C.onesr
    sum_row = rows[0:1, 0:W]
    sq_row = rows[0:1, W:2 * W]
    a_row = rows[0:1, 2 * W:3 * W]
    b_row = rows[0:1, 3 * W:4 * W]

    def sq(e):
        last = None
        for k in range(n_k):
            last = e.activation(xsq[:, k * W:(k + 1) * W], src_sl(k),
                                AF.Square)
        return last
    s_sq = C.st("ac", dep, sq)

    def mm_stats(e):
        last = None
        if not rms:
            for k in range(n_k):
                last = e.matmul(st_sum, ones1[:, 0:1], src_sl(k),
                                start=(k == 0), stop=(k == n_k - 1))
        for k in range(n_k):
            last = e.matmul(st_sq, ones1[:, 0:1],
                            xsq[:, k * W:(k + 1) * W],
                            start=(k == 0), stop=(k == n_k - 1))
        return last
    s_mm = C.st("pe", dep + s_sq, mm_stats)

    if rms:
        s = C.st("dv", s_mm, lambda e: e.tensor_copy(sq_row, st_sq))
        s = C.st("dv", s, lambda e: e.tensor_scalar_mul(sq_row, sq_row,
                                                        inv_n))
        s = C.st("ac", s, lambda e: e.activation(a_row, sq_row, AF.Ln,
                                                 bias=C.eps1))
        s = C.st("ac", s, lambda e: e.activation(a_row, a_row, AF.Exp,
                                                 scale=C.mhalf1))
        s_rep = C.st("pe", s, lambda e: e.matmul(rep_a, onesr[0:1, :], a_row,
                                                 start=True, stop=True))
        done = []
        for k in range(n_k):
            done += C.st("dv", s_rep, lambda e, k=k: e.tensor_mul(
                dst_f32_sl(k), src_sl(k), rep_a))
        return done, s

    def cps(e):
        e.tensor_copy(sum_row, st_sum)
        return e.tensor_copy(sq_row, st_sq)
    s = C.st("dv", s_mm, cps)

    def mumath(e):
        e.tensor_scalar_mul(sum_row, sum_row, inv_n)
        return e.tensor_scalar_mul(sq_row, sq_row, inv_n)
    s = C.st("dv", s, mumath)
    s = C.st("dv", s, lambda e: e.tensor_mul(b_row, sum_row, sum_row))
    s = C.st("dv", s, lambda e: e.tensor_sub(sq_row, sq_row, b_row))
    s = C.st("ac", s, lambda e: e.activation(a_row, sq_row, AF.Ln,
                                             bias=C.eps1))
    s = C.st("ac", s, lambda e: e.activation(a_row, a_row, AF.Exp,
                                             scale=C.mhalf1))
    s = C.st("dv", s, lambda e: e.tensor_mul(b_row, sum_row, a_row))
    s = C.st("dv", s, lambda e: e.tensor_scalar_mul(b_row, b_row, -1.0))

    def mmrep(e):
        e.matmul(rep_a, onesr[0:1, :], a_row, start=True, stop=True)
        return e.matmul(rep_b, onesr[0:1, :], b_row, start=True, stop=True)
    s_rep = C.st("pe", s, mmrep)

    done = []
    for k in range(n_k):
        s_t = C.st("dv", s_rep, lambda e, k=k: e.tensor_mul(
            tls[:, 0:W], src_sl(k), rep_a))
        if dst_f32_sl is not None:
            s_x = C.st("dv", s_t, lambda e, k=k: e.tensor_add(
                dst_f32_sl(k), tls[:, 0:W], rep_b))
            if dst_bf_sl is not None:
                g, b = gb
                s_x = C.st("dv", s_x, lambda e, k=k: e.tensor_scalar(
                    dst_bf_sl(k), dst_f32_sl(k), g(k), b(k), ALU.mult,
                    ALU.add))
        else:
            s_x = C.st("dv", s_t, lambda e, k=k: e.tensor_add(
                dst_bf_sl(k), tls[:, 0:W], rep_b))
        done += s_x
    return done, s_rep


def emit_conv(C, raw_sl, acc_sl, tail, wcol, dep, n_k):
    def tap3(e):
        last = None
        for k in range(n_k):
            last = e.tensor_scalar_mul(acc_sl(k), raw_sl(k), wcol(k, 3))
        return last
    s = C.st("dv", dep, tap3)
    for j in (2, 1, 0):
        sft = 3 - j

        def tapj(e, j=j, sft=sft):
            last = None
            for k in range(n_k):
                acc = acc_sl(k)
                raw = raw_sl(k)
                last = e.scalar_tensor_tensor(
                    acc[:, sft:W], raw[:, 0:W - sft], wcol(k, j),
                    acc[:, sft:W], ALU.mult, ALU.add)
                last = e.scalar_tensor_tensor(
                    acc[:, 0:sft], tail[:, k * 3 + 3 - sft:k * 3 + 3],
                    wcol(k, j), acc[:, 0:sft], ALU.mult, ALU.add)
            return last
        s = C.st("dv", s, tapj)

    def upd(e):
        last = None
        for k in range(n_k):
            last = e.tensor_copy(tail[:, k * 3:(k + 1) * 3],
                                 raw_sl(k)[:, W - 3:W])
        return last
    s_tail = C.st("dv", s, upd)
    return s, s_tail


def _r3(ap, p=128):
    return ap.rearrange("(k p) t -> p k t", p=p)


def _emit_ffn(C, nc, ST, xsq, mk, ruT, wcb, xf_sc, y, identf,
              dbg_dst=None, tag="", wdep0=()):
    with contextlib.ExitStack() as PS:
        sb = lambda *a: PS.enter_context(nc.sbuf_tensor(*a))
        pst = lambda *a: PS.enter_context(nc.psum_tensor(*a))
        mkb = sb("mkb" + tag, [128, 32 * 128], BF16)
        rub = sb("rub" + tag, [128, 64 * 128], BF16)

        with contextlib.ExitStack() as QS:
            mkf = QS.enter_context(nc.sbuf_tensor("mkf" + tag, [128, 32 * 128],
                                                  F32))
            ruf = QS.enter_context(nc.sbuf_tensor("ruf" + tag, [128, 64 * 128],
                                                  F32))
            with nc.Block(no_gpsimd_drain=True) as blk:
                C.block = blk

                def zz(e):
                    e.memset(mkf[:, :], 0.0)
                    return e.memset(ruf[:, :], 0.0)
                s_z = C.st("gp", [], zz)
                pairs_sy = []
                pairs_ac = []
                for hd in range(4):
                    for rg in range(8):
                        off = (hd * 8 + rg) * 128
                        for rp in range(8):
                            pairs_sy.append((
                                mkf[rp * 16:(rp + 1) * 16,
                                    off + rp * 16:off + (rp + 1) * 16],
                                mk[hd, rg * 8 + rp]))
                    for m in range(32):
                        slot = 2 * (hd * 8 + m // 4) + (m % 2)
                        off = slot * 128
                        pbase = 64 * ((m // 2) % 2)
                        for c in range(2):
                            rail = 4 * (m // 2) + 2 * (m % 2) + c
                            rr = 2 * (m % 2) + c
                            pairs_ac.append((
                                ruf[pbase + rr * 16:pbase + (rr + 1) * 16,
                                    off + c * 64:off + (c + 1) * 64],
                                ruT[hd, rail]))
                half = len(pairs_ac) // 2
                s_d1 = C.dma("sy", s_z, pairs_sy + pairs_ac[:half])
                s_d2 = C.dma("ac", s_z, pairs_ac[half:])

                def cvt(e):
                    e.tensor_copy(mkb[:, :], mkf[:, :])
                    return e.tensor_copy(rub[:, :], ruf[:, :])
                C.st("dv", s_d1 + s_d2, cvt)

        rows = sb("rows_f" + tag, [1, 4 * W], F32)
        tls = sb("tls_f" + tag, [128, W], F32)
        xnb = sb("xnb_f" + tag, [128, NT * GW], BF16)
        ybt = [sb("ybt0" + tag, [128, GW], BF16), sb("ybt1" + tag, [128, GW], BF16)]
        hh = sb("hh" + tag, [128, 32 * GW], BF16)
        wcf = [sb("wcf0" + tag, [128, 32 * 128], BF16),
               sb("wcf1" + tag, [128, 32 * 128], BF16)]
        oac = sb("oac" + tag, [128, NT * GW], F32)
        xgs = sb("xgs" + tag, [128, NT * W], F32)
        zev = sb("zev" + tag, [128, GW], F32)
        rsx = zev
        ytok = zev

        big = [pst(f"fps{i}" + tag, [128, 1024], F32).ap()
               for i in range(4)]
        st_sum = big[0][0:1, 0:W]
        st_sq = big[0][0:1, W:2 * W]
        rep_a = big[1][:, 0:W]
        rep_b = big[1][:, W:2 * W]
        ps_y = big[2]
        ps_zs2 = [big[0], big[1], big[3]]
        ps_cs = [big[3], big[2]]
        ps_tr = big[1]

        ffn_tail = []
        last_maker = []
        s_ye_prev = [[], []]
        ec_prev = [[], []]
        wc_prev = [list(wdep0), list(wdep0)]
        s_gel_prev2 = [[], []]
        prev_half = []
        with nc.Block(no_gpsimd_drain=True) as blk:
            C.block = blk
            for g in range(NG):
                s_gel_prev = s_gel_prev + ffn_tail if g else []
                s_ln_all = []
                for hf in range(2):
                    c0 = g * GW + hf * W
                    s_x = C.dma("sy", prev_half + last_maker, [
                        (xgs[:, :].rearrange("p (k t) -> p k t", t=W),
                         _r3(xf_sc)[:, :, c0:c0 + W])])
                    psdep = list(gel_last) if g else []
                    s_ln, s_lrep = emit_ln(
                        C, lambda k: xgs[:, k * W:(k + 1) * W], xsq,
                        st_sum, st_sq, rep_a, rep_b, rows, tls,
                        s_x + psdep,
                        dst_bf_sl=lambda k: xnb[:, k * GW + hf * W:
                                                k * GW + hf * W + W])
                    s_ln_all += s_ln
                    prev_half = s_ln + s_lrep
                gel_last = []
                gelq = []
                s_gel_prev = s_gel_prev if g else []
                s_gel_prev2 = [s_ln_all + s_gel_prev2[0],
                               s_ln_all + s_gel_prev2[1],
                               s_ln_all + (s_gel_prev2[2]
                                           if len(s_gel_prev2) > 2 else
                                           ec_prev[0])]
                for hd in range(4):
                    for rg in range(8):
                        mti = hd * 8 + rg
                        pyb = rg % 2

                        def mmy(e, mti=mti, rg=rg):
                            off = mti * 128
                            last = None
                            for h2 in range(2):
                                last = e.matmul(
                                    ps_y[:, h2 * W:(h2 + 1) * W],
                                    mkb[:, off:off + 128],
                                    xnb[:, rg * GW + h2 * W:
                                        rg * GW + (h2 + 1) * W],
                                    start=True, stop=True)
                            return last
                        s_my = C.st("pe", s_ln_all + s_ye_prev[pyb]
                                    + ec_prev[0] + ec_prev[1], mmy)
                        last_maker = s_my
                        s_ye = C.st("dv", s_my, lambda e, pyb=pyb:
                                    e.tensor_copy(ybt[pyb][:, :], ps_y))
                        s_ye_prev[pyb] = s_ye
                        for mq in range(4):
                            m = rg * 4 + mq
                            q = m // 2
                            rbase = (q % 2) * 64
                            zpar = (hd * 32 + m) % 3

                            def mmz(e, m=m, hd=hd, rbase=rbase, pyb=pyb,
                                    zpar=zpar):
                                slot = 2 * (hd * 8 + m // 4) + (m % 2)
                                off = slot * 128
                                last = None
                                for h2 in range(2):
                                    last = e.matmul(
                                        ps_zs2[zpar][:, h2 * W:(h2 + 1) * W],
                                        rub[rbase:rbase + 64,
                                            off:off + 128],
                                        ybt[pyb][rbase:rbase + 64,
                                                 h2 * W:(h2 + 1) * W],
                                        start=True, stop=True)
                                return last
                            s_mz = C.st("pe", s_ye + s_gel_prev2[zpar], mmz)
                            gelq.append((m, zpar, s_mz))
                            if len(gelq) == 3 or m == 31:
                                batch = list(gelq)
                                gelq.clear()

                                def gtri(e, batch=batch):
                                    last = None
                                    for (gm, gz, _) in batch:
                                        last = e.activation(
                                            hh[:, gm * GW:(gm + 1) * GW],
                                            ps_zs2[gz], AF.Gelu)
                                    return last
                                gdep = []
                                for (_, _, d) in batch:
                                    gdep += d
                                s_gel = C.st("ac", gdep, gtri)
                                s_gel_prev = s_gel
                                for (_, gz, _) in batch:
                                    s_gel_prev2[gz] = (s_gel_prev2[gz]
                                                       + s_gel)
                    gel_last = s_gel_prev
                    for et in range(NT):
                        par = et % 2
                        s_w = C.dma("sy", wc_prev[par], [
                            (wcf[par][:, :].rearrange(
                                "p (m e) -> p m e", e=128),
                             wcb[hd * 4096:(hd + 1) * 4096,
                                 et * 128:(et + 1) * 128].rearrange(
                                     "(m p) e -> p m e", p=128))])

                        def mmc(e, par=par):
                            last = None
                            for m in range(32):
                                for h2 in range(2):
                                    last = e.matmul(
                                        ps_cs[par][:, h2 * W:(h2 + 1) * W],
                                        wcf[par][:, m * 128:(m + 1) * 128],
                                        hh[:, m * GW + h2 * W:
                                           m * GW + (h2 + 1) * W],
                                        start=(m == 0), stop=(m == 31))
                            return last
                        dep_mc = gel_last + s_w + ec_prev[par]
                        if par == 0:
                            dep_mc = dep_mc + s_gel_prev2[2]
                        s_mc = C.st("pe", dep_mc, mmc)
                        wc_prev[par] = s_mc
                        if hd == 0:
                            s_ec = C.st("dv", s_mc, lambda e, et=et, par=par:
                                        e.tensor_copy(
                                            oac[:, et * GW:(et + 1) * GW],
                                            ps_cs[par]))
                        else:
                            s_ec = C.st("dv", s_mc, lambda e, et=et, par=par:
                                        e.tensor_add(
                                            oac[:, et * GW:(et + 1) * GW],
                                            oac[:, et * GW:(et + 1) * GW],
                                            ps_cs[par]))
                        ec_prev[par] = s_ec
                    s_gel_prev2[2] = s_gel_prev2[2] + ec_prev[0]
                s_tr_prev = [[], []]
                for et in range(NT):
                    s_xe = C.dma("sy", ec_prev[0] + ec_prev[1] + ffn_tail, [
                        (rsx[:, :],
                         xf_sc[et * 128:(et + 1) * 128,
                               g * GW:(g + 1) * GW])])
                    s_r = C.st("dv", s_xe + ec_prev[et % 2], lambda e, et=et:
                               e.tensor_add(oac[:, et * GW:(et + 1) * GW],
                                            oac[:, et * GW:(et + 1) * GW],
                                            rsx[:, :]))
                    if y is None:
                        s_so = C.dma("sy", s_r, [
                            (xf_sc[et * 128:(et + 1) * 128,
                                   g * GW:(g + 1) * GW],
                             oac[:, et * GW:(et + 1) * GW])])
                        if dbg_dst is not None:
                            s_so = s_so + C.dma("sy", s_r, [
                                (dbg_dst[et * 128:(et + 1) * 128,
                                         g * GW:(g + 1) * GW],
                                 oac[:, et * GW:(et + 1) * GW])])
                        ec_prev[et % 2] = s_r + s_so
                        ffn_tail = s_r + s_so
                    else:
                        ec_prev[et % 2] = s_r
                if y is not None:
                    res_done = ec_prev[0] + ec_prev[1]
                    s_yst = []
                    for sv in range(8):
                        s_cp_all = []
                        for et in range(NT):
                            par = et % 2
                            tsl2 = slice(par * 512, par * 512 + 128)

                            def tr(e, et=et, sv=sv, tsl2=tsl2):
                                return e.transpose(
                                    ps_tr[:, tsl2],
                                    oac[:, et * GW + sv * 128:
                                        et * GW + (sv + 1) * 128],
                                    identf[:, :])
                            s_t = C.st("pe", res_done + s_tr_prev[par], tr)
                            s_cp = C.st("dv", s_t + s_yst,
                                        lambda e, et=et, tsl2=tsl2:
                                        e.tensor_copy(
                                            ytok[:, et * 128:(et + 1) * 128],
                                            ps_tr[:, tsl2]))
                            s_tr_prev[par] = s_cp
                            s_cp_all += s_cp
                        s_yst = C.dma("sy", s_cp_all, [
                            (y[g * GW + sv * 128:g * GW + (sv + 1) * 128, :],
                             ytok[:, :])])
                    ffn_tail = s_yst + s_tr_prev[0] + s_tr_prev[1]
        with nc.Block(no_gpsimd_drain=True) as blk:
            C.block = blk
            C.st("sy", C.snap("sd") + C.snap("ad"), lambda e: None)


def build_program(dbg=(), phases=("p1", "p2", "p3", "p4", "p5")):
    nc = bass.Bass("TRN2", target_bir_lowering=False)
    C = Ctx(nc)

    xf = nc.dram_tensor("xf", [D, T], F32, kind="ExternalInput").ap()
    wi = nc.dram_tensor("wi", [D, D], F32, kind="ExternalInput").ap()
    wf_ = nc.dram_tensor("wf", [D, D], F32, kind="ExternalInput").ap()
    wg = nc.dram_tensor("wg", [D, D], F32, kind="ExternalInput").ap()
    wo = nc.dram_tensor("wo", [D, D], F32, kind="ExternalInput").ap()
    cvi = nc.dram_tensor("cvi", [128, NT * 4], F32,
                         kind="ExternalInput").ap()
    cvf = nc.dram_tensor("cvf", [128, NT * 4], F32,
                         kind="ExternalInput").ap()
    gnw = nc.dram_tensor("gnw", [128, NT], F32, kind="ExternalInput").ap()
    lng = nc.dram_tensor("lng", [128, NT], F32, kind="ExternalInput").ap()
    lnb = nc.dram_tensor("lnb", [128, NT], F32, kind="ExternalInput").ap()
    winT = nc.dram_tensor("winT", [D, 2 * DI], F32,
                          kind="ExternalInput").ap()
    cw = nc.dram_tensor("cw", [128, NTI * 4], F32,
                        kind="ExternalInput").ap()
    cb = nc.dram_tensor("cb", [128, NTI], F32, kind="ExternalInput").ap()
    xpT = nc.dram_tensor("xpT", [DI, 96], F32, kind="ExternalInput").ap()
    dtpT = nc.dram_tensor("dtpT", [64, DI], F32, kind="ExternalInput").ap()
    dtb = nc.dram_tensor("dtb", [128, NTI], F32, kind="ExternalInput").ap()
    alog = nc.dram_tensor("alog", [128, NTI * NS], F32,
                          kind="ExternalInput").ap()
    dd = nc.dram_tensor("dd", [128, NTI], F32, kind="ExternalInput").ap()
    woutT = nc.dram_tensor("woutT", [DI, D], F32, kind="ExternalInput").ap()
    mk = nc.dram_tensor("mk", [4, 64, 16, 16], F32,
                        kind="ExternalInput").ap()
    ruT = nc.dram_tensor("ruT", [4, 64, 16, 64], F32,
                         kind="ExternalInput").ap()
    wcT = nc.dram_tensor("wcT", [16 * D, D], F32, kind="ExternalInput").ap()

    y = nc.dram_tensor("y", [T, D], F32, kind="ExternalOutput").ap()
    dbg_t = {}
    for name in dbg:
        shp = {"x1": [D, T], "x2": [D, T], "x3": [D, T], "xl": [D, T],
               "u": [DI, T], "z": [DI, T], "dtv": [DI, T],
               "xdbl": [128, T]}[name]
        dbg_t[name] = nc.dram_tensor("dbg_" + name, shp, F32,
                                     kind="ExternalOutput").ap()

    xf_sc = nc.dram_tensor("xf_sc", [D, T], F32, kind="Internal").ap()
    xl_sc = nc.dram_tensor("xl_sc", [D, T], F32, kind="Internal").ap()
    u_sc = nc.dram_tensor("u_sc", [DI, T], F32, kind="Internal").ap()
    z_sc = nc.dram_tensor("z_sc", [DI, T], F32, kind="Internal").ap()
    wcb = nc.dram_tensor("wcb", [16 * D, D], BF16, kind="Internal").ap()
    wob_sc = nc.dram_tensor("wob_sc", [DI, D], BF16, kind="Internal").ap()
    zs_sc = nc.dram_tensor("zs_sc", [DI, W], F32, kind="Internal").ap()
    xdbl_sc = nc.dram_tensor("xdbl_sc", [96, T], F32, kind="Internal").ap()

    with contextlib.ExitStack() as ST:
        C.init_sems(ST)
        sb = lambda *a: ST.enter_context(nc.sbuf_tensor(*a))

        ones1 = sb("ones1", [128, 1], F32)
        onesr = sb("onesr", [1, 128], F32)
        cst = sb("cst", [128, 2], F32)
        C.ones1, C.onesr = ones1, onesr
        C.eps1 = cst[0:1, 0:1]
        C.mhalf1 = cst[0:1, 1:2]
        identf = sb("identf", [128, 128], F32)
        cvi_sb = sb("cvi_sb", [128, NT * 4], F32)
        cvf_sb = sb("cvf_sb", [128, NT * 4], F32)
        gnw_sb = sb("gnw_sb", [128, NT], F32)
        lng_sb = sb("lng_sb", [128, NT], F32)
        lnb_sb = sb("lnb_sb", [128, NT], F32)
        cw_sb = sb("cw_sb", [128, NTI * 4], F32)
        cb_sb = sb("cb_sb", [128, NTI], F32)
        dtb_sb = sb("dtb_sb", [128, NTI], F32)
        dd_sb = sb("dd_sb", [128, NTI], F32)
        a_sb = sb("a_sb", [128, NTI * NS], F32)
        hst_g = sb("hst_g", [128, NT], F32)
        hst_m = sb("hst_m", [128, NTI * NS], F32)
        itail = sb("itail", [128, NT * 3], F32)
        ftail = sb("ftail", [128, NT * 3], F32)
        utail = sb("utail", [128, NTI * 3], F32)
        xsq = sb("xsq", [128, NT * W], F32)

        with nc.Block(no_gpsimd_drain=True) as blk:
            C.block = blk
            ld = C.dma("sy", [], [
                (cvi_sb[:, :], cvi), (cvf_sb[:, :], cvf),
                (gnw_sb[:, :], gnw), (lng_sb[:, :], lng),
                (lnb_sb[:, :], lnb), (cw_sb[:, :], cw),
                (cb_sb[:, :], cb), (dtb_sb[:, :], dtb),
                (dd_sb[:, :], dd), (a_sb[:, :], alog),
            ])
            s_a1 = C.st("ac", ld, lambda e: e.activation(a_sb[:, :],
                                                         a_sb[:, :], AF.Exp))
            C.st("dv", s_a1, lambda e: e.tensor_scalar_mul(
                a_sb[:, :], a_sb[:, :], -1.0))

            def zeros(e):
                e.memset(hst_g[:, :], 0.0)
                e.memset(hst_m[:, :], 0.0)
                e.memset(itail[:, :], 0.0)
                e.memset(ftail[:, :], 0.0)
                e.memset(utail[:, :], 0.0)
                e.memset(ones1[:, :], 1.0)
                e.memset(cst[:, 0:1], LN_EPS)
                e.memset(cst[:, 1:2], -0.5)
                return e.memset(onesr[:, :], 1.0)
            C.st("dv", [], zeros)

            def gpinit(gpsimd):
                sem = C.sems["gp"][0]
                cnt = C.sems["gp"][1]
                gpsimd.memset(identf[:, :], 0.0).then_inc(sem, 1)
                cnt += 1
                gpsimd.wait_ge(sem, cnt)
                gpsimd.affine_select(
                    out=identf[:, :], in_=identf[:, :],
                    compare_op=ALU.not_equal, fill=1.0, base=0,
                    pattern=[[-1, 128]], channel_multiplier=1).then_inc(
                        sem, 1)
                cnt += 1
                C.sems["gp"][1] = cnt
            blk.gpsimd(gpinit)

        need_wcb = ("p2" in phases) or ("p5" in phases) or ("p4" in phases)
        wcb_done = []
        if need_wcb:
            with nc.Block(no_gpsimd_drain=True) as blk:
                C.block = blk
                for r in range(16):
                    wcb_done += C.dma("gp", [], [
                        (wcb[r * 1024:(r + 1) * 1024, :],
                         wcT[r * 1024:(r + 1) * 1024, :])])
                wcb_done += C.dma("gp", [], [(wob_sc, woutT)])

        if "p1" in phases:
            with contextlib.ExitStack() as PS:
                psb = lambda *a: PS.enter_context(nc.sbuf_tensor(*a))
                ppt = lambda *a: PS.enter_context(nc.psum_tensor(*a))
                wib = psb("wib", [128, NT * D], BF16)
                wfb = psb("wfb", [128, NT * D], BF16)
                wgb = psb("wgb", [128, NT * D], BF16)
                wob = psb("wob", [128, NT * D], BF16)
                rows = psb("rows", [1, 4 * W], F32)
                tls = psb("tls", [128, W], F32)
                xg = psb("xg", [128, NT * W], F32)
                xnb = psb("xnb", [128, NT * W], BF16)
                iraw = psb("iraw", [128, NT * W], F32)
                fraw = psb("fraw", [128, NT * W], F32)
                icv = psb("icv", [128, NT * W], F32)
                fcv = psb("fcv", [128, NT * W], F32)
                ob = psb("ob", [128, NT * W], BF16)
                big = [ppt(f"p1ps{i}", [128, 1024], F32).ap()
                       for i in range(2)]
                st_sum = big[0][0:1, 0:W]
                st_sq = big[0][0:1, W:2 * W]
                rep_a = big[1][:, 0:W]
                rep_b = big[1][:, W:2 * W]
                mm_ps = [ppt("p1mm", [128, 1024], F32).ap()]
                mm = [mm_ps[0][:, 0:W], mm_ps[0][:, W:2 * W]]

                with nc.Block(no_gpsimd_drain=True) as blk:
                    C.block = blk
                    C.dma("gp", [], [
                        (wib[:, :].rearrange("p (k e) -> p k e", e=D),
                         _r3(wi)),
                        (wfb[:, :].rearrange("p (k e) -> p k e", e=D),
                         _r3(wf_)),
                        (wgb[:, :].rearrange("p (k e) -> p k e", e=D),
                         _r3(wg)),
                        (wob[:, :].rearrange("p (k e) -> p k e", e=D),
                         _r3(wo)),
                    ])
                wdep = C.snap("gd")

                p1tail = []
                prev_ev = [[], []]
                prev_o = [[], []]
                with nc.Block(no_gpsimd_drain=True) as blk:
                    C.block = blk
                    for tc in range(NCH):
                        tsl = slice(tc * W, (tc + 1) * W)
                        ksl = (lambda buf:
                               (lambda k: buf[:, k * W:(k + 1) * W]))
                        s_x = C.dma("sy", p1tail, [
                            (xg[:, :].rearrange("p (k t) -> p k t", t=W),
                             _r3(xf)[:, :, tsl])])
                        s_ln, _ = emit_ln(
                            C, ksl(xg), xsq, st_sum, st_sq, rep_a, rep_b,
                            rows, tls, s_x + p1tail, dst_bf_sl=ksl(xnb))
                        evs = []
                        gi = 0
                        for wsb, dst in ((wib, iraw), (wfb, fraw)):
                            for m in range(NT):
                                par = gi % 2

                                def mmg(e, wsb=wsb, m=m, par=par):
                                    last = None
                                    for k in range(NT):
                                        last = e.matmul(
                                            mm[par],
                                            wsb[:, k * D + m * 128:
                                                k * D + (m + 1) * 128],
                                            xnb[:, k * W:(k + 1) * W],
                                            start=(k == 0),
                                            stop=(k == NT - 1))
                                    return last
                                s_mm = C.st("pe",
                                            s_ln + wdep + prev_ev[par], mmg)
                                s_ev = C.st("dv", s_mm,
                                            lambda e, dst=dst, m=m, par=par:
                                            e.tensor_copy(
                                                dst[:, m * W:(m + 1) * W],
                                                mm[par]))
                                prev_ev[par] = s_ev
                                evs += s_ev
                                gi += 1
                        s_ic, s_it = emit_conv(
                            C, ksl(iraw), ksl(icv), itail,
                            lambda k, j: cvi_sb[:, k * 4 + j:k * 4 + j + 1],
                            evs, NT)
                        s_fc, s_ft = emit_conv(
                            C, ksl(fraw), ksl(fcv), ftail,
                            lambda k, j: cvf_sb[:, k * 4 + j:k * 4 + j + 1],
                            evs, NT)
                        s_fg = C.st("ac", s_fc + s_ft, lambda e:
                                    e.activation(fraw[:, :], fcv[:, :],
                                                 AF.Sigmoid))
                        s_si = C.st("ac", s_ic + s_it + s_fg, lambda e:
                                    e.activation(iraw[:, :], icv[:, :],
                                                 AF.Silu))
                        s_g1 = C.st("dv", s_fg + s_si, lambda e:
                                    e.tensor_mul(icv[:, :], iraw[:, :],
                                                 fraw[:, :]))
                        s_g2 = C.st("dv", s_g1, lambda e:
                                    e.tensor_sub(icv[:, :], iraw[:, :],
                                                 icv[:, :]))

                        def scans(e):
                            last = None
                            for k in range(NT):
                                sl = slice(k * W, (k + 1) * W)
                                last = e.tensor_tensor_scan(
                                    fcv[:, sl], fraw[:, sl], icv[:, sl],
                                    hst_g[:, k:k + 1], ALU.mult, ALU.add)
                            return last
                        s_sc = C.st("dv", s_g2, scans)

                        def savest(e):
                            last = None
                            for k in range(NT):
                                last = e.tensor_copy(
                                    hst_g[:, k:k + 1],
                                    fcv[:, k * W + W - 1:k * W + W])
                            return last
                        s_hs = C.st("dv", s_sc, savest)
                        prev_g = s_sc
                        for m in range(NT):
                            par = m % 2

                            def mmg2(e, m=m, par=par):
                                last = None
                                for k in range(NT):
                                    last = e.matmul(
                                        mm[par],
                                        wgb[:, k * D + m * 128:
                                            k * D + (m + 1) * 128],
                                        xnb[:, k * W:(k + 1) * W],
                                        start=(k == 0), stop=(k == NT - 1))
                                return last
                            s_mm = C.st("pe", s_sc + prev_g, mmg2)
                            s_ev = C.st("dv", s_mm, lambda e, m=m, par=par:
                                        e.tensor_copy(
                                            icv[:, m * W:(m + 1) * W],
                                            mm[par]))
                            prev_g = s_ev
                        s_sg = C.st("ac", prev_g, lambda e: e.activation(
                            fraw[:, :], icv[:, :], AF.Silu))
                        s_rms, _ = emit_ln(
                            C, ksl(fcv), xsq, st_sum, st_sq, rep_a, rep_b,
                            rows, tls, s_sc + s_hs, rms=True,
                            dst_f32_sl=ksl(iraw))

                        def gate(e):
                            last = None
                            for k in range(NT):
                                sl = slice(k * W, (k + 1) * W)
                                last = e.scalar_tensor_tensor(
                                    ob[:, sl], iraw[:, sl],
                                    gnw_sb[:, k:k + 1], fraw[:, sl],
                                    ALU.mult, ALU.mult)
                            return last
                        s_ob = C.st("dv", s_rms + s_sg, gate)
                        prev_o = [prev_o[0] + s_ob, prev_o[1] + s_ob]
                        s_all = []
                        for m in range(NT):
                            par = m % 2

                            def mmo(e, m=m, par=par):
                                last = None
                                for k in range(NT):
                                    last = e.matmul(
                                        mm[par],
                                        wob[:, k * D + m * 128:
                                            k * D + (m + 1) * 128],
                                        ob[:, k * W:(k + 1) * W],
                                        start=(k == 0), stop=(k == NT - 1))
                                return last
                            s_mm = C.st("pe", s_ob + prev_o[par], mmo)
                            s_e = C.st("dv", s_mm, lambda e, m=m, par=par:
                                       e.tensor_add(
                                           xg[:, m * W:(m + 1) * W],
                                           xg[:, m * W:(m + 1) * W],
                                           mm[par]))
                            prev_o[par] = s_e
                            s_all += s_e
                        p1tail = C.dma("sy", s_all, [
                            (_r3(xf_sc)[:, :, tsl],
                             xg[:, :].rearrange("p (k t) -> p k t",
                                                t=W))]) + s_all
                        prev_ev = [prev_o[0], prev_o[1]]
                with nc.Block(no_gpsimd_drain=True) as blk:
                    C.block = blk
                    C.st("sy", C.snap("sd"), lambda e: None)

            if "x1" in dbg_t:
                _dump(C, nc, xf_sc, dbg_t["x1"], D, "dx1")

        if "p2" in phases:
            _emit_ffn(C, nc, ST, xsq, mk, ruT, wcb, xf_sc, None,
                      identf, dbg_dst=dbg_t.get("x2"), tag="a",
                      wdep0=wcb_done)

        if "p3" in phases:
            _emit_mamba_a(C, nc, xsq, xf_sc, xl_sc, u_sc, z_sc,
                          winT, xpT, cw_sb, cb_sb, lng_sb, lnb_sb, utail,
                          xdbl_sc)
            if "xl" in dbg_t:
                _dump(C, nc, xl_sc, dbg_t["xl"], D, "dxl")
            if "u" in dbg_t:
                _dump(C, nc, u_sc, dbg_t["u"], DI, "du")
            if "z" in dbg_t:
                _dump(C, nc, z_sc, dbg_t["z"], DI, "dz")

        if "p4" in phases:
            _emit_mamba_b(C, nc, xsq, xf_sc, xl_sc, u_sc, z_sc, wob_sc,
                          dtpT, xdbl_sc, a_sb, dtb_sb, dd_sb, hst_m, dbg_t,
                          zs_sc, wdep0=wcb_done)
            if "x3" in dbg_t:
                _dump(C, nc, xf_sc, dbg_t["x3"], D, "dx3")

        if "p5" in phases:
            _emit_ffn(C, nc, ST, xsq, mk, ruT, wcb, xf_sc, y,
                      identf, tag="b")

        with nc.Block(no_gpsimd_drain=True) as blk:
            C.block = blk
            C.st("sy", C.snap("sd") + C.snap("gd") + C.snap("ad"),
                 lambda e: None)

    return nc


def _emit_mamba_a(C, nc, xsq, xf_sc, xl_sc, u_sc, z_sc, winT,
                  xpT, cw_sb, cb_sb, lng_sb, lnb_sb, utail, xdbl_sc):
    with contextlib.ExitStack() as PS:
        sb = lambda *a: PS.enter_context(nc.sbuf_tensor(*a))
        pst = lambda *a: PS.enter_context(nc.psum_tensor(*a))
        rows = sb("rows_m", [1, 4 * W], F32)
        tls = sb("tls_m", [128, W], F32)
        winb = sb("winb", [128, NT * 2 * DI], BF16)
        xpb = sb("xpb", [128, NTI * 96], BF16)
        xg = sb("xg3", [128, NT * W], F32)
        xl_t = sb("xl3", [128, NT * W], F32)
        xnb = sb("xnb3", [128, NT * W], BF16)
        ub = sb("ub3", [128, NTI * W], BF16)
        xinr = sb("xinr", [128, W], F32)
        cacc = sb("cacc", [128, W], F32)
        uu = sb("uu", [128, W], F32)
        zz_t = sb("zz_t", [128, W], F32)
        xdb_s = sb("xdb_s", [96, W], F32)
        xdb_c = sb("xdb_c", [16, W], F32)
        big = [pst(f"p3ps{i}", [128, 1024], F32).ap() for i in range(3)]
        st_sum = big[0][0:1, 0:W]
        st_sq = big[0][0:1, W:2 * W]
        rep_a = big[1][:, 0:W]
        rep_b = big[1][:, W:2 * W]
        mm = [big[2][:, 0:W], big[2][:, W:2 * W]]
        ps_x = pst("p3psx", [128, 1024], F32).ap()
        ps_dt = ps_x[0:64, 0:W]
        ps_bb = ps_x[0:16, W:2 * W]
        ps_cc = ps_x[32:48, W:2 * W]

        with nc.Block(no_gpsimd_drain=True) as blk:
            C.block = blk
            C.dma("gp", [], [
                (winb[:, :].rearrange("p (k e) -> p k e", e=2 * DI),
                 _r3(winT)),
                (xpb[:, :].rearrange("p (k e) -> p k e", e=96),
                 xpT.rearrange("(k p) e -> p k e", p=128)),
            ])
        wdep = C.snap("gd")

        p3tail = []
        prev_ev = [[], []]
        prev_ust = []
        prev_zst = []
        prev_xst = []
        prev_xlst = []
        with nc.Block(no_gpsimd_drain=True) as blk:
            C.block = blk
            for tc in range(NCH):
                tsl = slice(tc * W, (tc + 1) * W)
                s_x = C.dma("sy", p3tail, [
                    (xg[:, :].rearrange("p (k t) -> p k t", t=W),
                     _r3(xf_sc)[:, :, tsl])])
                s_ln, _ = emit_ln(
                    C, lambda k: xg[:, k * W:(k + 1) * W], xsq, st_sum,
                    st_sq, rep_a, rep_b, rows, tls,
                    s_x + p3tail + prev_xlst,
                    dst_bf_sl=lambda k: xnb[:, k * W:(k + 1) * W],
                    dst_f32_sl=lambda k: xl_t[:, k * W:(k + 1) * W],
                    gb=(lambda k: lng_sb[:, k:k + 1],
                        lambda k: lnb_sb[:, k:k + 1]))
                s_xs = C.st("dv", s_ln, lambda e:
                            e.tensor_add(xl_t[:, :], xl_t[:, :],
                                         xg[:, :]))
                prev_xlst = C.dma("sy", s_xs, [
                    (_r3(xl_sc)[:, :, tsl],
                     xl_t[:, :].rearrange("p (k t) -> p k t", t=W))])
                u_done = []
                for m in range(2 * NTI):
                    par = m % 2

                    def mmi(e, m=m, par=par):
                        last = None
                        for k in range(NT):
                            last = e.matmul(
                                mm[par],
                                winb[:, k * 2 * DI + m * 128:
                                     k * 2 * DI + (m + 1) * 128],
                                xnb[:, k * W:(k + 1) * W],
                                start=(k == 0), stop=(k == NT - 1))
                        return last
                    s_mm = C.st("pe", s_ln + wdep + prev_ev[par], mmi)
                    if m < NTI:
                        s_e = C.st("dv", s_mm, lambda e, par=par:
                                   e.tensor_copy(xinr[:, :], mm[par]))
                        s_cv, s_ut = emit_conv(
                            C, lambda k: xinr[:, :], lambda k: cacc[:, :],
                            utail[:, m * 3:(m + 1) * 3],
                            lambda kk, j, m=m: cw_sb[:, m * 4 + j:
                                                     m * 4 + j + 1],
                            s_e, 1)
                        s_u = C.st("ac", s_cv + s_ut + prev_ust,
                                   lambda e, m=m:
                                   e.activation(uu[:, :], cacc[:, :],
                                                AF.Silu,
                                                bias=cb_sb[:, m:m + 1]))
                        s_ub = C.st("dv", s_u, lambda e, m=m:
                                    e.tensor_copy(ub[:, m * W:(m + 1) * W],
                                                  uu[:, :]))
                        s_st = C.dma("sy", s_u, [
                            (u_sc[m * 128:(m + 1) * 128, tsl], uu[:, :])])
                        prev_ust = s_st
                        u_done += s_ub
                        prev_ev[par] = s_e + s_st + s_ub
                    else:
                        s_e = C.st("dv", s_mm + prev_zst, lambda e, par=par:
                                   e.tensor_copy(zz_t[:, :], mm[par]))
                        s_st = C.dma("sy", s_e, [
                            (z_sc[(m - NTI) * 128:(m - NTI + 1) * 128, tsl],
                             zz_t[:, :])])
                        prev_zst = s_st
                        prev_ev[par] = s_e + s_st

                def mk_mm(dst, lo, hi):
                    def f(e):
                        last = None
                        for k in range(NTI):
                            last = e.matmul(
                                dst, xpb[:, k * 96 + lo:k * 96 + hi],
                                ub[:, k * W:(k + 1) * W],
                                start=(k == 0), stop=(k == NTI - 1))
                        return last
                    return f
                s_m1 = C.st("pe", u_done + prev_ev[0] + prev_ev[1],
                            mk_mm(ps_dt, 0, 64))
                s_m2 = C.st("pe", [], mk_mm(ps_bb, 64, 80))
                s_m3 = C.st("pe", [], mk_mm(ps_cc, 80, 96))

                def evx(e):
                    e.tensor_copy(xdb_s[0:64, :], ps_dt)
                    e.tensor_copy(xdb_s[64:80, :], ps_bb)
                    return e.tensor_copy(xdb_c[:, :], ps_cc)
                s_ev = C.st("dv", s_m1 + s_m2 + s_m3 + prev_xst, evx)
                prev_xst = C.dma("sy", s_ev, [
                    (xdbl_sc[0:80, tsl], xdb_s[0:80, :]),
                    (xdbl_sc[80:96, tsl], xdb_c[:, :])])
                p3tail = prev_xst + s_ev
        with nc.Block(no_gpsimd_drain=True) as blk:
            C.block = blk
            C.st("sy", C.snap("sd"), lambda e: None)


def _emit_mamba_b(C, nc, xsq, xf_sc, xl_sc, u_sc, z_sc, wob_sc, dtpT,
                  xdbl_sc, a_sb, dtb_sb, dd_sb, hst_m, dbg_t, zs_sc,
                  wdep0=()):
    with contextlib.ExitStack() as PS:
        sb = lambda *a: PS.enter_context(nc.sbuf_tensor(*a))
        pst = lambda *a: PS.enter_context(nc.psum_tensor(*a))
        dtpb = sb("dtpb", [64, DI], BF16)
        dtrb = sb("dtrb", [64, W], BF16)
        brep = sb("brep", [128, NS * W], F32)
        crep = sb("crep", [128, NS * W], F32)
        bbuf = [sb("bb0", [128, 4 * W], F32), sb("bb1", [128, 4 * W], F32),
                sb("bb2", [128, 4 * W], F32)]
        ab2 = sb("ab2", [128, 4 * W], F32)
        hcb = sb("hcb0", [128, 4 * W], F32)
        dt2 = [sb("dt20", [128, W], F32), sb("dt21", [128, W], F32)]
        sp1 = sb("sp1", [128, W], F32)
        yb = [sb("yb4a", [128, NTI * W], BF16),
              sb("yb4b", [128, NTI * W], BF16)]
        u_t = [sb("u_t0", [128, W], F32), sb("u_t1", [128, W], F32)]
        z_t = [sb("z_t0", [128, W], F32), sb("z_t1", [128, W], F32)]
        dtu = sb("dtu", [128, W], F32)
        yt2 = sb("yt2", [128, W], F32)
        fixt = sb("fixt", [128, 8], F32)
        wou = [sb("wou0", [128, NTI * 128], BF16),
               sb("wou1", [128, NTI * 128], BF16)]
        xg_t = sb("xg4", [128, W], F32)
        xl_t = sb("xl4", [128, W], F32)
        spsc = sb("spsc", [128, 2 * W], F32)
        big = pst("p4ps", [128, 1024], F32).ap()
        ps_r = [big[:, 0:W], big[:, W:2 * W]]
        pso_t = [pst("p4pso0", [128, 1024], F32).ap(),
                 pst("p4pso1", [128, 1024], F32).ap()]
        ps_o = [pso_t[0][:, 0:W], pso_t[0][:, W:2 * W],
                pso_t[1][:, 0:W], pso_t[1][:, W:2 * W]]
        abuf = [xsq[:, 0:4 * W], xsq[:, 4 * W:8 * W], ab2[:, :]]

        with nc.Block(no_gpsimd_drain=True) as blk:
            C.block = blk
            C.dma("gp", [], [(dtpb[:, :], dtpT)])
        wdep = C.snap("gd")

        scan_done = [[], [], []]
        gp_done = [[], [], []]
        uld = [[], []]
        zld = [[], []]
        hs_done = [[], [], []]
        ybfree2 = [[], []]
        pend_out = [None]
        brep_rd = [[], [], [], []]
        crep_rd = [[], [], [], []]
        prev_dbg = []
        prev_dtmm = [[], []]
        s_wprev = [list(wdep0), list(wdep0)]
        prev_out = [[], [], [], []]
        ps_free = [[], []]
        ps_free4 = [[], [], [], []]
        ybd = [[]]


        with nc.Block(no_gpsimd_drain=True) as blk:
            C.block = blk
            xgp = [xg_t, spsc[:, 0:W], xl_t, spsc[:, W:2 * W]]

            def _emit_out_m(po, m):
                ptc, pybd = po
                ptsl = slice(ptc * W, (ptc + 1) * W)
                ybuf = yb[ptc % 2]
                par = m % 2
                p4r = m % 4
                s_w = C.dma("sy", s_wprev[par], [
                    (wou[par][:, :].rearrange("p (k e) -> p k e", e=128),
                     _r3(wob_sc)[:, :, m * 128:(m + 1) * 128])])

                def mmo(e, m=m, par=par, p4r=p4r, ybuf=ybuf):
                    last = None
                    for kk in range(NTI):
                        last = e.matmul(
                            ps_o[p4r],
                            wou[par][:, kk * 128:(kk + 1) * 128],
                            ybuf[:, kk * W:(kk + 1) * W],
                            start=(kk == 0), stop=(kk == NTI - 1))
                    return last
                s_mm = C.st("pe", pybd + s_w + ps_free4[p4r], mmo)
                s_wprev[par] = s_mm
                ybfree2[ptc % 2] = ybfree2[ptc % 2] + s_mm
                s_x = C.dma("sy", prev_out[p4r], [
                    (xgp[p4r][:, :],
                     xl_sc[m * 128:(m + 1) * 128, ptsl])])
                s_r2 = C.st("dv", s_mm + s_x, lambda e, p4r=p4r:
                            e.tensor_add(xgp[p4r][:, :], xgp[p4r][:, :],
                                         ps_o[p4r]))
                s_st = C.dma("ac", s_r2, [
                    (xf_sc[m * 128:(m + 1) * 128, ptsl],
                     xgp[p4r][:, :])])
                prev_out[p4r] = s_st + s_r2
                ps_free4[p4r] = s_r2

            for tc in range(NCH):
                tsl = slice(tc * W, (tc + 1) * W)
                ydep_cur = ybfree2[tc % 2]
                ybfree2[tc % 2] = []
                s_db = C.dma("gp", prev_dbg + prev_dtmm[0] + prev_dtmm[1],
                             [(dtrb[:, :], xdbl_sc[0:64, tsl])])
                def _bc(r0, n0, n1):
                    return xdbl_sc[r0 + n0:r0 + n1, tsl].rearrange(
                        "n (o t) -> o n t", o=1).broadcast_to(
                        [128, n1 - n0, W])
                s_repb = []
                s_repc = []
                for qq in range(4):
                    nsl4 = slice(qq * 4 * W, (qq + 1) * 4 * W)
                    s_repb.append(C.dma("sy", brep_rd[qq], [
                        (brep[:, nsl4].rearrange("p (n t) -> p n t", n=4),
                         _bc(64, qq * 4, qq * 4 + 4))]))
                for qq in range(4):
                    nsl4 = slice(qq * 4 * W, (qq + 1) * 4 * W)
                    s_repc.append(C.dma("sy", crep_rd[qq], [
                        (crep[:, nsl4].rearrange("p (n t) -> p n t", n=4),
                         _bc(80, qq * 4, qq * 4 + 4))]))

                def dt_build(k):
                    p2 = k % 2
                    par = k % 2

                    def mmdt(e, k=k, par=par):
                        return e.matmul(
                            ps_r[par], dtpb[:, k * 128:(k + 1) * 128],
                            dtrb[:, :], start=True, stop=True)
                    s_mm = C.st("pe", s_db + wdep + ps_free[par]
                                + prev_dtmm[par], mmdt)
                    prev_dtmm[par] = s_mm

                    def evd(e, k=k, par=par, p2=p2):
                        return e.tensor_scalar_add(
                            dt2[p2][:, :], ps_r[par], dtb_sb[:, k:k + 1])
                    s_ev = C.st("dv", s_mm + uld[p2], evd)
                    ps_free[par] = s_ev
                    s_ab = C.st("ac", s_ev + sp_free[0], lambda e, p2=p2:
                                e.activation(sp1[:, :], dt2[p2][:, :],
                                             AF.Abs))
                    s_e1 = C.st("ac", s_ab, lambda e: e.activation(
                        sp1[:, :], sp1[:, :], AF.Exp, scale=-1.0))
                    s_rl = C.st("ac", s_ab, lambda e, p2=p2:
                                e.activation(dt2[p2][:, :], dt2[p2][:, :],
                                             AF.Relu))
                    s_e2 = C.st("ac", s_e1, lambda e: e.activation(
                        sp1[:, :], sp1[:, :], AF.Ln, bias=1.0))
                    def tail(p2=p2, s_e2=s_e2, s_rl=s_rl):
                        s_ad = C.st("dv", s_e2 + s_rl, lambda e, p2=p2:
                                    e.tensor_add(dt2[p2][:, :],
                                                 dt2[p2][:, :],
                                                 sp1[:, :]))
                        sp_free[0] = s_ad
                        return s_ad
                    return tail

                sp_free = [[]]
                s_dt = [None, None]
                s_dt[0] = dt_build(0)()
                for k in range(NTI):
                    p2 = k % 2
                    s_u = C.dma("sy", uld[p2], [
                        (u_t[p2][:, :], u_sc[k * 128:(k + 1) * 128, tsl])])
                    s_zt = C.dma("sy", zld[p2], [
                        (z_t[p2][:, :], z_sc[k * 128:(k + 1) * 128, tsl])])
                    s_sz = C.st("ac", s_zt + zld[p2], lambda e, p2=p2:
                                e.activation(z_t[p2][:, :], z_t[p2][:, :],
                                             AF.Silu))
                    s_du = C.st("dv", s_u + s_dt[p2], lambda e, p2=p2:
                                e.tensor_mul(dtu[:, :], dt2[p2][:, :],
                                             u_t[p2][:, :]))

                    def emit_prep(q):
                        p = (4 * k + q) % 3
                        nsl = slice(q * 4 * W, (q + 1) * 4 * W)
                        hsl = slice(k * NS + q * 4, k * NS + q * 4 + 4)

                        def a_exp(e, k=k, q=q, p=p, p2=p2):
                            last = None
                            for j in range(4):
                                n = q * 4 + j
                                last = e.activation(
                                    abuf[p][:, j * W:(j + 1) * W],
                                    dt2[p2][:, :], AF.Exp,
                                    scale=a_sb[:, k * NS + n:
                                               k * NS + n + 1])
                            return last
                        s_a = C.st("ac", s_dt[p2] + scan_done[p], a_exp)

                        def bbody(e, p=p, q=q):
                            last = None
                            for nn in range(4):
                                last = e.tensor_mul(
                                    bbuf[p][:, nn * W:(nn + 1) * W],
                                    brep[:, (q * 4 + nn) * W:
                                         (q * 4 + nn + 1) * W],
                                    dtu[:, :])
                            return last
                        s_b = C.st("dv", s_repb[q] + s_du + hs_done[p],
                                   bbody)
                        brep_rd[q] = s_b
                        acols = abuf[p][:, :].rearrange(
                            "p (n t) -> p n t", n=4)[:, :, 0:1]
                        bcols = bbuf[p][:, :].rearrange(
                            "p (n t) -> p n t", n=4)[:, :, 0:1]
                        ecols = bbuf[p][:, :].rearrange(
                            "p (n t) -> p n t", n=4)[:, :, W - 1:W]
                        hcols = hst_m[:, hsl].rearrange(
                            "p (a o) -> p a o", o=1)
                        fo = 4 * (q % 2)
                        ytmp = fixt[:, fo:fo + 4].rearrange(
                            "p (a o) -> p a o", o=1)
                        s_f1 = C.st("gp", s_a + s_b, lambda e, acols=acols,
                                    hcols=hcols, ytmp=ytmp:
                                    e.tensor_mul(ytmp, acols, hcols))
                        s_f2 = C.st("gp", s_f1, lambda e, bcols=bcols,
                                    ytmp=ytmp:
                                    e.tensor_add(bcols, bcols, ytmp))
                        s_f3 = C.st("gp", s_f2, lambda e, acols=acols:
                                    e.memset(acols, 0.0))
                        return (p, nsl, s_f3, ecols, hcols)

                    prep = [emit_prep(0), emit_prep(1), emit_prep(2),
                            None]
                    dt_tail = dt_build(k + 1) if k + 1 < NTI else None
                    for q in range(4):
                        if q == 2 and dt_tail is not None:
                            s_dt[(k + 1) % 2] = dt_tail()
                        if q == 3:
                            prep[3] = emit_prep(3)
                        p, nsl, s_f3, ecols, hcols = prep[q]
                        s_scan = C.st("dv", s_f3,
                                      lambda e, p=p:
                                      e.tensor_tensor_scan(
                                          bbuf[p][:, :], abuf[p][:, :],
                                          bbuf[p][:, :], 0.0, ALU.mult,
                                          ALU.add))
                        scan_done[p] = s_scan
                        hs_done[p] = C.st("gp", s_scan,
                                          lambda e, hcols=hcols,
                                          ecols=ecols:
                                          e.tensor_copy(hcols, ecols))
                        s_hc = C.st("dv", s_repc[q],
                                    lambda e, p=p, nsl=nsl:
                                    e.tensor_mul(hcb[:, :], bbuf[p][:, :],
                                                 crep[:, nsl]))
                        gp_done[p] = s_hc
                        crep_rd[q] = s_hc
                        C.st("dv", [], lambda e:
                             e.tensor_add(hcb[:, 0:2 * W], hcb[:, 0:2 * W],
                                          hcb[:, 2 * W:4 * W]))
                        if q == 0:
                            C.st("dv", [], lambda e:
                                 e.tensor_add(yt2[:, :], hcb[:, 0:W],
                                              hcb[:, W:2 * W]))
                        else:
                            C.st("dv", [], lambda e:
                                 e.tensor_add(yt2[:, :], yt2[:, :],
                                              hcb[:, 0:W]))
                            C.st("dv", [], lambda e:
                                 e.tensor_add(yt2[:, :], yt2[:, :],
                                              hcb[:, W:2 * W]))
                    s_y3 = C.st("dv", [], lambda e, k=k, p2=p2:
                                e.scalar_tensor_tensor(
                                    yt2[:, :], u_t[p2][:, :],
                                    dd_sb[:, k:k + 1], yt2[:, :],
                                    ALU.mult, ALU.add))
                    s_y4 = C.st("dv", s_y3 + s_sz + ydep_cur,
                                lambda e, k=k, p2=p2, tc=tc:
                                e.tensor_mul(
                                    yb[tc % 2][:, k * W:(k + 1) * W],
                                    yt2[:, :], z_t[p2][:, :]))
                    uld[p2] = s_y3
                    zld[p2] = s_y4
                    ybd[0] = s_y4
                    if pend_out[0] is not None and k < NT:
                        _emit_out_m(pend_out[0], k)
                pend_out[0] = (tc, ybd[0])
                if "dtv" in dbg_t:
                    prev_dbg = []
            for m in range(NT):
                _emit_out_m(pend_out[0], m)
        with nc.Block(no_gpsimd_drain=True) as blk:
            C.block = blk
            C.st("sy", C.snap("sd"), lambda e: None)


def _dump(C, nc, src, dst, rows_n, tag):
    with contextlib.ExitStack() as ES:
        tmp = ES.enter_context(nc.sbuf_tensor(f"dump_{tag}", [128, T], F32))
        with nc.Block(no_gpsimd_drain=True) as blk:
            C.block = blk
            prev = []
            for r in range(rows_n // 128):
                s = C.dma("sy", prev, [(tmp[:, :],
                                        src[r * 128:(r + 1) * 128, :])])
                prev = C.dma("sy", s, [(dst[r * 128:(r + 1) * 128, :],
                                        tmp[:, :])])
            C.st("sy", C.snap("sd"), lambda e: None)


def _vec(v, nk):
    v = np.asarray(v)
    if v.ndim == 1:
        return np.ascontiguousarray(v.reshape(nk, 128).T)
    c = v.shape[1]
    return np.ascontiguousarray(
        v.reshape(nk, 128, c).transpose(1, 0, 2).reshape(128, nk * c))


def prep_inputs(inputs):
    x = np.asarray(inputs["L0_attn_a_x"])
    base = {
        "wi": np.asarray(inputs["L0_attn_a_Wi"]),
        "wf": np.asarray(inputs["L0_attn_a_Wf"]),
        "wg": np.asarray(inputs["L0_attn_a_Wg"]),
        "wo": np.asarray(inputs["L0_attn_a_Wo"]),
        "cvi": _vec(inputs["L0_attn_a_conv_i"], NT),
        "cvf": _vec(inputs["L0_attn_a_conv_f"], NT),
        "gnw": _vec(inputs["L0_attn_a_g_norm_weight"], NT),
        "lng": _vec(inputs["L0_attn_b_ln_gamma"], NT),
        "lnb": _vec(inputs["L0_attn_b_ln_beta"], NT),
        "winT": np.ascontiguousarray(
            np.asarray(inputs["L0_attn_b_in_proj_w"]).T),
        "cw": _vec(inputs["L0_attn_b_conv_w"], NTI),
        "cb": _vec(inputs["L0_attn_b_conv_b"], NTI),
        "xpT": np.ascontiguousarray(
            np.asarray(inputs["L0_attn_b_x_proj_w"]).T),
        "dtpT": np.ascontiguousarray(
            np.asarray(inputs["L0_attn_b_dt_proj_w"]).T),
        "dtb": _vec(inputs["L0_attn_b_dt_proj_b"], NTI),
        "alog": _vec(inputs["L0_attn_b_A_log"], NTI),
        "dd": _vec(inputs["L0_attn_b_D"], NTI),
        "woutT": np.ascontiguousarray(
            np.asarray(inputs["L0_attn_b_out_proj_w"]).T),
        "mk": np.asarray(inputs["L0_ffn_rail_maker"]),
        "ruT": np.ascontiguousarray(
            np.asarray(inputs["L0_ffn_rail_up_proj"]).transpose(0, 1, 3, 2)),
        "wcT": np.ascontiguousarray(np.asarray(inputs["L0_ffn_w_cproj"]).T),
    }
    in_maps = []
    for b in range(8):
        m = dict(base)
        m["xf"] = np.ascontiguousarray(x[b].T)
        in_maps.append(m)
    return in_maps


def kernel(**inputs):
    in_maps = prep_inputs(inputs)
    nc = build_program()
    res = run_bass_kernel_spmd(nc, in_maps, core_ids=list(range(8)))
    return np.stack([r["y"] for r in res.results], axis=0)


# version 22
# speedup vs baseline: 1.0277x; 1.0277x over previous
import contextlib
import numpy as np
import concourse.bass as bass
import concourse.mybir as mybir
from concourse.bass_utils import run_bass_kernel_spmd

F32 = mybir.dt.float32
BF16 = mybir.dt.bfloat16
AF = mybir.ActivationFunctionType
ALU = mybir.AluOpType

T = 4096
D = 1024
W = 512
NCH = T // W
NT = D // 128
DI = 2048
NTI = DI // 128
NS = 16
GW = 1024
NG = T // GW
LN_EPS = 1e-5

_ENG = {"pe": "tensor", "dv": "vector", "ac": "scalar", "gp": "gpsimd",
        "sy": "sync"}


class Ctx:

    def __init__(self, nc):
        self.nc = nc
        self.block = None
        self.sems = {}

    def init_sems(self, stack):
        for name in ("pe", "dv", "ac", "gp", "sy"):
            h = stack.enter_context(self.nc.semaphore(f"sem_{name}"))
            self.sems[name] = [h, 0]
        self.lanes = {}
        self.lane_rr = {}
        for q, nl in (("sd", 8), ("gd", 6), ("ad", 4)):
            self.lanes[q] = []
            self.lane_rr[q] = 0
            for i in range(nl):
                h = stack.enter_context(self.nc.semaphore(f"lane_{q}{i}"))
                self.lanes[q].append([h, 0])

    def snap(self, *names):
        out = []
        for n in names:
            if n in self.sems:
                out.append((self.sems[n][0], self.sems[n][1]))
            else:
                out += [(h, v) for h, v in self.lanes[n]]
        return out

    def _waits(self, eng, deps):
        merged = {}
        for s, v in deps:
            if v > 0 and merged.get(s, 0) < v:
                merged[s] = v
        for s, v in merged.items():
            eng.wait_ge(s, v)

    def st(self, sel, deps, emit):
        sem, _ = self.sems[sel]

        def f(eng):
            self._waits(eng, deps)
            last = emit(eng)
            if last is not None:
                last.then_inc(sem, 1)
        getattr(self.block, _ENG[sel])(f)
        self.sems[sel][1] += 1
        return self.snap(sel)

    def dma(self, sel, deps, pairs):
        q = {"sy": "sd", "gp": "gd", "ac": "ad"}[sel]
        lanes = self.lanes[q]
        plan = []
        for _ in pairs:
            li = self.lane_rr[q] % len(lanes)
            self.lane_rr[q] += 1
            lane = lanes[li]
            plan.append((lane, lane[1], lane[1] + 16))
            lane[1] += 16
        out_snap = [(lane[0], after) for lane, _, after in plan]

        def f(eng):
            self._waits(eng, deps)
            for (lane, before, _), (out, in_) in zip(plan, pairs):
                if before > 0:
                    eng.wait_ge(lane[0], before)
                eng.dma_start(out=out, in_=in_).then_inc(lane[0], 16)
        getattr(self.block, _ENG[sel])(f)
        return out_snap


def emit_ln(C, src_sl, xsq, st_sum, st_sq, rep_a, rep_b, rows, tls, dep, *,
            rms=False, dst_bf_sl=None, dst_f32_sl=None, gb=None, n_k=NT):
    inv_n = 1.0 / (n_k * 128)
    ones1, onesr = C.ones1, C.onesr
    sum_row = rows[0:1, 0:W]
    sq_row = rows[0:1, W:2 * W]
    a_row = rows[0:1, 2 * W:3 * W]
    b_row = rows[0:1, 3 * W:4 * W]

    def sq(e):
        last = None
        for k in range(n_k):
            last = e.activation(xsq[:, k * W:(k + 1) * W], src_sl(k),
                                AF.Square)
        return last
    s_sq = C.st("ac", dep, sq)

    def mm_stats(e):
        last = None
        if not rms:
            for k in range(n_k):
                last = e.matmul(st_sum, ones1[:, 0:1], src_sl(k),
                                start=(k == 0), stop=(k == n_k - 1))
        for k in range(n_k):
            last = e.matmul(st_sq, ones1[:, 0:1],
                            xsq[:, k * W:(k + 1) * W],
                            start=(k == 0), stop=(k == n_k - 1))
        return last
    s_mm = C.st("pe", dep + s_sq, mm_stats)

    if rms:
        s = C.st("dv", s_mm, lambda e: e.tensor_copy(sq_row, st_sq))
        s = C.st("dv", s, lambda e: e.tensor_scalar_mul(sq_row, sq_row,
                                                        inv_n))
        s = C.st("ac", s, lambda e: e.activation(a_row, sq_row, AF.Ln,
                                                 bias=C.eps1))
        s = C.st("ac", s, lambda e: e.activation(a_row, a_row, AF.Exp,
                                                 scale=C.mhalf1))
        s_rep = C.st("pe", s, lambda e: e.matmul(rep_a, onesr[0:1, :], a_row,
                                                 start=True, stop=True))
        done = []
        for k in range(n_k):
            done += C.st("dv", s_rep, lambda e, k=k: e.tensor_mul(
                dst_f32_sl(k), src_sl(k), rep_a))
        return done, s

    def cps(e):
        e.tensor_copy(sum_row, st_sum)
        return e.tensor_copy(sq_row, st_sq)
    s = C.st("dv", s_mm, cps)

    def mumath(e):
        e.tensor_scalar_mul(sum_row, sum_row, inv_n)
        return e.tensor_scalar_mul(sq_row, sq_row, inv_n)
    s = C.st("dv", s, mumath)
    s = C.st("dv", s, lambda e: e.tensor_mul(b_row, sum_row, sum_row))
    s = C.st("dv", s, lambda e: e.tensor_sub(sq_row, sq_row, b_row))
    s = C.st("ac", s, lambda e: e.activation(a_row, sq_row, AF.Ln,
                                             bias=C.eps1))
    s = C.st("ac", s, lambda e: e.activation(a_row, a_row, AF.Exp,
                                             scale=C.mhalf1))
    s = C.st("dv", s, lambda e: e.tensor_mul(b_row, sum_row, a_row))
    s = C.st("dv", s, lambda e: e.tensor_scalar_mul(b_row, b_row, -1.0))

    def mmrep(e):
        e.matmul(rep_a, onesr[0:1, :], a_row, start=True, stop=True)
        return e.matmul(rep_b, onesr[0:1, :], b_row, start=True, stop=True)
    s_rep = C.st("pe", s, mmrep)

    done = []
    for k in range(n_k):
        s_t = C.st("dv", s_rep, lambda e, k=k: e.tensor_mul(
            tls[:, 0:W], src_sl(k), rep_a))
        if dst_f32_sl is not None:
            s_x = C.st("dv", s_t, lambda e, k=k: e.tensor_add(
                dst_f32_sl(k), tls[:, 0:W], rep_b))
            if dst_bf_sl is not None:
                g, b = gb
                s_x = C.st("dv", s_x, lambda e, k=k: e.tensor_scalar(
                    dst_bf_sl(k), dst_f32_sl(k), g(k), b(k), ALU.mult,
                    ALU.add))
        else:
            s_x = C.st("dv", s_t, lambda e, k=k: e.tensor_add(
                dst_bf_sl(k), tls[:, 0:W], rep_b))
        done += s_x
    return done, s_rep


def emit_conv(C, raw_sl, acc_sl, tail, wcol, dep, n_k):
    def tap3(e):
        last = None
        for k in range(n_k):
            last = e.tensor_scalar_mul(acc_sl(k), raw_sl(k), wcol(k, 3))
        return last
    s = C.st("dv", dep, tap3)
    for j in (2, 1, 0):
        sft = 3 - j

        def tapj(e, j=j, sft=sft):
            last = None
            for k in range(n_k):
                acc = acc_sl(k)
                raw = raw_sl(k)
                last = e.scalar_tensor_tensor(
                    acc[:, sft:W], raw[:, 0:W - sft], wcol(k, j),
                    acc[:, sft:W], ALU.mult, ALU.add)
                last = e.scalar_tensor_tensor(
                    acc[:, 0:sft], tail[:, k * 3 + 3 - sft:k * 3 + 3],
                    wcol(k, j), acc[:, 0:sft], ALU.mult, ALU.add)
            return last
        s = C.st("dv", s, tapj)

    def upd(e):
        last = None
        for k in range(n_k):
            last = e.tensor_copy(tail[:, k * 3:(k + 1) * 3],
                                 raw_sl(k)[:, W - 3:W])
        return last
    s_tail = C.st("dv", s, upd)
    return s, s_tail


def _r3(ap, p=128):
    return ap.rearrange("(k p) t -> p k t", p=p)


def _emit_ffn(C, nc, ST, xsq, mkT, ruT, wcb, xf_sc, y, identf,
              dbg_dst=None, tag="", wdep0=()):
    with contextlib.ExitStack() as PS:
        sb = lambda *a: PS.enter_context(nc.sbuf_tensor(*a))
        pst = lambda *a: PS.enter_context(nc.psum_tensor(*a))
        rub = sb("rub" + tag, [128, 64 * 128], BF16)

        with contextlib.ExitStack() as QS:
            mtf = QS.enter_context(nc.sbuf_tensor("mtf" + tag,
                                                  [128, 64 * 128], F32))
            ruf = QS.enter_context(nc.sbuf_tensor("ruf" + tag, [128, 64 * 128],
                                                  F32))
            psW = QS.enter_context(nc.psum_tensor("psW" + tag,
                                                  [128, 256], F32))
            psWa = psW.ap()
            with nc.Block(no_gpsimd_drain=True) as blk:
                C.block = blk

                def zz(e):
                    e.memset(mtf[:, :], 0.0)
                    return e.memset(ruf[:, :], 0.0)
                s_z = C.st("gp", [], zz)
                pairs_sy = []
                pairs_ac = []
                for hd in range(4):
                    for m in range(32):
                        slot = 2 * (hd * 8 + m // 4) + (m % 2)
                        off = slot * 128
                        pbase = 64 * ((m // 2) % 2)
                        for c in range(2):
                            rail = 4 * (m // 2) + 2 * (m % 2) + c
                            rr = 2 * (m % 2) + c
                            pairs_sy.append((
                                mtf[pbase + rr * 16:pbase + (rr + 1) * 16,
                                    off + pbase + rr * 16:
                                    off + pbase + (rr + 1) * 16],
                                mkT[hd, rail]))
                            pairs_ac.append((
                                ruf[pbase + rr * 16:pbase + (rr + 1) * 16,
                                    off + c * 64:off + (c + 1) * 64],
                                ruT[hd, rail]))
                half = len(pairs_ac) // 2
                s_d1 = C.dma("sy", s_z, pairs_sy + pairs_ac[:half])
                s_d2 = C.dma("ac", s_z, pairs_ac[half:])
                ev_prev = [[], []]
                for slot in range(64):
                    off = slot * 128
                    par = slot % 2
                    psl = psWa[:, par * 128:(par + 1) * 128]

                    def mmw(e, off=off, psl=psl):
                        return e.matmul(psl, mtf[:, off:off + 128],
                                        ruf[:, off:off + 128],
                                        start=True, stop=True)
                    s_mw = C.st("pe", s_d1 + s_d2 + ev_prev[par], mmw)
                    s_ev = C.st("dv", s_mw, lambda e, off=off, psl=psl:
                                e.tensor_copy(rub[:, off:off + 128], psl))
                    ev_prev[par] = s_ev
                s_wb = ev_prev[0] + ev_prev[1]

        rows = sb("rows_f" + tag, [1, 4 * W], F32)
        tls = sb("tls_f" + tag, [128, W], F32)
        xnb = sb("xnb_f" + tag, [128, NT * GW], BF16)
        hh = sb("hh" + tag, [128, 32 * GW], BF16)
        wcf = [sb("wcf0" + tag, [128, 32 * 128], BF16),
               sb("wcf1" + tag, [128, 32 * 128], BF16)]
        oac = sb("oac" + tag, [128, NT * GW], F32)
        xgs = sb("xgs" + tag, [128, NT * W], F32)
        zev = sb("zev" + tag, [128, GW], F32)
        rsx = zev
        ytok = zev

        big = [pst(f"fps{i}" + tag, [128, 1024], F32).ap()
               for i in range(4)]
        st_sum = big[0][0:1, 0:W]
        st_sq = big[0][0:1, W:2 * W]
        rep_a = big[1][:, 0:W]
        rep_b = big[1][:, W:2 * W]
        ps_y = big[2]
        ps_zs2 = [big[0], big[1], big[3]]
        ps_cs = [big[3], big[2]]
        ps_tr = big[1]

        ffn_tail = []
        last_maker = []
        s_ye_prev = [[], []]
        ec_prev = [[], []]
        wc_prev = [list(wdep0), list(wdep0)]
        s_gel_prev2 = [[], []]
        prev_half = []
        with nc.Block(no_gpsimd_drain=True) as blk:
            C.block = blk
            for g in range(NG):
                s_gel_prev = s_gel_prev + ffn_tail if g else []
                s_ln_all = []
                for hf in range(2):
                    c0 = g * GW + hf * W
                    s_x = C.dma("sy", prev_half + last_maker, [
                        (xgs[:, :].rearrange("p (k t) -> p k t", t=W),
                         _r3(xf_sc)[:, :, c0:c0 + W])])
                    psdep = (s_gel_prev + ec_prev[0] + ec_prev[1]
                             + ffn_tail) if g else []
                    s_ln, s_lrep = emit_ln(
                        C, lambda k: xgs[:, k * W:(k + 1) * W], xsq,
                        st_sum, st_sq, rep_a, rep_b, rows, tls,
                        s_x + psdep,
                        dst_bf_sl=lambda k: xnb[:, k * GW + hf * W:
                                                k * GW + hf * W + W])
                    s_ln_all += s_ln
                    prev_half = s_ln + s_lrep
                gel_last = []
                gelq = []
                s_gel_prev = s_gel_prev if g else []
                s_gel_prev2 = [s_ln_all + s_gel_prev2[0] + s_wb,
                               s_ln_all + s_gel_prev2[1] + s_wb,
                               s_ln_all + (s_gel_prev2[2]
                                           if len(s_gel_prev2) > 2 else
                                           ec_prev[0]) + s_wb]
                for hd in range(4):
                    for rg in range(8):
                        for mq in range(4):
                            m = rg * 4 + mq
                            q = m // 2
                            rbase = (q % 2) * 64
                            zpar = (hd * 32 + m) % 3

                            def mmz(e, m=m, hd=hd, rg=rg, rbase=rbase,
                                    zpar=zpar):
                                slot = 2 * (hd * 8 + m // 4) + (m % 2)
                                off = slot * 128
                                last = None
                                for h2 in range(2):
                                    last = e.matmul(
                                        ps_zs2[zpar][:, h2 * W:(h2 + 1) * W],
                                        rub[rbase:rbase + 64,
                                            off:off + 128],
                                        xnb[rbase:rbase + 64,
                                            rg * GW + h2 * W:
                                            rg * GW + (h2 + 1) * W],
                                        start=True, stop=True)
                                return last
                            s_mz = C.st("pe", s_ln_all + s_gel_prev2[zpar],
                                        mmz)
                            last_maker = s_mz
                            gelq.append((m, zpar, s_mz))
                            if len(gelq) == 3 or m == 31:
                                batch = list(gelq)
                                gelq.clear()

                                def gtri(e, batch=batch):
                                    last = None
                                    for (gm, gz, _) in batch:
                                        last = e.activation(
                                            hh[:, gm * GW:(gm + 1) * GW],
                                            ps_zs2[gz], AF.Gelu)
                                    return last
                                gdep = []
                                for (_, _, d) in batch:
                                    gdep += d
                                s_gel = C.st("ac", gdep, gtri)
                                s_gel_prev = s_gel
                                for (_, gz, _) in batch:
                                    s_gel_prev2[gz] = (s_gel_prev2[gz]
                                                       + s_gel)
                    gel_last = s_gel_prev
                    for et in range(NT):
                        par = et % 2
                        s_w = C.dma("sy", wc_prev[par], [
                            (wcf[par][:, :].rearrange(
                                "p (m e) -> p m e", e=128),
                             wcb[hd * 4096:(hd + 1) * 4096,
                                 et * 128:(et + 1) * 128].rearrange(
                                     "(m p) e -> p m e", p=128))])

                        def mmc(e, par=par):
                            last = None
                            for m in range(32):
                                for h2 in range(2):
                                    last = e.matmul(
                                        ps_cs[par][:, h2 * W:(h2 + 1) * W],
                                        wcf[par][:, m * 128:(m + 1) * 128],
                                        hh[:, m * GW + h2 * W:
                                           m * GW + (h2 + 1) * W],
                                        start=(m == 0), stop=(m == 31))
                            return last
                        dep_mc = gel_last + s_w + ec_prev[par]
                        if par == 0:
                            dep_mc = dep_mc + s_gel_prev2[2]
                        s_mc = C.st("pe", dep_mc, mmc)
                        wc_prev[par] = s_mc
                        if hd == 0:
                            s_ec = C.st("dv", s_mc, lambda e, et=et, par=par:
                                        e.tensor_copy(
                                            oac[:, et * GW:(et + 1) * GW],
                                            ps_cs[par]))
                        else:
                            s_ec = C.st("dv", s_mc, lambda e, et=et, par=par:
                                        e.tensor_add(
                                            oac[:, et * GW:(et + 1) * GW],
                                            oac[:, et * GW:(et + 1) * GW],
                                            ps_cs[par]))
                        ec_prev[par] = s_ec
                    s_gel_prev2[2] = s_gel_prev2[2] + ec_prev[0]
                s_tr_prev = [[], []]
                for et in range(NT):
                    s_xe = C.dma("sy", ec_prev[0] + ec_prev[1] + ffn_tail, [
                        (rsx[:, :],
                         xf_sc[et * 128:(et + 1) * 128,
                               g * GW:(g + 1) * GW])])
                    s_r = C.st("dv", s_xe + ec_prev[et % 2], lambda e, et=et:
                               e.tensor_add(oac[:, et * GW:(et + 1) * GW],
                                            oac[:, et * GW:(et + 1) * GW],
                                            rsx[:, :]))
                    if y is None:
                        s_so = C.dma("sy", s_r, [
                            (xf_sc[et * 128:(et + 1) * 128,
                                   g * GW:(g + 1) * GW],
                             oac[:, et * GW:(et + 1) * GW])])
                        if dbg_dst is not None:
                            s_so = s_so + C.dma("sy", s_r, [
                                (dbg_dst[et * 128:(et + 1) * 128,
                                         g * GW:(g + 1) * GW],
                                 oac[:, et * GW:(et + 1) * GW])])
                        ec_prev[et % 2] = s_r + s_so
                        ffn_tail = s_r + s_so
                    else:
                        ec_prev[et % 2] = s_r
                if y is not None:
                    res_done = ec_prev[0] + ec_prev[1]
                    s_yst = []
                    for sv in range(8):
                        s_cp_all = []
                        for et in range(NT):
                            par = et % 2
                            tsl2 = slice(par * 512, par * 512 + 128)

                            def tr(e, et=et, sv=sv, tsl2=tsl2):
                                return e.transpose(
                                    ps_tr[:, tsl2],
                                    oac[:, et * GW + sv * 128:
                                        et * GW + (sv + 1) * 128],
                                    identf[:, :])
                            s_t = C.st("pe", res_done + s_tr_prev[par], tr)
                            s_cp = C.st("dv", s_t + s_yst,
                                        lambda e, et=et, tsl2=tsl2:
                                        e.tensor_copy(
                                            ytok[:, et * 128:(et + 1) * 128],
                                            ps_tr[:, tsl2]))
                            s_tr_prev[par] = s_cp
                            s_cp_all += s_cp
                        s_yst = C.dma("sy", s_cp_all, [
                            (y[g * GW + sv * 128:g * GW + (sv + 1) * 128, :],
                             ytok[:, :])])
                    ffn_tail = s_yst + s_tr_prev[0] + s_tr_prev[1]
        with nc.Block(no_gpsimd_drain=True) as blk:
            C.block = blk
            C.st("sy", C.snap("sd") + C.snap("ad"), lambda e: None)


def build_program(dbg=(), phases=("p1", "p2", "p3", "p4", "p5")):
    nc = bass.Bass("TRN2", target_bir_lowering=False)
    C = Ctx(nc)

    xf = nc.dram_tensor("xf", [D, T], F32, kind="ExternalInput").ap()
    wi = nc.dram_tensor("wi", [D, D], F32, kind="ExternalInput").ap()
    wf_ = nc.dram_tensor("wf", [D, D], F32, kind="ExternalInput").ap()
    wg = nc.dram_tensor("wg", [D, D], F32, kind="ExternalInput").ap()
    wo = nc.dram_tensor("wo", [D, D], F32, kind="ExternalInput").ap()
    cvi = nc.dram_tensor("cvi", [128, NT * 4], F32,
                         kind="ExternalInput").ap()
    cvf = nc.dram_tensor("cvf", [128, NT * 4], F32,
                         kind="ExternalInput").ap()
    gnw = nc.dram_tensor("gnw", [128, NT], F32, kind="ExternalInput").ap()
    lng = nc.dram_tensor("lng", [128, NT], F32, kind="ExternalInput").ap()
    lnb = nc.dram_tensor("lnb", [128, NT], F32, kind="ExternalInput").ap()
    winT = nc.dram_tensor("winT", [D, 2 * DI], F32,
                          kind="ExternalInput").ap()
    cw = nc.dram_tensor("cw", [128, NTI * 4], F32,
                        kind="ExternalInput").ap()
    cb = nc.dram_tensor("cb", [128, NTI], F32, kind="ExternalInput").ap()
    xpT = nc.dram_tensor("xpT", [DI, 96], F32, kind="ExternalInput").ap()
    dtpT = nc.dram_tensor("dtpT", [64, DI], F32, kind="ExternalInput").ap()
    dtb = nc.dram_tensor("dtb", [128, NTI], F32, kind="ExternalInput").ap()
    alog = nc.dram_tensor("alog", [128, NTI * NS], F32,
                          kind="ExternalInput").ap()
    dd = nc.dram_tensor("dd", [128, NTI], F32, kind="ExternalInput").ap()
    woutT = nc.dram_tensor("woutT", [DI, D], F32, kind="ExternalInput").ap()
    mk = nc.dram_tensor("mk", [4, 64, 16, 16], F32,
                        kind="ExternalInput").ap()
    mkT = nc.dram_tensor("mkT", [4, 64, 16, 16], F32,
                         kind="ExternalInput").ap()
    ruT = nc.dram_tensor("ruT", [4, 64, 16, 64], F32,
                         kind="ExternalInput").ap()
    wcT = nc.dram_tensor("wcT", [16 * D, D], F32, kind="ExternalInput").ap()

    y = nc.dram_tensor("y", [T, D], F32, kind="ExternalOutput").ap()
    dbg_t = {}
    for name in dbg:
        shp = {"x1": [D, T], "x2": [D, T], "x3": [D, T], "xl": [D, T],
               "u": [DI, T], "z": [DI, T], "dtv": [DI, T],
               "xdbl": [128, T]}[name]
        dbg_t[name] = nc.dram_tensor("dbg_" + name, shp, F32,
                                     kind="ExternalOutput").ap()

    xf_sc = nc.dram_tensor("xf_sc", [D, T], F32, kind="Internal").ap()
    xl_sc = nc.dram_tensor("xl_sc", [D, T], F32, kind="Internal").ap()
    u_sc = nc.dram_tensor("u_sc", [DI, T], F32, kind="Internal").ap()
    z_sc = nc.dram_tensor("z_sc", [DI, T], F32, kind="Internal").ap()
    wcb = nc.dram_tensor("wcb", [16 * D, D], BF16, kind="Internal").ap()
    wob_sc = nc.dram_tensor("wob_sc", [DI, D], BF16, kind="Internal").ap()
    zs_sc = nc.dram_tensor("zs_sc", [DI, W], F32, kind="Internal").ap()
    xdbl_sc = nc.dram_tensor("xdbl_sc", [96, T], F32, kind="Internal").ap()

    with contextlib.ExitStack() as ST:
        C.init_sems(ST)
        sb = lambda *a: ST.enter_context(nc.sbuf_tensor(*a))

        ones1 = sb("ones1", [128, 1], F32)
        onesr = sb("onesr", [1, 128], F32)
        cst = sb("cst", [128, 2], F32)
        C.ones1, C.onesr = ones1, onesr
        C.eps1 = cst[0:1, 0:1]
        C.mhalf1 = cst[0:1, 1:2]
        identf = sb("identf", [128, 128], F32)
        cvi_sb = sb("cvi_sb", [128, NT * 4], F32)
        cvf_sb = sb("cvf_sb", [128, NT * 4], F32)
        gnw_sb = sb("gnw_sb", [128, NT], F32)
        lng_sb = sb("lng_sb", [128, NT], F32)
        lnb_sb = sb("lnb_sb", [128, NT], F32)
        cw_sb = sb("cw_sb", [128, NTI * 4], F32)
        cb_sb = sb("cb_sb", [128, NTI], F32)
        dtb_sb = sb("dtb_sb", [128, NTI], F32)
        dd_sb = sb("dd_sb", [128, NTI], F32)
        a_sb = sb("a_sb", [128, NTI * NS], F32)
        hst_g = sb("hst_g", [128, NT], F32)
        hst_m = sb("hst_m", [128, NTI * NS], F32)
        itail = sb("itail", [128, NT * 3], F32)
        ftail = sb("ftail", [128, NT * 3], F32)
        utail = sb("utail", [128, NTI * 3], F32)
        xsq = sb("xsq", [128, NT * W], F32)

        with nc.Block(no_gpsimd_drain=True) as blk:
            C.block = blk
            ld = C.dma("sy", [], [
                (cvi_sb[:, :], cvi), (cvf_sb[:, :], cvf),
                (gnw_sb[:, :], gnw), (lng_sb[:, :], lng),
                (lnb_sb[:, :], lnb), (cw_sb[:, :], cw),
                (cb_sb[:, :], cb), (dtb_sb[:, :], dtb),
                (dd_sb[:, :], dd), (a_sb[:, :], alog),
            ])
            s_a1 = C.st("ac", ld, lambda e: e.activation(a_sb[:, :],
                                                         a_sb[:, :], AF.Exp))
            C.st("dv", s_a1, lambda e: e.tensor_scalar_mul(
                a_sb[:, :], a_sb[:, :], -1.0))

            def zeros(e):
                e.memset(hst_g[:, :], 0.0)
                e.memset(hst_m[:, :], 0.0)
                e.memset(itail[:, :], 0.0)
                e.memset(ftail[:, :], 0.0)
                e.memset(utail[:, :], 0.0)
                e.memset(ones1[:, :], 1.0)
                e.memset(cst[:, 0:1], LN_EPS)
                e.memset(cst[:, 1:2], -0.5)
                return e.memset(onesr[:, :], 1.0)
            C.st("dv", [], zeros)

            def gpinit(gpsimd):
                sem = C.sems["gp"][0]
                cnt = C.sems["gp"][1]
                gpsimd.memset(identf[:, :], 0.0).then_inc(sem, 1)
                cnt += 1
                gpsimd.wait_ge(sem, cnt)
                gpsimd.affine_select(
                    out=identf[:, :], in_=identf[:, :],
                    compare_op=ALU.not_equal, fill=1.0, base=0,
                    pattern=[[-1, 128]], channel_multiplier=1).then_inc(
                        sem, 1)
                cnt += 1
                C.sems["gp"][1] = cnt
            blk.gpsimd(gpinit)

        need_wcb = ("p2" in phases) or ("p5" in phases) or ("p4" in phases)
        wcb_done = []
        if need_wcb:
            with nc.Block(no_gpsimd_drain=True) as blk:
                C.block = blk
                for r in range(16):
                    wcb_done += C.dma("gp", [], [
                        (wcb[r * 1024:(r + 1) * 1024, :],
                         wcT[r * 1024:(r + 1) * 1024, :])])
                wcb_done += C.dma("gp", [], [(wob_sc, woutT)])

        if "p1" in phases:
            with contextlib.ExitStack() as PS:
                psb = lambda *a: PS.enter_context(nc.sbuf_tensor(*a))
                ppt = lambda *a: PS.enter_context(nc.psum_tensor(*a))
                wib = psb("wib", [128, NT * D], BF16)
                wfb = psb("wfb", [128, NT * D], BF16)
                wgb = psb("wgb", [128, NT * D], BF16)
                wob = psb("wob", [128, NT * D], BF16)
                rows = psb("rows", [1, 4 * W], F32)
                tls = psb("tls", [128, W], F32)
                xg = psb("xg", [128, NT * W], F32)
                xnb = psb("xnb", [128, NT * W], BF16)
                iraw = psb("iraw", [128, NT * W], F32)
                fraw = psb("fraw", [128, NT * W], F32)
                icv = psb("icv", [128, NT * W], F32)
                fcv = psb("fcv", [128, NT * W], F32)
                ob = psb("ob", [128, NT * W], BF16)
                big = [ppt(f"p1ps{i}", [128, 1024], F32).ap()
                       for i in range(2)]
                st_sum = big[0][0:1, 0:W]
                st_sq = big[0][0:1, W:2 * W]
                rep_a = big[1][:, 0:W]
                rep_b = big[1][:, W:2 * W]
                mm_ps = [ppt("p1mm", [128, 1024], F32).ap()]
                mm = [mm_ps[0][:, 0:W], mm_ps[0][:, W:2 * W]]

                with nc.Block(no_gpsimd_drain=True) as blk:
                    C.block = blk
                    C.dma("gp", [], [
                        (wib[:, :].rearrange("p (k e) -> p k e", e=D),
                         _r3(wi)),
                        (wfb[:, :].rearrange("p (k e) -> p k e", e=D),
                         _r3(wf_)),
                        (wgb[:, :].rearrange("p (k e) -> p k e", e=D),
                         _r3(wg)),
                        (wob[:, :].rearrange("p (k e) -> p k e", e=D),
                         _r3(wo)),
                    ])
                wdep = C.snap("gd")

                p1tail = []
                prev_ev = [[], []]
                prev_o = [[], []]
                with nc.Block(no_gpsimd_drain=True) as blk:
                    C.block = blk
                    for tc in range(NCH):
                        tsl = slice(tc * W, (tc + 1) * W)
                        ksl = (lambda buf:
                               (lambda k: buf[:, k * W:(k + 1) * W]))
                        s_x = C.dma("sy", p1tail, [
                            (xg[:, :].rearrange("p (k t) -> p k t", t=W),
                             _r3(xf)[:, :, tsl])])
                        s_ln, _ = emit_ln(
                            C, ksl(xg), xsq, st_sum, st_sq, rep_a, rep_b,
                            rows, tls, s_x + p1tail, dst_bf_sl=ksl(xnb))
                        evs = []
                        gi = 0
                        for wsb, dst in ((wib, iraw), (wfb, fraw)):
                            for m in range(NT):
                                par = gi % 2

                                def mmg(e, wsb=wsb, m=m, par=par):
                                    last = None
                                    for k in range(NT):
                                        last = e.matmul(
                                            mm[par],
                                            wsb[:, k * D + m * 128:
                                                k * D + (m + 1) * 128],
                                            xnb[:, k * W:(k + 1) * W],
                                            start=(k == 0),
                                            stop=(k == NT - 1))
                                    return last
                                s_mm = C.st("pe",
                                            s_ln + wdep + prev_ev[par], mmg)
                                s_ev = C.st("dv", s_mm,
                                            lambda e, dst=dst, m=m, par=par:
                                            e.tensor_copy(
                                                dst[:, m * W:(m + 1) * W],
                                                mm[par]))
                                prev_ev[par] = s_ev
                                evs += s_ev
                                gi += 1
                        s_ic, s_it = emit_conv(
                            C, ksl(iraw), ksl(icv), itail,
                            lambda k, j: cvi_sb[:, k * 4 + j:k * 4 + j + 1],
                            evs, NT)
                        s_fc, s_ft = emit_conv(
                            C, ksl(fraw), ksl(fcv), ftail,
                            lambda k, j: cvf_sb[:, k * 4 + j:k * 4 + j + 1],
                            evs, NT)
                        s_fg = C.st("ac", s_fc + s_ft, lambda e:
                                    e.activation(fraw[:, :], fcv[:, :],
                                                 AF.Sigmoid))
                        s_si = C.st("ac", s_ic + s_it + s_fg, lambda e:
                                    e.activation(iraw[:, :], icv[:, :],
                                                 AF.Silu))
                        s_g1 = C.st("dv", s_fg + s_si, lambda e:
                                    e.tensor_mul(icv[:, :], iraw[:, :],
                                                 fraw[:, :]))
                        s_g2 = C.st("dv", s_g1, lambda e:
                                    e.tensor_sub(icv[:, :], iraw[:, :],
                                                 icv[:, :]))

                        def scans(e):
                            last = None
                            for k in range(NT):
                                sl = slice(k * W, (k + 1) * W)
                                last = e.tensor_tensor_scan(
                                    fcv[:, sl], fraw[:, sl], icv[:, sl],
                                    hst_g[:, k:k + 1], ALU.mult, ALU.add)
                            return last
                        s_sc = C.st("dv", s_g2, scans)

                        def savest(e):
                            last = None
                            for k in range(NT):
                                last = e.tensor_copy(
                                    hst_g[:, k:k + 1],
                                    fcv[:, k * W + W - 1:k * W + W])
                            return last
                        s_hs = C.st("dv", s_sc, savest)
                        prev_g = s_sc
                        for m in range(NT):
                            par = m % 2

                            def mmg2(e, m=m, par=par):
                                last = None
                                for k in range(NT):
                                    last = e.matmul(
                                        mm[par],
                                        wgb[:, k * D + m * 128:
                                            k * D + (m + 1) * 128],
                                        xnb[:, k * W:(k + 1) * W],
                                        start=(k == 0), stop=(k == NT - 1))
                                return last
                            s_mm = C.st("pe", s_sc + prev_g, mmg2)
                            s_ev = C.st("dv", s_mm, lambda e, m=m, par=par:
                                        e.tensor_copy(
                                            icv[:, m * W:(m + 1) * W],
                                            mm[par]))
                            prev_g = s_ev
                        s_sg = C.st("ac", prev_g, lambda e: e.activation(
                            fraw[:, :], icv[:, :], AF.Silu))
                        s_rms, _ = emit_ln(
                            C, ksl(fcv), xsq, st_sum, st_sq, rep_a, rep_b,
                            rows, tls, s_sc + s_hs, rms=True,
                            dst_f32_sl=ksl(iraw))

                        def gate(e):
                            last = None
                            for k in range(NT):
                                sl = slice(k * W, (k + 1) * W)
                                last = e.scalar_tensor_tensor(
                                    ob[:, sl], iraw[:, sl],
                                    gnw_sb[:, k:k + 1], fraw[:, sl],
                                    ALU.mult, ALU.mult)
                            return last
                        s_ob = C.st("dv", s_rms + s_sg, gate)
                        prev_o = [prev_o[0] + s_ob, prev_o[1] + s_ob]
                        s_all = []
                        for m in range(NT):
                            par = m % 2

                            def mmo(e, m=m, par=par):
                                last = None
                                for k in range(NT):
                                    last = e.matmul(
                                        mm[par],
                                        wob[:, k * D + m * 128:
                                            k * D + (m + 1) * 128],
                                        ob[:, k * W:(k + 1) * W],
                                        start=(k == 0), stop=(k == NT - 1))
                                return last
                            s_mm = C.st("pe", s_ob + prev_o[par], mmo)
                            s_e = C.st("dv", s_mm, lambda e, m=m, par=par:
                                       e.tensor_add(
                                           xg[:, m * W:(m + 1) * W],
                                           xg[:, m * W:(m + 1) * W],
                                           mm[par]))
                            prev_o[par] = s_e
                            s_all += s_e
                        p1tail = C.dma("sy", s_all, [
                            (_r3(xf_sc)[:, :, tsl],
                             xg[:, :].rearrange("p (k t) -> p k t",
                                                t=W))]) + s_all
                        prev_ev = [prev_o[0], prev_o[1]]
                with nc.Block(no_gpsimd_drain=True) as blk:
                    C.block = blk
                    C.st("sy", C.snap("sd"), lambda e: None)

            if "x1" in dbg_t:
                _dump(C, nc, xf_sc, dbg_t["x1"], D, "dx1")

        if "p2" in phases:
            _emit_ffn(C, nc, ST, xsq, mkT, ruT, wcb, xf_sc, None,
                      identf, dbg_dst=dbg_t.get("x2"), tag="a",
                      wdep0=wcb_done)

        if "p3" in phases:
            _emit_mamba_a(C, nc, xsq, xf_sc, xl_sc, u_sc, z_sc,
                          winT, xpT, cw_sb, cb_sb, lng_sb, lnb_sb, utail,
                          xdbl_sc)
            if "xl" in dbg_t:
                _dump(C, nc, xl_sc, dbg_t["xl"], D, "dxl")
            if "u" in dbg_t:
                _dump(C, nc, u_sc, dbg_t["u"], DI, "du")
            if "z" in dbg_t:
                _dump(C, nc, z_sc, dbg_t["z"], DI, "dz")

        if "p4" in phases:
            _emit_mamba_b(C, nc, xsq, xf_sc, xl_sc, u_sc, z_sc, wob_sc,
                          dtpT, xdbl_sc, a_sb, dtb_sb, dd_sb, hst_m, dbg_t,
                          zs_sc, wdep0=wcb_done)
            if "x3" in dbg_t:
                _dump(C, nc, xf_sc, dbg_t["x3"], D, "dx3")

        if "p5" in phases:
            _emit_ffn(C, nc, ST, xsq, mkT, ruT, wcb, xf_sc, y,
                      identf, tag="b")

        with nc.Block(no_gpsimd_drain=True) as blk:
            C.block = blk
            C.st("sy", C.snap("sd") + C.snap("gd") + C.snap("ad"),
                 lambda e: None)

    return nc


def _emit_mamba_a(C, nc, xsq, xf_sc, xl_sc, u_sc, z_sc, winT,
                  xpT, cw_sb, cb_sb, lng_sb, lnb_sb, utail, xdbl_sc):
    with contextlib.ExitStack() as PS:
        sb = lambda *a: PS.enter_context(nc.sbuf_tensor(*a))
        pst = lambda *a: PS.enter_context(nc.psum_tensor(*a))
        rows = sb("rows_m", [1, 4 * W], F32)
        tls = sb("tls_m", [128, W], F32)
        winb = sb("winb", [128, NT * 2 * DI], BF16)
        xpb = sb("xpb", [128, NTI * 96], BF16)
        xg = sb("xg3", [128, NT * W], F32)
        xl_t = sb("xl3", [128, NT * W], F32)
        xnb = sb("xnb3", [128, NT * W], BF16)
        ub = sb("ub3", [128, NTI * W], BF16)
        xinr = sb("xinr", [128, W], F32)
        cacc = sb("cacc", [128, W], F32)
        uu = sb("uu", [128, W], F32)
        zz_t = sb("zz_t", [128, W], F32)
        xdb_s = sb("xdb_s", [96, W], F32)
        xdb_c = sb("xdb_c", [16, W], F32)
        big = [pst(f"p3ps{i}", [128, 1024], F32).ap() for i in range(3)]
        st_sum = big[0][0:1, 0:W]
        st_sq = big[0][0:1, W:2 * W]
        rep_a = big[1][:, 0:W]
        rep_b = big[1][:, W:2 * W]
        mm = [big[2][:, 0:W], big[2][:, W:2 * W]]
        ps_x = pst("p3psx", [128, 1024], F32).ap()
        ps_dt = ps_x[0:64, 0:W]
        ps_bb = ps_x[0:16, W:2 * W]
        ps_cc = ps_x[32:48, W:2 * W]

        with nc.Block(no_gpsimd_drain=True) as blk:
            C.block = blk
            C.dma("gp", [], [
                (winb[:, :].rearrange("p (k e) -> p k e", e=2 * DI),
                 _r3(winT)),
                (xpb[:, :].rearrange("p (k e) -> p k e", e=96),
                 xpT.rearrange("(k p) e -> p k e", p=128)),
            ])
        wdep = C.snap("gd")

        p3tail = []
        prev_ev = [[], []]
        prev_ust = []
        prev_zst = []
        prev_xst = []
        prev_xlst = []
        with nc.Block(no_gpsimd_drain=True) as blk:
            C.block = blk
            for tc in range(NCH):
                tsl = slice(tc * W, (tc + 1) * W)
                s_x = C.dma("sy", p3tail, [
                    (xg[:, :].rearrange("p (k t) -> p k t", t=W),
                     _r3(xf_sc)[:, :, tsl])])
                s_ln, _ = emit_ln(
                    C, lambda k: xg[:, k * W:(k + 1) * W], xsq, st_sum,
                    st_sq, rep_a, rep_b, rows, tls,
                    s_x + p3tail + prev_xlst,
                    dst_bf_sl=lambda k: xnb[:, k * W:(k + 1) * W],
                    dst_f32_sl=lambda k: xl_t[:, k * W:(k + 1) * W],
                    gb=(lambda k: lng_sb[:, k:k + 1],
                        lambda k: lnb_sb[:, k:k + 1]))
                s_xs = C.st("dv", s_ln, lambda e:
                            e.tensor_add(xl_t[:, :], xl_t[:, :],
                                         xg[:, :]))
                prev_xlst = C.dma("sy", s_xs, [
                    (_r3(xl_sc)[:, :, tsl],
                     xl_t[:, :].rearrange("p (k t) -> p k t", t=W))])
                u_done = []
                for m in range(2 * NTI):
                    par = m % 2

                    def mmi(e, m=m, par=par):
                        last = None
                        for k in range(NT):
                            last = e.matmul(
                                mm[par],
                                winb[:, k * 2 * DI + m * 128:
                                     k * 2 * DI + (m + 1) * 128],
                                xnb[:, k * W:(k + 1) * W],
                                start=(k == 0), stop=(k == NT - 1))
                        return last
                    s_mm = C.st("pe", s_ln + wdep + prev_ev[par], mmi)
                    if m < NTI:
                        s_e = C.st("dv", s_mm, lambda e, par=par:
                                   e.tensor_copy(xinr[:, :], mm[par]))
                        s_cv, s_ut = emit_conv(
                            C, lambda k: xinr[:, :], lambda k: cacc[:, :],
                            utail[:, m * 3:(m + 1) * 3],
                            lambda kk, j, m=m: cw_sb[:, m * 4 + j:
                                                     m * 4 + j + 1],
                            s_e, 1)
                        s_u = C.st("ac", s_cv + s_ut + prev_ust,
                                   lambda e, m=m:
                                   e.activation(uu[:, :], cacc[:, :],
                                                AF.Silu,
                                                bias=cb_sb[:, m:m + 1]))
                        s_ub = C.st("dv", s_u, lambda e, m=m:
                                    e.tensor_copy(ub[:, m * W:(m + 1) * W],
                                                  uu[:, :]))
                        s_st = C.dma("sy", s_u, [
                            (u_sc[m * 128:(m + 1) * 128, tsl], uu[:, :])])
                        prev_ust = s_st
                        u_done += s_ub
                        prev_ev[par] = s_e + s_st + s_ub
                    else:
                        s_e = C.st("dv", s_mm + prev_zst, lambda e, par=par:
                                   e.tensor_copy(zz_t[:, :], mm[par]))
                        s_st = C.dma("sy", s_e, [
                            (z_sc[(m - NTI) * 128:(m - NTI + 1) * 128, tsl],
                             zz_t[:, :])])
                        prev_zst = s_st
                        prev_ev[par] = s_e + s_st

                def mk_mm(dst, lo, hi):
                    def f(e):
                        last = None
                        for k in range(NTI):
                            last = e.matmul(
                                dst, xpb[:, k * 96 + lo:k * 96 + hi],
                                ub[:, k * W:(k + 1) * W],
                                start=(k == 0), stop=(k == NTI - 1))
                        return last
                    return f
                s_m1 = C.st("pe", u_done + prev_ev[0] + prev_ev[1],
                            mk_mm(ps_dt, 0, 64))
                s_m2 = C.st("pe", [], mk_mm(ps_bb, 64, 80))
                s_m3 = C.st("pe", [], mk_mm(ps_cc, 80, 96))

                def evx(e):
                    e.tensor_copy(xdb_s[0:64, :], ps_dt)
                    e.tensor_copy(xdb_s[64:80, :], ps_bb)
                    return e.tensor_copy(xdb_c[:, :], ps_cc)
                s_ev = C.st("dv", s_m1 + s_m2 + s_m3 + prev_xst, evx)
                prev_xst = C.dma("sy", s_ev, [
                    (xdbl_sc[0:80, tsl], xdb_s[0:80, :]),
                    (xdbl_sc[80:96, tsl], xdb_c[:, :])])
                p3tail = prev_xst + s_ev
        with nc.Block(no_gpsimd_drain=True) as blk:
            C.block = blk
            C.st("sy", C.snap("sd"), lambda e: None)


def _emit_mamba_b(C, nc, xsq, xf_sc, xl_sc, u_sc, z_sc, wob_sc, dtpT,
                  xdbl_sc, a_sb, dtb_sb, dd_sb, hst_m, dbg_t, zs_sc,
                  wdep0=()):
    with contextlib.ExitStack() as PS:
        sb = lambda *a: PS.enter_context(nc.sbuf_tensor(*a))
        pst = lambda *a: PS.enter_context(nc.psum_tensor(*a))
        dtpb = sb("dtpb", [64, DI], BF16)
        dtrb = sb("dtrb", [64, W], BF16)
        brep = sb("brep", [128, NS * W], F32)
        crep = sb("crep", [128, NS * W], F32)
        bbuf = [sb("bb0", [128, 4 * W], F32), sb("bb1", [128, 4 * W], F32),
                sb("bb2", [128, 4 * W], F32)]
        ab2 = sb("ab2", [128, 4 * W], F32)
        hcb = sb("hcb0", [128, 4 * W], F32)
        dt2 = [sb("dt20", [128, W], F32), sb("dt21", [128, W], F32)]
        sp1 = sb("sp1", [128, W], F32)
        yb = [sb("yb4a", [128, NTI * W], BF16),
              sb("yb4b", [128, NTI * W], BF16)]
        u_t = [sb("u_t0", [128, W], F32), sb("u_t1", [128, W], F32)]
        z_t = [sb("z_t0", [128, W], F32), sb("z_t1", [128, W], F32)]
        dtu = sb("dtu", [128, W], F32)
        yt2 = sb("yt2", [128, W], F32)
        fixt = sb("fixt", [128, 8], F32)
        wou = [sb("wou0", [128, NTI * 128], BF16),
               sb("wou1", [128, NTI * 128], BF16)]
        xg_t = sb("xg4", [128, W], F32)
        xl_t = sb("xl4", [128, W], F32)
        spsc = sb("spsc", [128, 2 * W], F32)
        big = pst("p4ps", [128, 1024], F32).ap()
        ps_r = [big[:, 0:W], big[:, W:2 * W]]
        pso_t = [pst("p4pso0", [128, 1024], F32).ap(),
                 pst("p4pso1", [128, 1024], F32).ap()]
        ps_o = [pso_t[0][:, 0:W], pso_t[0][:, W:2 * W],
                pso_t[1][:, 0:W], pso_t[1][:, W:2 * W]]
        abuf = [xsq[:, 0:4 * W], xsq[:, 4 * W:8 * W], ab2[:, :]]

        with nc.Block(no_gpsimd_drain=True) as blk:
            C.block = blk
            C.dma("gp", [], [(dtpb[:, :], dtpT)])
        wdep = C.snap("gd")

        scan_done = [[], [], []]
        gp_done = [[], [], []]
        uld = [[], []]
        zld = [[], []]
        hs_done = [[], [], []]
        ybfree2 = [[], []]
        pend_out = [None]
        brep_rd = [[], [], [], []]
        crep_rd = [[], [], [], []]
        prev_dbg = []
        prev_dtmm = [[], []]
        s_wprev = [list(wdep0), list(wdep0)]
        prev_out = [[], [], [], []]
        ps_free = [[], []]
        ps_free4 = [[], [], [], []]
        ybd = [[]]


        with nc.Block(no_gpsimd_drain=True) as blk:
            C.block = blk
            xgp = [xg_t, spsc[:, 0:W], xl_t, spsc[:, W:2 * W]]

            def _emit_out_m(po, m):
                ptc, pybd = po
                ptsl = slice(ptc * W, (ptc + 1) * W)
                ybuf = yb[ptc % 2]
                par = m % 2
                p4r = m % 4
                s_w = C.dma("sy", s_wprev[par], [
                    (wou[par][:, :].rearrange("p (k e) -> p k e", e=128),
                     _r3(wob_sc)[:, :, m * 128:(m + 1) * 128])])

                def mmo(e, m=m, par=par, p4r=p4r, ybuf=ybuf):
                    last = None
                    for kk in range(NTI):
                        last = e.matmul(
                            ps_o[p4r],
                            wou[par][:, kk * 128:(kk + 1) * 128],
                            ybuf[:, kk * W:(kk + 1) * W],
                            start=(kk == 0), stop=(kk == NTI - 1))
                    return last
                s_mm = C.st("pe", pybd + s_w + ps_free4[p4r], mmo)
                s_wprev[par] = s_mm
                ybfree2[ptc % 2] = ybfree2[ptc % 2] + s_mm
                s_x = C.dma("sy", prev_out[p4r], [
                    (xgp[p4r][:, :],
                     xl_sc[m * 128:(m + 1) * 128, ptsl])])
                s_r2 = C.st("dv", s_mm + s_x, lambda e, p4r=p4r:
                            e.tensor_add(xgp[p4r][:, :], xgp[p4r][:, :],
                                         ps_o[p4r]))
                s_st = C.dma("ac", s_r2, [
                    (xf_sc[m * 128:(m + 1) * 128, ptsl],
                     xgp[p4r][:, :])])
                prev_out[p4r] = s_st + s_r2
                ps_free4[p4r] = s_r2

            for tc in range(NCH):
                tsl = slice(tc * W, (tc + 1) * W)
                ydep_cur = ybfree2[tc % 2]
                ybfree2[tc % 2] = []
                s_db = C.dma("gp", prev_dbg + prev_dtmm[0] + prev_dtmm[1],
                             [(dtrb[:, :], xdbl_sc[0:64, tsl])])
                def _bc(r0, n0, n1):
                    return xdbl_sc[r0 + n0:r0 + n1, tsl].rearrange(
                        "n (o t) -> o n t", o=1).broadcast_to(
                        [128, n1 - n0, W])
                s_repb = []
                s_repc = []
                for qq in range(4):
                    nsl4 = slice(qq * 4 * W, (qq + 1) * 4 * W)
                    s_repb.append(C.dma("sy", brep_rd[qq], [
                        (brep[:, nsl4].rearrange("p (n t) -> p n t", n=4),
                         _bc(64, qq * 4, qq * 4 + 4))]))
                for qq in range(4):
                    nsl4 = slice(qq * 4 * W, (qq + 1) * 4 * W)
                    s_repc.append(C.dma("sy", crep_rd[qq], [
                        (crep[:, nsl4].rearrange("p (n t) -> p n t", n=4),
                         _bc(80, qq * 4, qq * 4 + 4))]))

                def dt_build(k):
                    p2 = k % 2
                    par = k % 2

                    def mmdt(e, k=k, par=par):
                        return e.matmul(
                            ps_r[par], dtpb[:, k * 128:(k + 1) * 128],
                            dtrb[:, :], start=True, stop=True)
                    s_mm = C.st("pe", s_db + wdep + ps_free[par]
                                + prev_dtmm[par], mmdt)
                    prev_dtmm[par] = s_mm

                    def evd(e, k=k, par=par, p2=p2):
                        return e.tensor_scalar_add(
                            dt2[p2][:, :], ps_r[par], dtb_sb[:, k:k + 1])
                    s_ev = C.st("dv", s_mm + uld[p2], evd)
                    ps_free[par] = s_ev
                    s_ab = C.st("ac", s_ev + sp_free[0], lambda e, p2=p2:
                                e.activation(sp1[:, :], dt2[p2][:, :],
                                             AF.Abs))
                    s_e1 = C.st("ac", s_ab, lambda e: e.activation(
                        sp1[:, :], sp1[:, :], AF.Exp, scale=-1.0))
                    s_rl = C.st("ac", s_ab, lambda e, p2=p2:
                                e.activation(dt2[p2][:, :], dt2[p2][:, :],
                                             AF.Relu))
                    s_e2 = C.st("ac", s_e1, lambda e: e.activation(
                        sp1[:, :], sp1[:, :], AF.Ln, bias=1.0))
                    def tail(p2=p2, s_e2=s_e2, s_rl=s_rl):
                        s_ad = C.st("dv", s_e2 + s_rl, lambda e, p2=p2:
                                    e.tensor_add(dt2[p2][:, :],
                                                 dt2[p2][:, :],
                                                 sp1[:, :]))
                        sp_free[0] = s_ad
                        return s_ad
                    return tail

                sp_free = [[]]
                s_dt = [None, None]
                s_dt[0] = dt_build(0)()
                for k in range(NTI):
                    p2 = k % 2
                    s_u = C.dma("sy", uld[p2], [
                        (u_t[p2][:, :], u_sc[k * 128:(k + 1) * 128, tsl])])
                    s_zt = C.dma("sy", zld[p2], [
                        (z_t[p2][:, :], z_sc[k * 128:(k + 1) * 128, tsl])])
                    s_sz = C.st("ac", s_zt + zld[p2], lambda e, p2=p2:
                                e.activation(z_t[p2][:, :], z_t[p2][:, :],
                                             AF.Silu))
                    s_du = C.st("dv", s_u + s_dt[p2], lambda e, p2=p2:
                                e.tensor_mul(dtu[:, :], dt2[p2][:, :],
                                             u_t[p2][:, :]))

                    def emit_prep(q):
                        p = (4 * k + q) % 3
                        nsl = slice(q * 4 * W, (q + 1) * 4 * W)
                        hsl = slice(k * NS + q * 4, k * NS + q * 4 + 4)

                        def a_exp(e, k=k, q=q, p=p, p2=p2):
                            last = None
                            for j in range(4):
                                n = q * 4 + j
                                last = e.activation(
                                    abuf[p][:, j * W:(j + 1) * W],
                                    dt2[p2][:, :], AF.Exp,
                                    scale=a_sb[:, k * NS + n:
                                               k * NS + n + 1])
                            return last
                        s_a = C.st("ac", s_dt[p2] + scan_done[p], a_exp)

                        def bbody(e, p=p, q=q):
                            last = None
                            for nn in range(4):
                                last = e.tensor_mul(
                                    bbuf[p][:, nn * W:(nn + 1) * W],
                                    brep[:, (q * 4 + nn) * W:
                                         (q * 4 + nn + 1) * W],
                                    dtu[:, :])
                            return last
                        s_b = C.st("dv", s_repb[q] + s_du + hs_done[p],
                                   bbody)
                        brep_rd[q] = s_b
                        acols = abuf[p][:, :].rearrange(
                            "p (n t) -> p n t", n=4)[:, :, 0:1]
                        bcols = bbuf[p][:, :].rearrange(
                            "p (n t) -> p n t", n=4)[:, :, 0:1]
                        ecols = bbuf[p][:, :].rearrange(
                            "p (n t) -> p n t", n=4)[:, :, W - 1:W]
                        hcols = hst_m[:, hsl].rearrange(
                            "p (a o) -> p a o", o=1)
                        fo = 4 * (q % 2)
                        ytmp = fixt[:, fo:fo + 4].rearrange(
                            "p (a o) -> p a o", o=1)
                        s_f1 = C.st("gp", s_a + s_b, lambda e, acols=acols,
                                    hcols=hcols, ytmp=ytmp:
                                    e.tensor_mul(ytmp, acols, hcols))
                        s_f2 = C.st("gp", s_f1, lambda e, bcols=bcols,
                                    ytmp=ytmp:
                                    e.tensor_add(bcols, bcols, ytmp))
                        s_f3 = C.st("gp", s_f2, lambda e, acols=acols:
                                    e.memset(acols, 0.0))
                        return (p, nsl, s_f3, ecols, hcols)

                    prep = [emit_prep(0), emit_prep(1), emit_prep(2),
                            None]
                    dt_tail = dt_build(k + 1) if k + 1 < NTI else None
                    for q in range(4):
                        if q == 2 and dt_tail is not None:
                            s_dt[(k + 1) % 2] = dt_tail()
                        if q == 3:
                            prep[3] = emit_prep(3)
                        p, nsl, s_f3, ecols, hcols = prep[q]
                        s_scan = C.st("dv", s_f3,
                                      lambda e, p=p:
                                      e.tensor_tensor_scan(
                                          bbuf[p][:, :], abuf[p][:, :],
                                          bbuf[p][:, :], 0.0, ALU.mult,
                                          ALU.add))
                        scan_done[p] = s_scan
                        hs_done[p] = C.st("gp", s_scan,
                                          lambda e, hcols=hcols,
                                          ecols=ecols:
                                          e.tensor_copy(hcols, ecols))
                        s_hc = C.st("dv", s_repc[q],
                                    lambda e, p=p, nsl=nsl:
                                    e.tensor_mul(hcb[:, :], bbuf[p][:, :],
                                                 crep[:, nsl]))
                        gp_done[p] = s_hc
                        crep_rd[q] = s_hc
                        C.st("dv", [], lambda e:
                             e.tensor_add(hcb[:, 0:2 * W], hcb[:, 0:2 * W],
                                          hcb[:, 2 * W:4 * W]))
                        if q == 0:
                            C.st("dv", [], lambda e:
                                 e.tensor_add(yt2[:, :], hcb[:, 0:W],
                                              hcb[:, W:2 * W]))
                        else:
                            C.st("dv", [], lambda e:
                                 e.tensor_add(yt2[:, :], yt2[:, :],
                                              hcb[:, 0:W]))
                            C.st("dv", [], lambda e:
                                 e.tensor_add(yt2[:, :], yt2[:, :],
                                              hcb[:, W:2 * W]))
                    s_y3 = C.st("dv", [], lambda e, k=k, p2=p2:
                                e.scalar_tensor_tensor(
                                    yt2[:, :], u_t[p2][:, :],
                                    dd_sb[:, k:k + 1], yt2[:, :],
                                    ALU.mult, ALU.add))
                    s_y4 = C.st("dv", s_y3 + s_sz + ydep_cur,
                                lambda e, k=k, p2=p2, tc=tc:
                                e.tensor_mul(
                                    yb[tc % 2][:, k * W:(k + 1) * W],
                                    yt2[:, :], z_t[p2][:, :]))
                    uld[p2] = s_y3
                    zld[p2] = s_y4
                    ybd[0] = s_y4
                    if pend_out[0] is not None and k < NT:
                        _emit_out_m(pend_out[0], k)
                pend_out[0] = (tc, ybd[0])
                if "dtv" in dbg_t:
                    prev_dbg = []
            for m in range(NT):
                _emit_out_m(pend_out[0], m)
        with nc.Block(no_gpsimd_drain=True) as blk:
            C.block = blk
            C.st("sy", C.snap("sd"), lambda e: None)


def _dump(C, nc, src, dst, rows_n, tag):
    with contextlib.ExitStack() as ES:
        tmp = ES.enter_context(nc.sbuf_tensor(f"dump_{tag}", [128, T], F32))
        with nc.Block(no_gpsimd_drain=True) as blk:
            C.block = blk
            prev = []
            for r in range(rows_n // 128):
                s = C.dma("sy", prev, [(tmp[:, :],
                                        src[r * 128:(r + 1) * 128, :])])
                prev = C.dma("sy", s, [(dst[r * 128:(r + 1) * 128, :],
                                        tmp[:, :])])
            C.st("sy", C.snap("sd"), lambda e: None)


def _vec(v, nk):
    v = np.asarray(v)
    if v.ndim == 1:
        return np.ascontiguousarray(v.reshape(nk, 128).T)
    c = v.shape[1]
    return np.ascontiguousarray(
        v.reshape(nk, 128, c).transpose(1, 0, 2).reshape(128, nk * c))


def prep_inputs(inputs):
    x = np.asarray(inputs["L0_attn_a_x"])
    base = {
        "wi": np.asarray(inputs["L0_attn_a_Wi"]),
        "wf": np.asarray(inputs["L0_attn_a_Wf"]),
        "wg": np.asarray(inputs["L0_attn_a_Wg"]),
        "wo": np.asarray(inputs["L0_attn_a_Wo"]),
        "cvi": _vec(inputs["L0_attn_a_conv_i"], NT),
        "cvf": _vec(inputs["L0_attn_a_conv_f"], NT),
        "gnw": _vec(inputs["L0_attn_a_g_norm_weight"], NT),
        "lng": _vec(inputs["L0_attn_b_ln_gamma"], NT),
        "lnb": _vec(inputs["L0_attn_b_ln_beta"], NT),
        "winT": np.ascontiguousarray(
            np.asarray(inputs["L0_attn_b_in_proj_w"]).T),
        "cw": _vec(inputs["L0_attn_b_conv_w"], NTI),
        "cb": _vec(inputs["L0_attn_b_conv_b"], NTI),
        "xpT": np.ascontiguousarray(
            np.asarray(inputs["L0_attn_b_x_proj_w"]).T),
        "dtpT": np.ascontiguousarray(
            np.asarray(inputs["L0_attn_b_dt_proj_w"]).T),
        "dtb": _vec(inputs["L0_attn_b_dt_proj_b"], NTI),
        "alog": _vec(inputs["L0_attn_b_A_log"], NTI),
        "dd": _vec(inputs["L0_attn_b_D"], NTI),
        "woutT": np.ascontiguousarray(
            np.asarray(inputs["L0_attn_b_out_proj_w"]).T),
        "mk": np.asarray(inputs["L0_ffn_rail_maker"]),
        "mkT": np.ascontiguousarray(
            np.asarray(inputs["L0_ffn_rail_maker"]).transpose(0, 1, 3, 2)),
        "ruT": np.ascontiguousarray(
            np.asarray(inputs["L0_ffn_rail_up_proj"]).transpose(0, 1, 3, 2)),
        "wcT": np.ascontiguousarray(np.asarray(inputs["L0_ffn_w_cproj"]).T),
    }
    in_maps = []
    for b in range(8):
        m = dict(base)
        m["xf"] = np.ascontiguousarray(x[b].T)
        in_maps.append(m)
    return in_maps


def kernel(**inputs):
    in_maps = prep_inputs(inputs)
    nc = build_program()
    res = run_bass_kernel_spmd(nc, in_maps, core_ids=list(range(8)))
    return np.stack([r["y"] for r in res.results], axis=0)


# version 24
# speedup vs baseline: 1.0420x; 1.0139x over previous
import contextlib
import numpy as np
import concourse.bass as bass
import concourse.mybir as mybir
from concourse.bass_utils import run_bass_kernel_spmd

F32 = mybir.dt.float32
BF16 = mybir.dt.bfloat16
AF = mybir.ActivationFunctionType
ALU = mybir.AluOpType

T = 4096
D = 1024
W = 512
NCH = T // W
NT = D // 128
DI = 2048
NTI = DI // 128
NS = 16
GW = 1024
NG = T // GW
LN_EPS = 1e-5

_ENG = {"pe": "tensor", "dv": "vector", "ac": "scalar", "gp": "gpsimd",
        "sy": "sync"}


class Ctx:

    def __init__(self, nc):
        self.nc = nc
        self.block = None
        self.sems = {}

    def init_sems(self, stack):
        for name in ("pe", "dv", "ac", "gp", "sy"):
            h = stack.enter_context(self.nc.semaphore(f"sem_{name}"))
            self.sems[name] = [h, 0]
        self.lanes = {}
        self.lane_rr = {}
        for q, nl in (("sd", 8), ("gd", 6), ("ad", 4)):
            self.lanes[q] = []
            self.lane_rr[q] = 0
            for i in range(nl):
                h = stack.enter_context(self.nc.semaphore(f"lane_{q}{i}"))
                self.lanes[q].append([h, 0])

    def snap(self, *names):
        out = []
        for n in names:
            if n in self.sems:
                out.append((self.sems[n][0], self.sems[n][1]))
            else:
                out += [(h, v) for h, v in self.lanes[n]]
        return out

    def _waits(self, eng, deps):
        merged = {}
        for s, v in deps:
            if v > 0 and merged.get(s, 0) < v:
                merged[s] = v
        for s, v in merged.items():
            eng.wait_ge(s, v)

    def st(self, sel, deps, emit):
        sem, _ = self.sems[sel]

        def f(eng):
            self._waits(eng, deps)
            last = emit(eng)
            if last is not None:
                last.then_inc(sem, 1)
        getattr(self.block, _ENG[sel])(f)
        self.sems[sel][1] += 1
        return self.snap(sel)

    def dma(self, sel, deps, pairs):
        q = {"sy": "sd", "gp": "gd", "ac": "ad"}[sel]
        lanes = self.lanes[q]
        plan = []
        for _ in pairs:
            li = self.lane_rr[q] % len(lanes)
            self.lane_rr[q] += 1
            lane = lanes[li]
            plan.append((lane, lane[1], lane[1] + 16))
            lane[1] += 16
        out_snap = [(lane[0], after) for lane, _, after in plan]

        def f(eng):
            self._waits(eng, deps)
            for (lane, before, _), (out, in_) in zip(plan, pairs):
                if before > 0:
                    eng.wait_ge(lane[0], before)
                eng.dma_start(out=out, in_=in_).then_inc(lane[0], 16)
        getattr(self.block, _ENG[sel])(f)
        return out_snap


def emit_ln(C, src_sl, xsq, st_sum, st_sq, rep_a, rep_b, rows, tls, dep, *,
            rms=False, dst_bf_sl=None, dst_f32_sl=None, gb=None, n_k=NT):
    inv_n = 1.0 / (n_k * 128)
    ones1, onesr = C.ones1, C.onesr
    sum_row = rows[0:1, 0:W]
    sq_row = rows[0:1, W:2 * W]
    a_row = rows[0:1, 2 * W:3 * W]
    b_row = rows[0:1, 3 * W:4 * W]

    def sq(e):
        last = None
        for k in range(n_k):
            last = e.activation(xsq[:, k * W:(k + 1) * W], src_sl(k),
                                AF.Square)
        return last
    s_sq = C.st("ac", dep, sq)

    def mm_stats(e):
        last = None
        if not rms:
            for k in range(n_k):
                last = e.matmul(st_sum, ones1[:, 0:1], src_sl(k),
                                start=(k == 0), stop=(k == n_k - 1))
        for k in range(n_k):
            last = e.matmul(st_sq, ones1[:, 0:1],
                            xsq[:, k * W:(k + 1) * W],
                            start=(k == 0), stop=(k == n_k - 1))
        return last
    s_mm = C.st("pe", dep + s_sq, mm_stats)

    if rms:
        s = C.st("dv", s_mm, lambda e: e.tensor_copy(sq_row, st_sq))
        s = C.st("dv", s, lambda e: e.tensor_scalar_mul(sq_row, sq_row,
                                                        inv_n))
        s = C.st("ac", s, lambda e: e.activation(a_row, sq_row, AF.Ln,
                                                 bias=C.eps1))
        s = C.st("ac", s, lambda e: e.activation(a_row, a_row, AF.Exp,
                                                 scale=C.mhalf1))
        s_rep = C.st("pe", s, lambda e: e.matmul(rep_a, onesr[0:1, :], a_row,
                                                 start=True, stop=True))
        done = []
        for k in range(n_k):
            done += C.st("dv", s_rep, lambda e, k=k: e.tensor_mul(
                dst_f32_sl(k), src_sl(k), rep_a))
        return done, s

    def cps(e):
        e.tensor_copy(sum_row, st_sum)
        return e.tensor_copy(sq_row, st_sq)
    s = C.st("dv", s_mm, cps)

    def mumath(e):
        e.tensor_scalar_mul(sum_row, sum_row, inv_n)
        return e.tensor_scalar_mul(sq_row, sq_row, inv_n)
    s = C.st("dv", s, mumath)
    s = C.st("dv", s, lambda e: e.tensor_mul(b_row, sum_row, sum_row))
    s = C.st("dv", s, lambda e: e.tensor_sub(sq_row, sq_row, b_row))
    s = C.st("ac", s, lambda e: e.activation(a_row, sq_row, AF.Ln,
                                             bias=C.eps1))
    s = C.st("ac", s, lambda e: e.activation(a_row, a_row, AF.Exp,
                                             scale=C.mhalf1))
    s = C.st("dv", s, lambda e: e.tensor_mul(b_row, sum_row, a_row))
    s = C.st("dv", s, lambda e: e.tensor_scalar_mul(b_row, b_row, -1.0))

    def mmrep(e):
        e.matmul(rep_a, onesr[0:1, :], a_row, start=True, stop=True)
        return e.matmul(rep_b, onesr[0:1, :], b_row, start=True, stop=True)
    s_rep = C.st("pe", s, mmrep)

    done = []
    for k in range(n_k):
        s_t = C.st("dv", s_rep, lambda e, k=k: e.tensor_mul(
            tls[:, 0:W], src_sl(k), rep_a))
        if dst_f32_sl is not None:
            s_x = C.st("dv", s_t, lambda e, k=k: e.tensor_add(
                dst_f32_sl(k), tls[:, 0:W], rep_b))
            if dst_bf_sl is not None:
                g, b = gb
                s_x = C.st("dv", s_x, lambda e, k=k: e.tensor_scalar(
                    dst_bf_sl(k), dst_f32_sl(k), g(k), b(k), ALU.mult,
                    ALU.add))
        else:
            s_x = C.st("dv", s_t, lambda e, k=k: e.tensor_add(
                dst_bf_sl(k), tls[:, 0:W], rep_b))
        done += s_x
    return done, s_rep


def emit_conv(C, raw_sl, acc_sl, tail, wcol, dep, n_k):
    def tap3(e):
        last = None
        for k in range(n_k):
            last = e.tensor_scalar_mul(acc_sl(k), raw_sl(k), wcol(k, 3))
        return last
    s = C.st("dv", dep, tap3)
    for j in (2, 1, 0):
        sft = 3 - j

        def tapj(e, j=j, sft=sft):
            last = None
            for k in range(n_k):
                acc = acc_sl(k)
                raw = raw_sl(k)
                last = e.scalar_tensor_tensor(
                    acc[:, sft:W], raw[:, 0:W - sft], wcol(k, j),
                    acc[:, sft:W], ALU.mult, ALU.add)
                last = e.scalar_tensor_tensor(
                    acc[:, 0:sft], tail[:, k * 3 + 3 - sft:k * 3 + 3],
                    wcol(k, j), acc[:, 0:sft], ALU.mult, ALU.add)
            return last
        s = C.st("dv", s, tapj)

    def upd(e):
        last = None
        for k in range(n_k):
            last = e.tensor_copy(tail[:, k * 3:(k + 1) * 3],
                                 raw_sl(k)[:, W - 3:W])
        return last
    s_tail = C.st("dv", s, upd)
    return s, s_tail


def _r3(ap, p=128):
    return ap.rearrange("(k p) t -> p k t", p=p)


def _emit_ffn(C, nc, ST, xsq, mkT, ruT, wcb, xf_sc, y, identf,
              dbg_dst=None, tag="", wdep0=()):
    with contextlib.ExitStack() as PS:
        sb = lambda *a: PS.enter_context(nc.sbuf_tensor(*a))
        pst = lambda *a: PS.enter_context(nc.psum_tensor(*a))
        rub = sb("rub" + tag, [128, 64 * 128], BF16)

        with contextlib.ExitStack() as QS:
            mtf = QS.enter_context(nc.sbuf_tensor("mtf" + tag,
                                                  [128, 64 * 128], F32))
            ruf = QS.enter_context(nc.sbuf_tensor("ruf" + tag, [128, 64 * 128],
                                                  F32))
            psW = QS.enter_context(nc.psum_tensor("psW" + tag,
                                                  [128, 256], F32))
            psWa = psW.ap()
            with nc.Block(no_gpsimd_drain=True) as blk:
                C.block = blk

                def zz(e):
                    e.memset(mtf[:, :], 0.0)
                    return e.memset(ruf[:, :], 0.0)
                s_z = C.st("gp", [], zz)
                pairs_sy = []
                pairs_ac = []
                for hd in range(4):
                    for m in range(32):
                        slot = 2 * (hd * 8 + m // 4) + (m % 2)
                        off = slot * 128
                        pbase = 64 * ((m // 2) % 2)
                        for c in range(2):
                            rail = 4 * (m // 2) + 2 * (m % 2) + c
                            rr = 2 * (m % 2) + c
                            pairs_sy.append((
                                mtf[pbase + rr * 16:pbase + (rr + 1) * 16,
                                    off + pbase + rr * 16:
                                    off + pbase + (rr + 1) * 16],
                                mkT[hd, rail]))
                            pairs_ac.append((
                                ruf[pbase + rr * 16:pbase + (rr + 1) * 16,
                                    off + c * 64:off + (c + 1) * 64],
                                ruT[hd, rail]))
                half = len(pairs_ac) // 2
                s_d1 = C.dma("sy", s_z, pairs_sy + pairs_ac[:half])
                s_d2 = C.dma("ac", s_z, pairs_ac[half:])
                ev_prev = [[], []]
                for slot in range(64):
                    off = slot * 128
                    par = slot % 2
                    psl = psWa[:, par * 128:(par + 1) * 128]

                    def mmw(e, off=off, psl=psl):
                        return e.matmul(psl, mtf[:, off:off + 128],
                                        ruf[:, off:off + 128],
                                        start=True, stop=True)
                    s_mw = C.st("pe", s_d1 + s_d2 + ev_prev[par], mmw)
                    s_ev = C.st("dv", s_mw, lambda e, off=off, psl=psl:
                                e.tensor_copy(rub[:, off:off + 128], psl))
                    ev_prev[par] = s_ev
                s_wb = ev_prev[0] + ev_prev[1]

        rows = sb("rows_f" + tag, [1, 4 * W], F32)
        tls = sb("tls_f" + tag, [128, W], F32)
        xnb = sb("xnb_f" + tag, [128, NT * GW], BF16)
        hh = sb("hh" + tag, [128, 32 * GW], BF16)
        wcf = [sb("wcf0" + tag, [128, 32 * 128], BF16),
               sb("wcf1" + tag, [128, 32 * 128], BF16)]
        oac = sb("oac" + tag, [128, NT * GW], F32)
        xgs = sb("xgs" + tag, [128, NT * W], F32)
        zev = sb("zev" + tag, [128, GW], F32)
        rsx = zev
        ytok = zev

        big = [pst(f"fps{i}" + tag, [128, 1024], F32).ap()
               for i in range(4)]
        st_sum = big[0][0:1, 0:W]
        st_sq = big[0][0:1, W:2 * W]
        rep_a = big[1][:, 0:W]
        rep_b = big[1][:, W:2 * W]
        ps_y = big[2]
        ps_zs2 = [big[0], big[1], big[3], big[2]]
        ps_cs = [big[3], big[2]]
        ps_tr = big[1]

        ffn_tail = []
        last_maker = []
        s_ye_prev = [[], []]
        ec_prev = [[], []]
        wc_prev = [list(wdep0), list(wdep0)]
        s_gel_prev2 = [[], []]
        prev_half = []
        with nc.Block(no_gpsimd_drain=True) as blk:
            C.block = blk
            for g in range(NG):
                s_gel_prev = s_gel_prev + ffn_tail if g else []
                s_ln_all = []
                for hf in range(2):
                    c0 = g * GW + hf * W
                    s_x = C.dma("sy", prev_half + last_maker, [
                        (xgs[:, :].rearrange("p (k t) -> p k t", t=W),
                         _r3(xf_sc)[:, :, c0:c0 + W])])
                    psdep = (s_gel_prev + ec_prev[0] + ec_prev[1]
                             + ffn_tail) if g else []
                    s_ln, s_lrep = emit_ln(
                        C, lambda k: xgs[:, k * W:(k + 1) * W], xsq,
                        st_sum, st_sq, rep_a, rep_b, rows, tls,
                        s_x + psdep,
                        dst_bf_sl=lambda k: xnb[:, k * GW + hf * W:
                                                k * GW + hf * W + W])
                    s_ln_all += s_ln
                    prev_half = s_ln + s_lrep
                gel_last = []
                gelq = []
                s_gel_prev = s_gel_prev if g else []
                s_gel_prev2 = [s_ln_all + s_gel_prev2[0] + s_wb,
                               s_ln_all + s_gel_prev2[1] + s_wb,
                               s_ln_all + (s_gel_prev2[2]
                                           if len(s_gel_prev2) > 2 else
                                           ec_prev[0]) + s_wb,
                               s_ln_all + (s_gel_prev2[3]
                                           if len(s_gel_prev2) > 3 else
                                           ec_prev[1]) + s_wb]
                for hd in range(4):
                    for rg in range(8):
                        for mq in range(4):
                            m = rg * 4 + mq
                            q = m // 2
                            rbase = (q % 2) * 64
                            zpar = (hd * 32 + m) % 4

                            def mmz(e, m=m, hd=hd, rg=rg, rbase=rbase,
                                    zpar=zpar):
                                slot = 2 * (hd * 8 + m // 4) + (m % 2)
                                off = slot * 128
                                last = None
                                for h2 in range(2):
                                    last = e.matmul(
                                        ps_zs2[zpar][:, h2 * W:(h2 + 1) * W],
                                        rub[rbase:rbase + 64,
                                            off:off + 128],
                                        xnb[rbase:rbase + 64,
                                            rg * GW + h2 * W:
                                            rg * GW + (h2 + 1) * W],
                                        start=True, stop=True)
                                return last
                            s_mz = C.st("pe", s_ln_all + s_gel_prev2[zpar],
                                        mmz)
                            last_maker = s_mz
                            gelq.append((m, zpar, s_mz))
                            if len(gelq) == 3 or m == 31:
                                batch = list(gelq)
                                gelq.clear()

                                def gtri(e, batch=batch):
                                    last = None
                                    for (gm, gz, _) in batch:
                                        last = e.activation(
                                            hh[:, gm * GW:(gm + 1) * GW],
                                            ps_zs2[gz], AF.Gelu)
                                    return last
                                gdep = []
                                for (_, _, d) in batch:
                                    gdep += d
                                s_gel = C.st("ac", gdep, gtri)
                                s_gel_prev = s_gel
                                for (_, gz, _) in batch:
                                    s_gel_prev2[gz] = (s_gel_prev2[gz]
                                                       + s_gel)
                    gel_last = s_gel_prev
                    for et in range(NT):
                        par = et % 2
                        s_w = C.dma("sy", wc_prev[par], [
                            (wcf[par][:, :].rearrange(
                                "p (m e) -> p m e", e=128),
                             wcb[hd * 4096:(hd + 1) * 4096,
                                 et * 128:(et + 1) * 128].rearrange(
                                     "(m p) e -> p m e", p=128))])

                        def mmc(e, par=par):
                            last = None
                            for m in range(32):
                                for h2 in range(2):
                                    last = e.matmul(
                                        ps_cs[par][:, h2 * W:(h2 + 1) * W],
                                        wcf[par][:, m * 128:(m + 1) * 128],
                                        hh[:, m * GW + h2 * W:
                                           m * GW + (h2 + 1) * W],
                                        start=(m == 0), stop=(m == 31))
                            return last
                        dep_mc = gel_last + s_w + ec_prev[par]
                        if par == 0:
                            dep_mc = dep_mc + s_gel_prev2[2]
                        else:
                            dep_mc = dep_mc + s_gel_prev2[3]
                        s_mc = C.st("pe", dep_mc, mmc)
                        wc_prev[par] = s_mc
                        if hd == 0:
                            s_ec = C.st("dv", s_mc, lambda e, et=et, par=par:
                                        e.tensor_copy(
                                            oac[:, et * GW:(et + 1) * GW],
                                            ps_cs[par]))
                        else:
                            s_ec = C.st("dv", s_mc, lambda e, et=et, par=par:
                                        e.tensor_add(
                                            oac[:, et * GW:(et + 1) * GW],
                                            oac[:, et * GW:(et + 1) * GW],
                                            ps_cs[par]))
                        ec_prev[par] = s_ec
                    s_gel_prev2[2] = s_gel_prev2[2] + ec_prev[0]
                    s_gel_prev2[3] = s_gel_prev2[3] + ec_prev[1]
                s_tr_prev = [[], []]
                for et in range(NT):
                    s_xe = C.dma("sy", ec_prev[0] + ec_prev[1] + ffn_tail, [
                        (rsx[:, :],
                         xf_sc[et * 128:(et + 1) * 128,
                               g * GW:(g + 1) * GW])])
                    s_r = C.st("dv", s_xe + ec_prev[et % 2], lambda e, et=et:
                               e.tensor_add(oac[:, et * GW:(et + 1) * GW],
                                            oac[:, et * GW:(et + 1) * GW],
                                            rsx[:, :]))
                    if y is None:
                        s_so = C.dma("sy", s_r, [
                            (xf_sc[et * 128:(et + 1) * 128,
                                   g * GW:(g + 1) * GW],
                             oac[:, et * GW:(et + 1) * GW])])
                        if dbg_dst is not None:
                            s_so = s_so + C.dma("sy", s_r, [
                                (dbg_dst[et * 128:(et + 1) * 128,
                                         g * GW:(g + 1) * GW],
                                 oac[:, et * GW:(et + 1) * GW])])
                        ec_prev[et % 2] = s_r + s_so
                        ffn_tail = s_r + s_so
                    else:
                        ec_prev[et % 2] = s_r
                if y is not None:
                    res_done = ec_prev[0] + ec_prev[1]
                    s_yst = []
                    for sv in range(8):
                        s_cp_all = []
                        for et in range(NT):
                            par = et % 2
                            tsl2 = slice(par * 512, par * 512 + 128)

                            def tr(e, et=et, sv=sv, tsl2=tsl2):
                                return e.transpose(
                                    ps_tr[:, tsl2],
                                    oac[:, et * GW + sv * 128:
                                        et * GW + (sv + 1) * 128],
                                    identf[:, :])
                            s_t = C.st("pe", res_done + s_tr_prev[par], tr)
                            s_cp = C.st("dv", s_t + s_yst,
                                        lambda e, et=et, tsl2=tsl2:
                                        e.tensor_copy(
                                            ytok[:, et * 128:(et + 1) * 128],
                                            ps_tr[:, tsl2]))
                            s_tr_prev[par] = s_cp
                            s_cp_all += s_cp
                        s_yst = C.dma("sy", s_cp_all, [
                            (y[g * GW + sv * 128:g * GW + (sv + 1) * 128, :],
                             ytok[:, :])])
                    ffn_tail = s_yst + s_tr_prev[0] + s_tr_prev[1]
        with nc.Block(no_gpsimd_drain=True) as blk:
            C.block = blk
            C.st("sy", C.snap("sd") + C.snap("ad"), lambda e: None)


def build_program(dbg=(), phases=("p1", "p2", "p3", "p4", "p5")):
    nc = bass.Bass("TRN2", target_bir_lowering=False)
    C = Ctx(nc)

    xf = nc.dram_tensor("xf", [D, T], F32, kind="ExternalInput").ap()
    wi = nc.dram_tensor("wi", [D, D], F32, kind="ExternalInput").ap()
    wf_ = nc.dram_tensor("wf", [D, D], F32, kind="ExternalInput").ap()
    wg = nc.dram_tensor("wg", [D, D], F32, kind="ExternalInput").ap()
    wo = nc.dram_tensor("wo", [D, D], F32, kind="ExternalInput").ap()
    cvi = nc.dram_tensor("cvi", [128, NT * 4], F32,
                         kind="ExternalInput").ap()
    cvf = nc.dram_tensor("cvf", [128, NT * 4], F32,
                         kind="ExternalInput").ap()
    gnw = nc.dram_tensor("gnw", [128, NT], F32, kind="ExternalInput").ap()
    lng = nc.dram_tensor("lng", [128, NT], F32, kind="ExternalInput").ap()
    lnb = nc.dram_tensor("lnb", [128, NT], F32, kind="ExternalInput").ap()
    winT = nc.dram_tensor("winT", [D, 2 * DI], F32,
                          kind="ExternalInput").ap()
    cw = nc.dram_tensor("cw", [128, NTI * 4], F32,
                        kind="ExternalInput").ap()
    cb = nc.dram_tensor("cb", [128, NTI], F32, kind="ExternalInput").ap()
    xpT = nc.dram_tensor("xpT", [DI, 96], F32, kind="ExternalInput").ap()
    dtpT = nc.dram_tensor("dtpT", [64, DI], F32, kind="ExternalInput").ap()
    dtb = nc.dram_tensor("dtb", [128, NTI], F32, kind="ExternalInput").ap()
    alog = nc.dram_tensor("alog", [128, NTI * NS], F32,
                          kind="ExternalInput").ap()
    dd = nc.dram_tensor("dd", [128, NTI], F32, kind="ExternalInput").ap()
    woutT = nc.dram_tensor("woutT", [DI, D], F32, kind="ExternalInput").ap()
    mk = nc.dram_tensor("mk", [4, 64, 16, 16], F32,
                        kind="ExternalInput").ap()
    mkT = nc.dram_tensor("mkT", [4, 64, 16, 16], F32,
                         kind="ExternalInput").ap()
    ruT = nc.dram_tensor("ruT", [4, 64, 16, 64], F32,
                         kind="ExternalInput").ap()
    wcT = nc.dram_tensor("wcT", [16 * D, D], F32, kind="ExternalInput").ap()

    y = nc.dram_tensor("y", [T, D], F32, kind="ExternalOutput").ap()
    dbg_t = {}
    for name in dbg:
        shp = {"x1": [D, T], "x2": [D, T], "x3": [D, T], "xl": [D, T],
               "u": [DI, T], "z": [DI, T], "dtv": [DI, T],
               "xdbl": [128, T]}[name]
        dbg_t[name] = nc.dram_tensor("dbg_" + name, shp, F32,
                                     kind="ExternalOutput").ap()

    xf_sc = nc.dram_tensor("xf_sc", [D, T], F32, kind="Internal").ap()
    xl_sc = nc.dram_tensor("xl_sc", [D, T], F32, kind="Internal").ap()
    u_sc = nc.dram_tensor("u_sc", [DI, T], F32, kind="Internal").ap()
    z_sc = nc.dram_tensor("z_sc", [DI, T], F32, kind="Internal").ap()
    wcb = nc.dram_tensor("wcb", [16 * D, D], BF16, kind="Internal").ap()
    wob_sc = nc.dram_tensor("wob_sc", [DI, D], BF16, kind="Internal").ap()
    zs_sc = nc.dram_tensor("zs_sc", [DI, W], F32, kind="Internal").ap()
    xdbl_sc = nc.dram_tensor("xdbl_sc", [96, T], F32, kind="Internal").ap()

    with contextlib.ExitStack() as ST:
        C.init_sems(ST)
        sb = lambda *a: ST.enter_context(nc.sbuf_tensor(*a))

        ones1 = sb("ones1", [128, 1], F32)
        onesr = sb("onesr", [1, 128], F32)
        cst = sb("cst", [128, 2], F32)
        C.ones1, C.onesr = ones1, onesr
        C.eps1 = cst[0:1, 0:1]
        C.mhalf1 = cst[0:1, 1:2]
        identf = sb("identf", [128, 128], F32)
        cvi_sb = sb("cvi_sb", [128, NT * 4], F32)
        cvf_sb = sb("cvf_sb", [128, NT * 4], F32)
        gnw_sb = sb("gnw_sb", [128, NT], F32)
        lng_sb = sb("lng_sb", [128, NT], F32)
        lnb_sb = sb("lnb_sb", [128, NT], F32)
        cw_sb = sb("cw_sb", [128, NTI * 4], F32)
        cb_sb = sb("cb_sb", [128, NTI], F32)
        dtb_sb = sb("dtb_sb", [128, NTI], F32)
        dd_sb = sb("dd_sb", [128, NTI], F32)
        a_sb = sb("a_sb", [128, NTI * NS], F32)
        hst_g = sb("hst_g", [128, NT], F32)
        hst_m = sb("hst_m", [128, NTI * NS], F32)
        itail = sb("itail", [128, NT * 3], F32)
        ftail = sb("ftail", [128, NT * 3], F32)
        utail = sb("utail", [128, NTI * 3], F32)
        xsq = sb("xsq", [128, NT * W], F32)

        with nc.Block(no_gpsimd_drain=True) as blk:
            C.block = blk
            ld = C.dma("sy", [], [
                (cvi_sb[:, :], cvi), (cvf_sb[:, :], cvf),
                (gnw_sb[:, :], gnw), (lng_sb[:, :], lng),
                (lnb_sb[:, :], lnb), (cw_sb[:, :], cw),
                (cb_sb[:, :], cb), (dtb_sb[:, :], dtb),
                (dd_sb[:, :], dd), (a_sb[:, :], alog),
            ])
            s_a1 = C.st("ac", ld, lambda e: e.activation(a_sb[:, :],
                                                         a_sb[:, :], AF.Exp))
            C.st("dv", s_a1, lambda e: e.tensor_scalar_mul(
                a_sb[:, :], a_sb[:, :], -1.0))

            def zeros(e):
                e.memset(hst_g[:, :], 0.0)
                e.memset(hst_m[:, :], 0.0)
                e.memset(itail[:, :], 0.0)
                e.memset(ftail[:, :], 0.0)
                e.memset(utail[:, :], 0.0)
                e.memset(ones1[:, :], 1.0)
                e.memset(cst[:, 0:1], LN_EPS)
                e.memset(cst[:, 1:2], -0.5)
                return e.memset(onesr[:, :], 1.0)
            C.st("dv", [], zeros)

            def gpinit(gpsimd):
                sem = C.sems["gp"][0]
                cnt = C.sems["gp"][1]
                gpsimd.memset(identf[:, :], 0.0).then_inc(sem, 1)
                cnt += 1
                gpsimd.wait_ge(sem, cnt)
                gpsimd.affine_select(
                    out=identf[:, :], in_=identf[:, :],
                    compare_op=ALU.not_equal, fill=1.0, base=0,
                    pattern=[[-1, 128]], channel_multiplier=1).then_inc(
                        sem, 1)
                cnt += 1
                C.sems["gp"][1] = cnt
            blk.gpsimd(gpinit)

        need_wcb = ("p2" in phases) or ("p5" in phases) or ("p4" in phases)
        wcb_done = []
        if need_wcb:
            with nc.Block(no_gpsimd_drain=True) as blk:
                C.block = blk
                for r in range(16):
                    wcb_done += C.dma("gp", [], [
                        (wcb[r * 1024:(r + 1) * 1024, :],
                         wcT[r * 1024:(r + 1) * 1024, :])])
                wcb_done += C.dma("gp", [], [(wob_sc, woutT)])

        if "p1" in phases:
            with contextlib.ExitStack() as PS:
                psb = lambda *a: PS.enter_context(nc.sbuf_tensor(*a))
                ppt = lambda *a: PS.enter_context(nc.psum_tensor(*a))
                wib = psb("wib", [128, NT * D], BF16)
                wfb = psb("wfb", [128, NT * D], BF16)
                wgb = psb("wgb", [128, NT * D], BF16)
                wob = psb("wob", [128, NT * D], BF16)
                rows = psb("rows", [1, 4 * W], F32)
                tls = psb("tls", [128, W], F32)
                xg = psb("xg", [128, NT * W], F32)
                xnb = psb("xnb", [128, NT * W], BF16)
                iraw = psb("iraw", [128, NT * W], F32)
                fraw = psb("fraw", [128, NT * W], F32)
                icv = psb("icv", [128, NT * W], F32)
                fcv = psb("fcv", [128, NT * W], F32)
                ob = psb("ob", [128, NT * W], BF16)
                big = [ppt(f"p1ps{i}", [128, 1024], F32).ap()
                       for i in range(2)]
                st_sum = big[0][0:1, 0:W]
                st_sq = big[0][0:1, W:2 * W]
                rep_a = big[1][:, 0:W]
                rep_b = big[1][:, W:2 * W]
                mm_ps = [ppt("p1mm", [128, 1024], F32).ap(),
                         ppt("p1mm2", [128, 1024], F32).ap()]
                mm = [mm_ps[0][:, 0:W], mm_ps[0][:, W:2 * W],
                      mm_ps[1][:, 0:W], mm_ps[1][:, W:2 * W]]

                with nc.Block(no_gpsimd_drain=True) as blk:
                    C.block = blk
                    C.dma("gp", [], [
                        (wib[:, :].rearrange("p (k e) -> p k e", e=D),
                         _r3(wi)),
                        (wfb[:, :].rearrange("p (k e) -> p k e", e=D),
                         _r3(wf_)),
                        (wgb[:, :].rearrange("p (k e) -> p k e", e=D),
                         _r3(wg)),
                        (wob[:, :].rearrange("p (k e) -> p k e", e=D),
                         _r3(wo)),
                    ])
                wdep = C.snap("gd")

                p1tail = []
                prev_ev = [[], [], [], []]
                prev_o = [[], [], [], []]
                with nc.Block(no_gpsimd_drain=True) as blk:
                    C.block = blk
                    for tc in range(NCH):
                        tsl = slice(tc * W, (tc + 1) * W)
                        ksl = (lambda buf:
                               (lambda k: buf[:, k * W:(k + 1) * W]))
                        s_x = C.dma("sy", p1tail, [
                            (xg[:, :].rearrange("p (k t) -> p k t", t=W),
                             _r3(xf)[:, :, tsl])])
                        s_ln, _ = emit_ln(
                            C, ksl(xg), xsq, st_sum, st_sq, rep_a, rep_b,
                            rows, tls, s_x + p1tail, dst_bf_sl=ksl(xnb))
                        evs = []
                        gi = 0
                        for wsb, dst in ((wib, iraw), (wfb, fraw)):
                            for m in range(NT):
                                par = gi % 4

                                def mmg(e, wsb=wsb, m=m, par=par):
                                    last = None
                                    for k in range(NT):
                                        last = e.matmul(
                                            mm[par],
                                            wsb[:, k * D + m * 128:
                                                k * D + (m + 1) * 128],
                                            xnb[:, k * W:(k + 1) * W],
                                            start=(k == 0),
                                            stop=(k == NT - 1))
                                    return last
                                s_mm = C.st("pe",
                                            s_ln + wdep + prev_ev[par], mmg)
                                s_ev = C.st("dv", s_mm,
                                            lambda e, dst=dst, m=m, par=par:
                                            e.tensor_copy(
                                                dst[:, m * W:(m + 1) * W],
                                                mm[par]))
                                prev_ev[par] = s_ev
                                evs += s_ev
                                gi += 1
                        s_ic, s_it = emit_conv(
                            C, ksl(iraw), ksl(icv), itail,
                            lambda k, j: cvi_sb[:, k * 4 + j:k * 4 + j + 1],
                            evs, NT)
                        s_fc, s_ft = emit_conv(
                            C, ksl(fraw), ksl(fcv), ftail,
                            lambda k, j: cvf_sb[:, k * 4 + j:k * 4 + j + 1],
                            evs, NT)
                        s_fg = C.st("ac", s_fc + s_ft, lambda e:
                                    e.activation(fraw[:, :], fcv[:, :],
                                                 AF.Sigmoid))
                        s_si = C.st("ac", s_ic + s_it + s_fg, lambda e:
                                    e.activation(iraw[:, :], icv[:, :],
                                                 AF.Silu))
                        s_g1 = C.st("dv", s_fg + s_si, lambda e:
                                    e.tensor_mul(icv[:, :], iraw[:, :],
                                                 fraw[:, :]))
                        s_g2 = C.st("dv", s_g1, lambda e:
                                    e.tensor_sub(icv[:, :], iraw[:, :],
                                                 icv[:, :]))

                        def scans(e):
                            last = None
                            for k in range(NT):
                                sl = slice(k * W, (k + 1) * W)
                                last = e.tensor_tensor_scan(
                                    fcv[:, sl], fraw[:, sl], icv[:, sl],
                                    hst_g[:, k:k + 1], ALU.mult, ALU.add)
                            return last
                        s_sc = C.st("dv", s_g2, scans)

                        def savest(e):
                            last = None
                            for k in range(NT):
                                last = e.tensor_copy(
                                    hst_g[:, k:k + 1],
                                    fcv[:, k * W + W - 1:k * W + W])
                            return last
                        s_hs = C.st("dv", s_sc, savest)
                        prev_g4 = [s_sc, s_sc, s_sc, s_sc]
                        for m in range(NT):
                            par = m % 4

                            def mmg2(e, m=m, par=par):
                                last = None
                                for k in range(NT):
                                    last = e.matmul(
                                        mm[par],
                                        wgb[:, k * D + m * 128:
                                            k * D + (m + 1) * 128],
                                        xnb[:, k * W:(k + 1) * W],
                                        start=(k == 0), stop=(k == NT - 1))
                                return last
                            s_mm = C.st("pe", prev_g4[par], mmg2)
                            s_ev = C.st("dv", s_mm, lambda e, m=m, par=par:
                                        e.tensor_copy(
                                            icv[:, m * W:(m + 1) * W],
                                            mm[par]))
                            prev_g4[par] = s_ev
                        prev_g = prev_g4[0] + prev_g4[1] + prev_g4[2] \
                            + prev_g4[3]
                        s_sg = C.st("ac", prev_g, lambda e: e.activation(
                            fraw[:, :], icv[:, :], AF.Silu))
                        s_rms, _ = emit_ln(
                            C, ksl(fcv), xsq, st_sum, st_sq, rep_a, rep_b,
                            rows, tls, s_sc + s_hs, rms=True,
                            dst_f32_sl=ksl(iraw))

                        def gate(e):
                            last = None
                            for k in range(NT):
                                sl = slice(k * W, (k + 1) * W)
                                last = e.scalar_tensor_tensor(
                                    ob[:, sl], iraw[:, sl],
                                    gnw_sb[:, k:k + 1], fraw[:, sl],
                                    ALU.mult, ALU.mult)
                            return last
                        s_ob = C.st("dv", s_rms + s_sg, gate)
                        prev_o = [x + s_ob for x in prev_o]
                        s_all = []
                        for m in range(NT):
                            par = m % 4

                            def mmo(e, m=m, par=par):
                                last = None
                                for k in range(NT):
                                    last = e.matmul(
                                        mm[par],
                                        wob[:, k * D + m * 128:
                                            k * D + (m + 1) * 128],
                                        ob[:, k * W:(k + 1) * W],
                                        start=(k == 0), stop=(k == NT - 1))
                                return last
                            s_mm = C.st("pe", s_ob + prev_o[par], mmo)
                            s_e = C.st("dv", s_mm, lambda e, m=m, par=par:
                                       e.tensor_add(
                                           xg[:, m * W:(m + 1) * W],
                                           xg[:, m * W:(m + 1) * W],
                                           mm[par]))
                            prev_o[par] = s_e
                            s_all += s_e
                        p1tail = C.dma("sy", s_all, [
                            (_r3(xf_sc)[:, :, tsl],
                             xg[:, :].rearrange("p (k t) -> p k t",
                                                t=W))]) + s_all
                        prev_ev = list(prev_o)
                with nc.Block(no_gpsimd_drain=True) as blk:
                    C.block = blk
                    C.st("sy", C.snap("sd"), lambda e: None)

            if "x1" in dbg_t:
                _dump(C, nc, xf_sc, dbg_t["x1"], D, "dx1")

        if "p2" in phases:
            _emit_ffn(C, nc, ST, xsq, mkT, ruT, wcb, xf_sc, None,
                      identf, dbg_dst=dbg_t.get("x2"), tag="a",
                      wdep0=wcb_done)

        if "p3" in phases:
            _emit_mamba_a(C, nc, xsq, xf_sc, xl_sc, u_sc, z_sc,
                          winT, xpT, cw_sb, cb_sb, lng_sb, lnb_sb, utail,
                          xdbl_sc)
            if "xl" in dbg_t:
                _dump(C, nc, xl_sc, dbg_t["xl"], D, "dxl")
            if "u" in dbg_t:
                _dump(C, nc, u_sc, dbg_t["u"], DI, "du")
            if "z" in dbg_t:
                _dump(C, nc, z_sc, dbg_t["z"], DI, "dz")

        if "p4" in phases:
            _emit_mamba_b(C, nc, xsq, xf_sc, xl_sc, u_sc, z_sc, wob_sc,
                          dtpT, xdbl_sc, a_sb, dtb_sb, dd_sb, hst_m, dbg_t,
                          zs_sc, wdep0=wcb_done)
            if "x3" in dbg_t:
                _dump(C, nc, xf_sc, dbg_t["x3"], D, "dx3")

        if "p5" in phases:
            _emit_ffn(C, nc, ST, xsq, mkT, ruT, wcb, xf_sc, y,
                      identf, tag="b")

        with nc.Block(no_gpsimd_drain=True) as blk:
            C.block = blk
            C.st("sy", C.snap("sd") + C.snap("gd") + C.snap("ad"),
                 lambda e: None)

    return nc


def _emit_mamba_a(C, nc, xsq, xf_sc, xl_sc, u_sc, z_sc, winT,
                  xpT, cw_sb, cb_sb, lng_sb, lnb_sb, utail, xdbl_sc):
    with contextlib.ExitStack() as PS:
        sb = lambda *a: PS.enter_context(nc.sbuf_tensor(*a))
        pst = lambda *a: PS.enter_context(nc.psum_tensor(*a))
        rows = sb("rows_m", [1, 4 * W], F32)
        tls = sb("tls_m", [128, W], F32)
        winb = sb("winb", [128, NT * 2 * DI], BF16)
        xpb = sb("xpb", [128, NTI * 96], BF16)
        xg = sb("xg3", [128, NT * W], F32)
        xl_t = sb("xl3", [128, NT * W], F32)
        xnb = sb("xnb3", [128, NT * W], BF16)
        ub = sb("ub3", [128, NTI * W], BF16)
        xinr = sb("xinr", [128, W], F32)
        cacc = sb("cacc", [128, W], F32)
        uu = sb("uu", [128, W], F32)
        zz_t = sb("zz_t", [128, W], F32)
        xdb_s = sb("xdb_s", [96, W], F32)
        xdb_c = sb("xdb_c", [16, W], F32)
        big = [pst(f"p3ps{i}", [128, 1024], F32).ap() for i in range(3)]
        st_sum = big[0][0:1, 0:W]
        st_sq = big[0][0:1, W:2 * W]
        rep_a = big[1][:, 0:W]
        rep_b = big[1][:, W:2 * W]
        mm = [big[2][:, 0:W], big[2][:, W:2 * W],
              big[1][:, 0:W], big[1][:, W:2 * W]]
        ps_x = pst("p3psx", [128, 1024], F32).ap()
        ps_dt = ps_x[0:64, 0:W]
        ps_bb = ps_x[0:16, W:2 * W]
        ps_cc = ps_x[32:48, W:2 * W]

        with nc.Block(no_gpsimd_drain=True) as blk:
            C.block = blk
            C.dma("gp", [], [
                (winb[:, :].rearrange("p (k e) -> p k e", e=2 * DI),
                 _r3(winT)),
                (xpb[:, :].rearrange("p (k e) -> p k e", e=96),
                 xpT.rearrange("(k p) e -> p k e", p=128)),
            ])
        wdep = C.snap("gd")

        p3tail = []
        prev_ev = [[], [], [], []]
        prev_ust = []
        prev_zst = []
        prev_xst = []
        prev_xlst = []
        with nc.Block(no_gpsimd_drain=True) as blk:
            C.block = blk
            for tc in range(NCH):
                tsl = slice(tc * W, (tc + 1) * W)
                s_x = C.dma("sy", p3tail, [
                    (xg[:, :].rearrange("p (k t) -> p k t", t=W),
                     _r3(xf_sc)[:, :, tsl])])
                s_ln, _ = emit_ln(
                    C, lambda k: xg[:, k * W:(k + 1) * W], xsq, st_sum,
                    st_sq, rep_a, rep_b, rows, tls,
                    s_x + p3tail + prev_xlst,
                    dst_bf_sl=lambda k: xnb[:, k * W:(k + 1) * W],
                    dst_f32_sl=lambda k: xl_t[:, k * W:(k + 1) * W],
                    gb=(lambda k: lng_sb[:, k:k + 1],
                        lambda k: lnb_sb[:, k:k + 1]))
                s_xs = C.st("dv", s_ln, lambda e:
                            e.tensor_add(xl_t[:, :], xl_t[:, :],
                                         xg[:, :]))
                prev_xlst = C.dma("sy", s_xs, [
                    (_r3(xl_sc)[:, :, tsl],
                     xl_t[:, :].rearrange("p (k t) -> p k t", t=W))])
                u_done = []
                for m in range(2 * NTI):
                    par = m % 4

                    def mmi(e, m=m, par=par):
                        last = None
                        for k in range(NT):
                            last = e.matmul(
                                mm[par],
                                winb[:, k * 2 * DI + m * 128:
                                     k * 2 * DI + (m + 1) * 128],
                                xnb[:, k * W:(k + 1) * W],
                                start=(k == 0), stop=(k == NT - 1))
                        return last
                    s_mm = C.st("pe", s_ln + wdep + prev_ev[par], mmi)
                    if m < NTI:
                        s_e = C.st("dv", s_mm, lambda e, par=par:
                                   e.tensor_copy(xinr[:, :], mm[par]))
                        s_cv, s_ut = emit_conv(
                            C, lambda k: xinr[:, :], lambda k: cacc[:, :],
                            utail[:, m * 3:(m + 1) * 3],
                            lambda kk, j, m=m: cw_sb[:, m * 4 + j:
                                                     m * 4 + j + 1],
                            s_e, 1)
                        s_u = C.st("ac", s_cv + s_ut + prev_ust,
                                   lambda e, m=m:
                                   e.activation(uu[:, :], cacc[:, :],
                                                AF.Silu,
                                                bias=cb_sb[:, m:m + 1]))
                        s_ub = C.st("dv", s_u, lambda e, m=m:
                                    e.tensor_copy(ub[:, m * W:(m + 1) * W],
                                                  uu[:, :]))
                        s_st = C.dma("sy", s_u, [
                            (u_sc[m * 128:(m + 1) * 128, tsl], uu[:, :])])
                        prev_ust = s_st
                        u_done += s_ub
                        prev_ev[par] = s_e + s_st + s_ub
                    else:
                        s_e = C.st("dv", s_mm + prev_zst, lambda e, par=par:
                                   e.tensor_copy(zz_t[:, :], mm[par]))
                        s_st = C.dma("sy", s_e, [
                            (z_sc[(m - NTI) * 128:(m - NTI + 1) * 128, tsl],
                             zz_t[:, :])])
                        prev_zst = s_st
                        prev_ev[par] = s_e + s_st

                def mk_mm(dst, lo, hi):
                    def f(e):
                        last = None
                        for k in range(NTI):
                            last = e.matmul(
                                dst, xpb[:, k * 96 + lo:k * 96 + hi],
                                ub[:, k * W:(k + 1) * W],
                                start=(k == 0), stop=(k == NTI - 1))
                        return last
                    return f
                s_m1 = C.st("pe", u_done + prev_ev[0] + prev_ev[1],
                            mk_mm(ps_dt, 0, 64))
                s_m2 = C.st("pe", [], mk_mm(ps_bb, 64, 80))
                s_m3 = C.st("pe", [], mk_mm(ps_cc, 80, 96))

                def evx(e):
                    e.tensor_copy(xdb_s[0:64, :], ps_dt)
                    e.tensor_copy(xdb_s[64:80, :], ps_bb)
                    return e.tensor_copy(xdb_c[:, :], ps_cc)
                s_ev = C.st("dv", s_m1 + s_m2 + s_m3 + prev_xst, evx)
                prev_xst = C.dma("sy", s_ev, [
                    (xdbl_sc[0:80, tsl], xdb_s[0:80, :]),
                    (xdbl_sc[80:96, tsl], xdb_c[:, :])])
                p3tail = prev_xst + s_ev
        with nc.Block(no_gpsimd_drain=True) as blk:
            C.block = blk
            C.st("sy", C.snap("sd"), lambda e: None)


def _emit_mamba_b(C, nc, xsq, xf_sc, xl_sc, u_sc, z_sc, wob_sc, dtpT,
                  xdbl_sc, a_sb, dtb_sb, dd_sb, hst_m, dbg_t, zs_sc,
                  wdep0=()):
    with contextlib.ExitStack() as PS:
        sb = lambda *a: PS.enter_context(nc.sbuf_tensor(*a))
        pst = lambda *a: PS.enter_context(nc.psum_tensor(*a))
        dtpb = sb("dtpb", [64, DI], BF16)
        dtrb = sb("dtrb", [64, W], BF16)
        brep = sb("brep", [128, NS * W], F32)
        crep = sb("crep", [128, NS * W], F32)
        bbuf = [sb("bb0", [128, 4 * W], F32), sb("bb1", [128, 4 * W], F32),
                sb("bb2", [128, 4 * W], F32)]
        ab2 = sb("ab2", [128, 4 * W], F32)
        hcb = sb("hcb0", [128, 4 * W], F32)
        dt2 = [sb("dt20", [128, W], F32), sb("dt21", [128, W], F32)]
        sp1 = sb("sp1", [128, W], F32)
        yb = [sb("yb4a", [128, NTI * W], BF16),
              sb("yb4b", [128, NTI * W], BF16)]
        u_t = [sb("u_t0", [128, W], F32), sb("u_t1", [128, W], F32)]
        z_t = [sb("z_t0", [128, W], F32), sb("z_t1", [128, W], F32)]
        dtu = sb("dtu", [128, W], F32)
        yt2 = sb("yt2", [128, W], F32)
        fixt = sb("fixt", [128, 8], F32)
        wou = [sb("wou0", [128, NTI * 128], BF16),
               sb("wou1", [128, NTI * 128], BF16)]
        xg_t = sb("xg4", [128, W], F32)
        xl_t = sb("xl4", [128, W], F32)
        spsc = sb("spsc", [128, 2 * W], F32)
        big = pst("p4ps", [128, 1024], F32).ap()
        ps_r = [big[:, 0:W], big[:, W:2 * W]]
        pso_t = [pst("p4pso0", [128, 1024], F32).ap(),
                 pst("p4pso1", [128, 1024], F32).ap()]
        ps_o = [pso_t[0][:, 0:W], pso_t[0][:, W:2 * W],
                pso_t[1][:, 0:W], pso_t[1][:, W:2 * W]]
        abuf = [xsq[:, 0:4 * W], xsq[:, 4 * W:8 * W], ab2[:, :]]

        with nc.Block(no_gpsimd_drain=True) as blk:
            C.block = blk
            C.dma("gp", [], [(dtpb[:, :], dtpT)])
        wdep = C.snap("gd")

        scan_done = [[], [], []]
        gp_done = [[], [], []]
        uld = [[], []]
        zld = [[], []]
        hs_done = [[], [], []]
        ybfree2 = [[], []]
        pend_out = [None]
        brep_rd = [[], [], [], []]
        crep_rd = [[], [], [], []]
        prev_dbg = []
        prev_dtmm = [[], []]
        s_wprev = [list(wdep0), list(wdep0)]
        prev_out = [[], [], [], []]
        ps_free = [[], []]
        ps_free4 = [[], [], [], []]
        ybd = [[]]


        with nc.Block(no_gpsimd_drain=True) as blk:
            C.block = blk
            xgp = [xg_t, spsc[:, 0:W], xl_t, spsc[:, W:2 * W]]

            def _emit_out_m(po, m):
                ptc, pybd = po
                ptsl = slice(ptc * W, (ptc + 1) * W)
                ybuf = yb[ptc % 2]
                par = m % 2
                p4r = m % 4
                s_w = C.dma("sy", s_wprev[par], [
                    (wou[par][:, :].rearrange("p (k e) -> p k e", e=128),
                     _r3(wob_sc)[:, :, m * 128:(m + 1) * 128])])

                def mmo(e, m=m, par=par, p4r=p4r, ybuf=ybuf):
                    last = None
                    for kk in range(NTI):
                        last = e.matmul(
                            ps_o[p4r],
                            wou[par][:, kk * 128:(kk + 1) * 128],
                            ybuf[:, kk * W:(kk + 1) * W],
                            start=(kk == 0), stop=(kk == NTI - 1))
                    return last
                s_mm = C.st("pe", pybd + s_w + ps_free4[p4r], mmo)
                s_wprev[par] = s_mm
                ybfree2[ptc % 2] = ybfree2[ptc % 2] + s_mm
                s_x = C.dma("sy", prev_out[p4r], [
                    (xgp[p4r][:, :],
                     xl_sc[m * 128:(m + 1) * 128, ptsl])])
                s_r2 = C.st("dv", s_mm + s_x, lambda e, p4r=p4r:
                            e.tensor_add(xgp[p4r][:, :], xgp[p4r][:, :],
                                         ps_o[p4r]))
                s_st = C.dma("ac", s_r2, [
                    (xf_sc[m * 128:(m + 1) * 128, ptsl],
                     xgp[p4r][:, :])])
                prev_out[p4r] = s_st + s_r2
                ps_free4[p4r] = s_r2

            for tc in range(NCH):
                tsl = slice(tc * W, (tc + 1) * W)
                ydep_cur = ybfree2[tc % 2]
                ybfree2[tc % 2] = []
                s_db = C.dma("gp", prev_dbg + prev_dtmm[0] + prev_dtmm[1],
                             [(dtrb[:, :], xdbl_sc[0:64, tsl])])
                def _bc(r0, n0, n1):
                    return xdbl_sc[r0 + n0:r0 + n1, tsl].rearrange(
                        "n (o t) -> o n t", o=1).broadcast_to(
                        [128, n1 - n0, W])
                s_repb = []
                s_repc = []
                for qq in range(4):
                    nsl4 = slice(qq * 4 * W, (qq + 1) * 4 * W)
                    s_repb.append(C.dma("sy", brep_rd[qq], [
                        (brep[:, nsl4].rearrange("p (n t) -> p n t", n=4),
                         _bc(64, qq * 4, qq * 4 + 4))]))
                for qq in range(4):
                    nsl4 = slice(qq * 4 * W, (qq + 1) * 4 * W)
                    s_repc.append(C.dma("sy", crep_rd[qq], [
                        (crep[:, nsl4].rearrange("p (n t) -> p n t", n=4),
                         _bc(80, qq * 4, qq * 4 + 4))]))

                def dt_build(k):
                    p2 = k % 2
                    par = k % 2

                    def mmdt(e, k=k, par=par):
                        return e.matmul(
                            ps_r[par], dtpb[:, k * 128:(k + 1) * 128],
                            dtrb[:, :], start=True, stop=True)
                    s_mm = C.st("pe", s_db + wdep + ps_free[par]
                                + prev_dtmm[par], mmdt)
                    prev_dtmm[par] = s_mm

                    def evd(e, k=k, par=par, p2=p2):
                        return e.tensor_scalar_add(
                            dt2[p2][:, :], ps_r[par], dtb_sb[:, k:k + 1])
                    s_ev = C.st("dv", s_mm + uld[p2], evd)
                    ps_free[par] = s_ev
                    s_ab = C.st("ac", s_ev + sp_free[0], lambda e, p2=p2:
                                e.activation(sp1[:, :], dt2[p2][:, :],
                                             AF.Abs))
                    s_e1 = C.st("ac", s_ab, lambda e: e.activation(
                        sp1[:, :], sp1[:, :], AF.Exp, scale=-1.0))
                    s_rl = C.st("ac", s_ab, lambda e, p2=p2:
                                e.activation(dt2[p2][:, :], dt2[p2][:, :],
                                             AF.Relu))
                    s_e2 = C.st("ac", s_e1, lambda e: e.activation(
                        sp1[:, :], sp1[:, :], AF.Ln, bias=1.0))
                    def tail(p2=p2, s_e2=s_e2, s_rl=s_rl):
                        s_ad = C.st("dv", s_e2 + s_rl, lambda e, p2=p2:
                                    e.tensor_add(dt2[p2][:, :],
                                                 dt2[p2][:, :],
                                                 sp1[:, :]))
                        sp_free[0] = s_ad
                        return s_ad
                    return tail

                sp_free = [[]]
                s_dt = [None, None]
                s_dt[0] = dt_build(0)()
                for k in range(NTI):
                    p2 = k % 2
                    s_u = C.dma("sy", uld[p2], [
                        (u_t[p2][:, :], u_sc[k * 128:(k + 1) * 128, tsl])])
                    s_zt = C.dma("sy", zld[p2], [
                        (z_t[p2][:, :], z_sc[k * 128:(k + 1) * 128, tsl])])
                    s_sz = C.st("ac", s_zt + zld[p2], lambda e, p2=p2:
                                e.activation(z_t[p2][:, :], z_t[p2][:, :],
                                             AF.Silu))
                    s_du = C.st("dv", s_u + s_dt[p2], lambda e, p2=p2:
                                e.tensor_mul(dtu[:, :], dt2[p2][:, :],
                                             u_t[p2][:, :]))

                    def emit_prep(q):
                        p = (4 * k + q) % 3
                        nsl = slice(q * 4 * W, (q + 1) * 4 * W)
                        hsl = slice(k * NS + q * 4, k * NS + q * 4 + 4)

                        def a_exp(e, k=k, q=q, p=p, p2=p2):
                            last = None
                            for j in range(4):
                                n = q * 4 + j
                                last = e.activation(
                                    abuf[p][:, j * W:(j + 1) * W],
                                    dt2[p2][:, :], AF.Exp,
                                    scale=a_sb[:, k * NS + n:
                                               k * NS + n + 1])
                            return last
                        s_a = C.st("ac", s_dt[p2] + scan_done[p], a_exp)

                        def bbody(e, p=p, q=q):
                            last = None
                            for nn in range(4):
                                last = e.tensor_mul(
                                    bbuf[p][:, nn * W:(nn + 1) * W],
                                    brep[:, (q * 4 + nn) * W:
                                         (q * 4 + nn + 1) * W],
                                    dtu[:, :])
                            return last
                        s_b = C.st("dv", s_repb[q] + s_du + hs_done[p],
                                   bbody)
                        brep_rd[q] = s_b
                        acols = abuf[p][:, :].rearrange(
                            "p (n t) -> p n t", n=4)[:, :, 0:1]
                        bcols = bbuf[p][:, :].rearrange(
                            "p (n t) -> p n t", n=4)[:, :, 0:1]
                        ecols = bbuf[p][:, :].rearrange(
                            "p (n t) -> p n t", n=4)[:, :, W - 1:W]
                        hcols = hst_m[:, hsl].rearrange(
                            "p (a o) -> p a o", o=1)
                        fo = 4 * (q % 2)
                        ytmp = fixt[:, fo:fo + 4].rearrange(
                            "p (a o) -> p a o", o=1)
                        s_f1 = C.st("gp", s_a + s_b, lambda e, acols=acols,
                                    hcols=hcols, ytmp=ytmp:
                                    e.tensor_mul(ytmp, acols, hcols))
                        s_f2 = C.st("gp", s_f1, lambda e, bcols=bcols,
                                    ytmp=ytmp:
                                    e.tensor_add(bcols, bcols, ytmp))
                        s_f3 = C.st("gp", s_f2, lambda e, acols=acols:
                                    e.memset(acols, 0.0))
                        return (p, nsl, s_f3, ecols, hcols)

                    prep = [emit_prep(0), emit_prep(1), emit_prep(2),
                            None]
                    dt_tail = dt_build(k + 1) if k + 1 < NTI else None
                    for q in range(4):
                        if q == 2 and dt_tail is not None:
                            s_dt[(k + 1) % 2] = dt_tail()
                        if q == 3:
                            prep[3] = emit_prep(3)
                        p, nsl, s_f3, ecols, hcols = prep[q]
                        s_scan = C.st("dv", s_f3,
                                      lambda e, p=p:
                                      e.tensor_tensor_scan(
                                          bbuf[p][:, :], abuf[p][:, :],
                                          bbuf[p][:, :], 0.0, ALU.mult,
                                          ALU.add))
                        scan_done[p] = s_scan
                        hs_done[p] = C.st("gp", s_scan,
                                          lambda e, hcols=hcols,
                                          ecols=ecols:
                                          e.tensor_copy(hcols, ecols))
                        s_hc = C.st("dv", s_repc[q],
                                    lambda e, p=p, nsl=nsl:
                                    e.tensor_mul(hcb[:, :], bbuf[p][:, :],
                                                 crep[:, nsl]))
                        gp_done[p] = s_hc
                        crep_rd[q] = s_hc
                        C.st("dv", [], lambda e:
                             e.tensor_add(hcb[:, 0:2 * W], hcb[:, 0:2 * W],
                                          hcb[:, 2 * W:4 * W]))
                        if q == 0:
                            C.st("dv", [], lambda e:
                                 e.tensor_add(yt2[:, :], hcb[:, 0:W],
                                              hcb[:, W:2 * W]))
                        else:
                            C.st("dv", [], lambda e:
                                 e.tensor_add(yt2[:, :], yt2[:, :],
                                              hcb[:, 0:W]))
                            C.st("dv", [], lambda e:
                                 e.tensor_add(yt2[:, :], yt2[:, :],
                                              hcb[:, W:2 * W]))
                    s_y3 = C.st("dv", [], lambda e, k=k, p2=p2:
                                e.scalar_tensor_tensor(
                                    yt2[:, :], u_t[p2][:, :],
                                    dd_sb[:, k:k + 1], yt2[:, :],
                                    ALU.mult, ALU.add))
                    s_y4 = C.st("dv", s_y3 + s_sz + ydep_cur,
                                lambda e, k=k, p2=p2, tc=tc:
                                e.tensor_mul(
                                    yb[tc % 2][:, k * W:(k + 1) * W],
                                    yt2[:, :], z_t[p2][:, :]))
                    uld[p2] = s_y3
                    zld[p2] = s_y4
                    ybd[0] = s_y4
                    if pend_out[0] is not None and k < NT:
                        _emit_out_m(pend_out[0], k)
                pend_out[0] = (tc, ybd[0])
                if "dtv" in dbg_t:
                    prev_dbg = []
            for m in range(NT):
                _emit_out_m(pend_out[0], m)
        with nc.Block(no_gpsimd_drain=True) as blk:
            C.block = blk
            C.st("sy", C.snap("sd"), lambda e: None)


def _dump(C, nc, src, dst, rows_n, tag):
    with contextlib.ExitStack() as ES:
        tmp = ES.enter_context(nc.sbuf_tensor(f"dump_{tag}", [128, T], F32))
        with nc.Block(no_gpsimd_drain=True) as blk:
            C.block = blk
            prev = []
            for r in range(rows_n // 128):
                s = C.dma("sy", prev, [(tmp[:, :],
                                        src[r * 128:(r + 1) * 128, :])])
                prev = C.dma("sy", s, [(dst[r * 128:(r + 1) * 128, :],
                                        tmp[:, :])])
            C.st("sy", C.snap("sd"), lambda e: None)


def _vec(v, nk):
    v = np.asarray(v)
    if v.ndim == 1:
        return np.ascontiguousarray(v.reshape(nk, 128).T)
    c = v.shape[1]
    return np.ascontiguousarray(
        v.reshape(nk, 128, c).transpose(1, 0, 2).reshape(128, nk * c))


def prep_inputs(inputs):
    x = np.asarray(inputs["L0_attn_a_x"])
    base = {
        "wi": np.asarray(inputs["L0_attn_a_Wi"]),
        "wf": np.asarray(inputs["L0_attn_a_Wf"]),
        "wg": np.asarray(inputs["L0_attn_a_Wg"]),
        "wo": np.asarray(inputs["L0_attn_a_Wo"]),
        "cvi": _vec(inputs["L0_attn_a_conv_i"], NT),
        "cvf": _vec(inputs["L0_attn_a_conv_f"], NT),
        "gnw": _vec(inputs["L0_attn_a_g_norm_weight"], NT),
        "lng": _vec(inputs["L0_attn_b_ln_gamma"], NT),
        "lnb": _vec(inputs["L0_attn_b_ln_beta"], NT),
        "winT": np.ascontiguousarray(
            np.asarray(inputs["L0_attn_b_in_proj_w"]).T),
        "cw": _vec(inputs["L0_attn_b_conv_w"], NTI),
        "cb": _vec(inputs["L0_attn_b_conv_b"], NTI),
        "xpT": np.ascontiguousarray(
            np.asarray(inputs["L0_attn_b_x_proj_w"]).T),
        "dtpT": np.ascontiguousarray(
            np.asarray(inputs["L0_attn_b_dt_proj_w"]).T),
        "dtb": _vec(inputs["L0_attn_b_dt_proj_b"], NTI),
        "alog": _vec(inputs["L0_attn_b_A_log"], NTI),
        "dd": _vec(inputs["L0_attn_b_D"], NTI),
        "woutT": np.ascontiguousarray(
            np.asarray(inputs["L0_attn_b_out_proj_w"]).T),
        "mk": np.asarray(inputs["L0_ffn_rail_maker"]),
        "mkT": np.ascontiguousarray(
            np.asarray(inputs["L0_ffn_rail_maker"]).transpose(0, 1, 3, 2)),
        "ruT": np.ascontiguousarray(
            np.asarray(inputs["L0_ffn_rail_up_proj"]).transpose(0, 1, 3, 2)),
        "wcT": np.ascontiguousarray(np.asarray(inputs["L0_ffn_w_cproj"]).T),
    }
    in_maps = []
    for b in range(8):
        m = dict(base)
        m["xf"] = np.ascontiguousarray(x[b].T)
        in_maps.append(m)
    return in_maps


def kernel(**inputs):
    in_maps = prep_inputs(inputs)
    nc = build_program()
    res = run_bass_kernel_spmd(nc, in_maps, core_ids=list(range(8)))
    return np.stack([r["y"] for r in res.results], axis=0)


# version 26
# speedup vs baseline: 1.0749x; 1.0315x over previous
import contextlib
import numpy as np
import concourse.bass as bass
import concourse.mybir as mybir
from concourse.bass_utils import run_bass_kernel_spmd

F32 = mybir.dt.float32
BF16 = mybir.dt.bfloat16
AF = mybir.ActivationFunctionType
ALU = mybir.AluOpType

T = 4096
D = 1024
W = 512
NCH = T // W
NT = D // 128
DI = 2048
NTI = DI // 128
NS = 16
GW = 1024
NG = T // GW
LN_EPS = 1e-5

_ENG = {"pe": "tensor", "dv": "vector", "ac": "scalar", "gp": "gpsimd",
        "sy": "sync"}


class Ctx:

    def __init__(self, nc):
        self.nc = nc
        self.block = None
        self.sems = {}

    def init_sems(self, stack):
        for name in ("pe", "dv", "ac", "gp", "sy"):
            h = stack.enter_context(self.nc.semaphore(f"sem_{name}"))
            self.sems[name] = [h, 0]
        self.lanes = {}
        self.lane_rr = {}
        for q, nl in (("sd", 8), ("gd", 6), ("ad", 4)):
            self.lanes[q] = []
            self.lane_rr[q] = 0
            for i in range(nl):
                h = stack.enter_context(self.nc.semaphore(f"lane_{q}{i}"))
                self.lanes[q].append([h, 0])

    def snap(self, *names):
        out = []
        for n in names:
            if n in self.sems:
                out.append((self.sems[n][0], self.sems[n][1]))
            else:
                out += [(h, v) for h, v in self.lanes[n]]
        return out

    def _waits(self, eng, deps):
        merged = {}
        for s, v in deps:
            if v > 0 and merged.get(s, 0) < v:
                merged[s] = v
        for s, v in merged.items():
            eng.wait_ge(s, v)

    def st(self, sel, deps, emit):
        sem, _ = self.sems[sel]

        def f(eng):
            self._waits(eng, deps)
            last = emit(eng)
            if last is not None:
                last.then_inc(sem, 1)
        getattr(self.block, _ENG[sel])(f)
        self.sems[sel][1] += 1
        return self.snap(sel)

    def dma(self, sel, deps, pairs):
        q = {"sy": "sd", "gp": "gd", "ac": "ad"}[sel]
        lanes = self.lanes[q]
        plan = []
        for _ in pairs:
            li = self.lane_rr[q] % len(lanes)
            self.lane_rr[q] += 1
            lane = lanes[li]
            plan.append((lane, lane[1], lane[1] + 16))
            lane[1] += 16
        out_snap = [(lane[0], after) for lane, _, after in plan]

        def f(eng):
            self._waits(eng, deps)
            for (lane, before, _), (out, in_) in zip(plan, pairs):
                if before > 0:
                    eng.wait_ge(lane[0], before)
                eng.dma_start(out=out, in_=in_).then_inc(lane[0], 16)
        getattr(self.block, _ENG[sel])(f)
        return out_snap


def emit_ln(C, src_sl, xsq, st_sum, st_sq, rep_a, rep_b, rows, tls, dep, *,
            rms=False, dst_bf_sl=None, dst_f32_sl=None, gb=None, n_k=NT):
    inv_n = 1.0 / (n_k * 128)
    ones1, onesr = C.ones1, C.onesr
    sum_row = rows[0:1, 0:W]
    sq_row = rows[0:1, W:2 * W]
    a_row = rows[0:1, 2 * W:3 * W]
    b_row = rows[0:1, 3 * W:4 * W]

    def sq(e):
        last = None
        for k in range(n_k):
            last = e.activation(xsq[:, k * W:(k + 1) * W], src_sl(k),
                                AF.Square)
        return last
    s_sq = C.st("ac", dep, sq)

    def mm_stats(e):
        last = None
        if not rms:
            for k in range(n_k):
                last = e.matmul(st_sum, ones1[:, 0:1], src_sl(k),
                                start=(k == 0), stop=(k == n_k - 1))
        for k in range(n_k):
            last = e.matmul(st_sq, ones1[:, 0:1],
                            xsq[:, k * W:(k + 1) * W],
                            start=(k == 0), stop=(k == n_k - 1))
        return last
    s_mm = C.st("pe", dep + s_sq, mm_stats)

    if rms:
        s = C.st("dv", s_mm, lambda e: e.tensor_copy(sq_row, st_sq))
        s = C.st("dv", s, lambda e: e.tensor_scalar_mul(sq_row, sq_row,
                                                        inv_n))
        s = C.st("ac", s, lambda e: e.activation(a_row, sq_row, AF.Ln,
                                                 bias=C.eps1))
        s = C.st("ac", s, lambda e: e.activation(a_row, a_row, AF.Exp,
                                                 scale=C.mhalf1))
        s_rep = C.st("pe", s, lambda e: e.matmul(rep_a, onesr[0:1, :], a_row,
                                                 start=True, stop=True))
        done = []
        for k in range(n_k):
            done += C.st("dv", s_rep, lambda e, k=k: e.tensor_mul(
                dst_f32_sl(k), src_sl(k), rep_a))
        return done, s

    def cps(e):
        e.tensor_copy(sum_row, st_sum)
        return e.tensor_copy(sq_row, st_sq)
    s = C.st("dv", s_mm, cps)

    def mumath(e):
        e.tensor_scalar_mul(sum_row, sum_row, inv_n)
        return e.tensor_scalar_mul(sq_row, sq_row, inv_n)
    s = C.st("dv", s, mumath)
    s = C.st("dv", s, lambda e: e.tensor_mul(b_row, sum_row, sum_row))
    s = C.st("dv", s, lambda e: e.tensor_sub(sq_row, sq_row, b_row))
    s = C.st("ac", s, lambda e: e.activation(a_row, sq_row, AF.Ln,
                                             bias=C.eps1))
    s = C.st("ac", s, lambda e: e.activation(a_row, a_row, AF.Exp,
                                             scale=C.mhalf1))
    s = C.st("dv", s, lambda e: e.tensor_mul(b_row, sum_row, a_row))
    s = C.st("dv", s, lambda e: e.tensor_scalar_mul(b_row, b_row, -1.0))

    def mmrep(e):
        e.matmul(rep_a, onesr[0:1, :], a_row, start=True, stop=True)
        return e.matmul(rep_b, onesr[0:1, :], b_row, start=True, stop=True)
    s_rep = C.st("pe", s, mmrep)

    done = []
    for k in range(n_k):
        s_t = C.st("dv", s_rep, lambda e, k=k: e.tensor_mul(
            tls[:, 0:W], src_sl(k), rep_a))
        if dst_f32_sl is not None:
            s_x = C.st("dv", s_t, lambda e, k=k: e.tensor_add(
                dst_f32_sl(k), tls[:, 0:W], rep_b))
            if dst_bf_sl is not None:
                g, b = gb
                s_x = C.st("dv", s_x, lambda e, k=k: e.tensor_scalar(
                    dst_bf_sl(k), dst_f32_sl(k), g(k), b(k), ALU.mult,
                    ALU.add))
        else:
            s_x = C.st("dv", s_t, lambda e, k=k: e.tensor_add(
                dst_bf_sl(k), tls[:, 0:W], rep_b))
        done += s_x
    return done, s_rep


def emit_conv(C, raw_sl, acc_sl, tail, wcol, dep, n_k):
    def tap3(e):
        last = None
        for k in range(n_k):
            last = e.tensor_scalar_mul(acc_sl(k), raw_sl(k), wcol(k, 3))
        return last
    s = C.st("dv", dep, tap3)
    for j in (2, 1, 0):
        sft = 3 - j

        def tapj(e, j=j, sft=sft):
            last = None
            for k in range(n_k):
                acc = acc_sl(k)
                raw = raw_sl(k)
                last = e.scalar_tensor_tensor(
                    acc[:, sft:W], raw[:, 0:W - sft], wcol(k, j),
                    acc[:, sft:W], ALU.mult, ALU.add)
                last = e.scalar_tensor_tensor(
                    acc[:, 0:sft], tail[:, k * 3 + 3 - sft:k * 3 + 3],
                    wcol(k, j), acc[:, 0:sft], ALU.mult, ALU.add)
            return last
        s = C.st("dv", s, tapj)

    def upd(e):
        last = None
        for k in range(n_k):
            last = e.tensor_copy(tail[:, k * 3:(k + 1) * 3],
                                 raw_sl(k)[:, W - 3:W])
        return last
    s_tail = C.st("dv", s, upd)
    return s, s_tail


def _r3(ap, p=128):
    return ap.rearrange("(k p) t -> p k t", p=p)


def _emit_ffn(C, nc, ST, xsq, mtfp, rufp, wcb, xf_sc, y, identf,
              dbg_dst=None, tag="", wdep0=()):
    with contextlib.ExitStack() as PS:
        sb = lambda *a: PS.enter_context(nc.sbuf_tensor(*a))
        pst = lambda *a: PS.enter_context(nc.psum_tensor(*a))
        rub = sb("rub" + tag, [128, 64 * 128], BF16)

        with contextlib.ExitStack() as QS:
            mtf = QS.enter_context(nc.sbuf_tensor("mtf" + tag,
                                                  [128, 64 * 128], F32))
            ruf = QS.enter_context(nc.sbuf_tensor("ruf" + tag, [128, 64 * 128],
                                                  F32))
            psW = QS.enter_context(nc.psum_tensor("psW" + tag,
                                                  [128, 256], F32))
            psWa = psW.ap()
            with nc.Block(no_gpsimd_drain=True) as blk:
                C.block = blk

                def zz(e):
                    e.memset(mtf[:, :], 0.0)
                    return e.memset(ruf[:, :], 0.0)
                s_z = C.st("gp", [], zz)
                qw = 16 * 128
                pre_pairs = []
                for qq in range(4):
                    sl = slice(qq * qw, (qq + 1) * qw)
                    pre_pairs.append((mtf[:, sl], mtfp[:, sl]))
                    pre_pairs.append((ruf[:, sl], rufp[:, sl]))
                s_d1 = C.dma("sy", s_z, pre_pairs)
                s_d2 = s_d1
                ev_prev = [[], []]
                for slot in range(64):
                    off = slot * 128
                    par = slot % 2
                    psl = psWa[:, par * 128:(par + 1) * 128]

                    def mmw(e, off=off, psl=psl):
                        return e.matmul(psl, mtf[:, off:off + 128],
                                        ruf[:, off:off + 128],
                                        start=True, stop=True)
                    s_mw = C.st("pe", s_d1 + s_d2 + ev_prev[par], mmw)
                    s_ev = C.st("dv", s_mw, lambda e, off=off, psl=psl:
                                e.tensor_copy(rub[:, off:off + 128], psl))
                    ev_prev[par] = s_ev
                s_wb = ev_prev[0] + ev_prev[1]

        rows = sb("rows_f" + tag, [1, 4 * W], F32)
        tls = sb("tls_f" + tag, [128, W], F32)
        xnb = sb("xnb_f" + tag, [128, NT * GW], BF16)
        hh = sb("hh" + tag, [128, 32 * GW], BF16)
        wcf = [sb("wcf0" + tag, [128, 32 * 128], BF16),
               sb("wcf1" + tag, [128, 32 * 128], BF16)]
        oac = sb("oac" + tag, [128, NT * GW], F32)
        xgs = sb("xgs" + tag, [128, NT * W], F32)
        zev = sb("zev" + tag, [128, GW], F32)
        rsx = zev
        ytok = zev

        big = [pst(f"fps{i}" + tag, [128, 1024], F32).ap()
               for i in range(4)]
        st_sum = big[0][0:1, 0:W]
        st_sq = big[0][0:1, W:2 * W]
        rep_a = big[1][:, 0:W]
        rep_b = big[1][:, W:2 * W]
        ps_y = big[2]
        ps_zs2 = [big[0], big[1], big[3], big[2]]
        ps_cs = [big[3], big[2]]
        ps_tr = big[1]

        ffn_tail = []
        last_maker = []
        s_ye_prev = [[], []]
        ec_prev = [[], []]
        wc_prev = [list(wdep0), list(wdep0)]
        s_gel_prev2 = [[], []]
        prev_half = []
        with nc.Block(no_gpsimd_drain=True) as blk:
            C.block = blk
            for g in range(NG):
                s_gel_prev = s_gel_prev + ffn_tail if g else []
                s_ln_all = []
                for hf in range(2):
                    c0 = g * GW + hf * W
                    s_x = C.dma("sy", prev_half + last_maker, [
                        (xgs[:, :].rearrange("p (k t) -> p k t", t=W),
                         _r3(xf_sc)[:, :, c0:c0 + W])])
                    psdep = (s_gel_prev + ec_prev[0] + ec_prev[1]
                             + ffn_tail) if g else []
                    s_ln, s_lrep = emit_ln(
                        C, lambda k: xgs[:, k * W:(k + 1) * W], xsq,
                        st_sum, st_sq, rep_a, rep_b, rows, tls,
                        s_x + psdep,
                        dst_bf_sl=lambda k: xnb[:, k * GW + hf * W:
                                                k * GW + hf * W + W])
                    s_ln_all += s_ln
                    prev_half = s_ln + s_lrep
                gel_last = []
                gelq = []
                s_gel_prev = s_gel_prev if g else []
                s_gel_prev2 = [s_ln_all + s_gel_prev2[0] + s_wb,
                               s_ln_all + s_gel_prev2[1] + s_wb,
                               s_ln_all + (s_gel_prev2[2]
                                           if len(s_gel_prev2) > 2 else
                                           ec_prev[0]) + s_wb,
                               s_ln_all + (s_gel_prev2[3]
                                           if len(s_gel_prev2) > 3 else
                                           ec_prev[1]) + s_wb]
                for hd in range(4):
                    for rg in range(8):
                        for mq in range(4):
                            m = rg * 4 + mq
                            q = m // 2
                            rbase = (q % 2) * 64
                            zpar = (hd * 32 + m) % 4

                            def mmz(e, m=m, hd=hd, rg=rg, rbase=rbase,
                                    zpar=zpar):
                                slot = 2 * (hd * 8 + m // 4) + (m % 2)
                                off = slot * 128
                                last = None
                                for h2 in range(2):
                                    last = e.matmul(
                                        ps_zs2[zpar][:, h2 * W:(h2 + 1) * W],
                                        rub[rbase:rbase + 64,
                                            off:off + 128],
                                        xnb[rbase:rbase + 64,
                                            rg * GW + h2 * W:
                                            rg * GW + (h2 + 1) * W],
                                        start=True, stop=True)
                                return last
                            s_mz = C.st("pe", s_ln_all + s_gel_prev2[zpar],
                                        mmz)
                            last_maker = s_mz
                            gelq.append((m, zpar, s_mz))
                            if len(gelq) == 3 or m == 31:
                                batch = list(gelq)
                                gelq.clear()

                                def gtri(e, batch=batch):
                                    last = None
                                    for (gm, gz, _) in batch:
                                        last = e.activation(
                                            hh[:, gm * GW:(gm + 1) * GW],
                                            ps_zs2[gz], AF.Gelu)
                                    return last
                                gdep = []
                                for (_, _, d) in batch:
                                    gdep += d
                                s_gel = C.st("ac", gdep, gtri)
                                s_gel_prev = s_gel
                                for (_, gz, _) in batch:
                                    s_gel_prev2[gz] = (s_gel_prev2[gz]
                                                       + s_gel)
                    gel_last = s_gel_prev
                    for et in range(NT):
                        par = et % 2
                        s_w = C.dma("sy", wc_prev[par], [
                            (wcf[par][:, :].rearrange(
                                "p (m e) -> p m e", e=128),
                             wcb[hd * 4096:(hd + 1) * 4096,
                                 et * 128:(et + 1) * 128].rearrange(
                                     "(m p) e -> p m e", p=128))])

                        def mmc(e, par=par):
                            last = None
                            for m in range(32):
                                for h2 in range(2):
                                    last = e.matmul(
                                        ps_cs[par][:, h2 * W:(h2 + 1) * W],
                                        wcf[par][:, m * 128:(m + 1) * 128],
                                        hh[:, m * GW + h2 * W:
                                           m * GW + (h2 + 1) * W],
                                        start=(m == 0), stop=(m == 31))
                            return last
                        dep_mc = gel_last + s_w + ec_prev[par]
                        if par == 0:
                            dep_mc = dep_mc + s_gel_prev2[2]
                        else:
                            dep_mc = dep_mc + s_gel_prev2[3]
                        s_mc = C.st("pe", dep_mc, mmc)
                        wc_prev[par] = s_mc
                        if hd == 0:
                            s_ec = C.st("dv", s_mc, lambda e, et=et, par=par:
                                        e.tensor_copy(
                                            oac[:, et * GW:(et + 1) * GW],
                                            ps_cs[par]))
                        else:
                            s_ec = C.st("dv", s_mc, lambda e, et=et, par=par:
                                        e.tensor_add(
                                            oac[:, et * GW:(et + 1) * GW],
                                            oac[:, et * GW:(et + 1) * GW],
                                            ps_cs[par]))
                        ec_prev[par] = s_ec
                    s_gel_prev2[2] = s_gel_prev2[2] + ec_prev[0]
                    s_gel_prev2[3] = s_gel_prev2[3] + ec_prev[1]
                s_tr_prev = [[], []]
                for et in range(NT):
                    s_xe = C.dma("sy", ec_prev[0] + ec_prev[1] + ffn_tail, [
                        (rsx[:, :],
                         xf_sc[et * 128:(et + 1) * 128,
                               g * GW:(g + 1) * GW])])
                    s_r = C.st("dv", s_xe + ec_prev[et % 2], lambda e, et=et:
                               e.tensor_add(oac[:, et * GW:(et + 1) * GW],
                                            oac[:, et * GW:(et + 1) * GW],
                                            rsx[:, :]))
                    if y is None:
                        s_so = C.dma("sy", s_r, [
                            (xf_sc[et * 128:(et + 1) * 128,
                                   g * GW:(g + 1) * GW],
                             oac[:, et * GW:(et + 1) * GW])])
                        if dbg_dst is not None:
                            s_so = s_so + C.dma("sy", s_r, [
                                (dbg_dst[et * 128:(et + 1) * 128,
                                         g * GW:(g + 1) * GW],
                                 oac[:, et * GW:(et + 1) * GW])])
                        ec_prev[et % 2] = s_r + s_so
                        ffn_tail = s_r + s_so
                    else:
                        ec_prev[et % 2] = s_r
                if y is not None:
                    res_done = ec_prev[0] + ec_prev[1]
                    s_yst = []
                    for sv in range(8):
                        s_cp_all = []
                        for et in range(NT):
                            par = et % 2
                            tsl2 = slice(par * 512, par * 512 + 128)

                            def tr(e, et=et, sv=sv, tsl2=tsl2):
                                return e.transpose(
                                    ps_tr[:, tsl2],
                                    oac[:, et * GW + sv * 128:
                                        et * GW + (sv + 1) * 128],
                                    identf[:, :])
                            s_t = C.st("pe", res_done + s_tr_prev[par], tr)
                            s_cp = C.st("dv", s_t + s_yst,
                                        lambda e, et=et, tsl2=tsl2:
                                        e.tensor_copy(
                                            ytok[:, et * 128:(et + 1) * 128],
                                            ps_tr[:, tsl2]))
                            s_tr_prev[par] = s_cp
                            s_cp_all += s_cp
                        s_yst = C.dma("sy", s_cp_all, [
                            (y[g * GW + sv * 128:g * GW + (sv + 1) * 128, :],
                             ytok[:, :])])
                    ffn_tail = s_yst + s_tr_prev[0] + s_tr_prev[1]
        with nc.Block(no_gpsimd_drain=True) as blk:
            C.block = blk
            C.st("sy", C.snap("sd") + C.snap("ad"), lambda e: None)


def build_program(dbg=(), phases=("p1", "p2", "p3", "p4", "p5")):
    nc = bass.Bass("TRN2", target_bir_lowering=False)
    C = Ctx(nc)

    xf = nc.dram_tensor("xf", [D, T], F32, kind="ExternalInput").ap()
    wi = nc.dram_tensor("wi", [D, D], F32, kind="ExternalInput").ap()
    wf_ = nc.dram_tensor("wf", [D, D], F32, kind="ExternalInput").ap()
    wg = nc.dram_tensor("wg", [D, D], F32, kind="ExternalInput").ap()
    wo = nc.dram_tensor("wo", [D, D], F32, kind="ExternalInput").ap()
    cvi = nc.dram_tensor("cvi", [128, NT * 4], F32,
                         kind="ExternalInput").ap()
    cvf = nc.dram_tensor("cvf", [128, NT * 4], F32,
                         kind="ExternalInput").ap()
    gnw = nc.dram_tensor("gnw", [128, NT], F32, kind="ExternalInput").ap()
    lng = nc.dram_tensor("lng", [128, NT], F32, kind="ExternalInput").ap()
    lnb = nc.dram_tensor("lnb", [128, NT], F32, kind="ExternalInput").ap()
    winT = nc.dram_tensor("winT", [D, 2 * DI], F32,
                          kind="ExternalInput").ap()
    cw = nc.dram_tensor("cw", [128, NTI * 4], F32,
                        kind="ExternalInput").ap()
    cb = nc.dram_tensor("cb", [128, NTI], F32, kind="ExternalInput").ap()
    xpT = nc.dram_tensor("xpT", [DI, 96], F32, kind="ExternalInput").ap()
    dtpT = nc.dram_tensor("dtpT", [64, DI], F32, kind="ExternalInput").ap()
    dtb = nc.dram_tensor("dtb", [128, NTI], F32, kind="ExternalInput").ap()
    alog = nc.dram_tensor("alog", [128, NTI * NS], F32,
                          kind="ExternalInput").ap()
    dd = nc.dram_tensor("dd", [128, NTI], F32, kind="ExternalInput").ap()
    woutT = nc.dram_tensor("woutT", [DI, D], F32, kind="ExternalInput").ap()
    mk = nc.dram_tensor("mk", [4, 64, 16, 16], F32,
                        kind="ExternalInput").ap()
    mkT = nc.dram_tensor("mkT", [4, 64, 16, 16], F32,
                         kind="ExternalInput").ap()
    mtfp = nc.dram_tensor("mtfp", [128, 64 * 128], F32,
                          kind="ExternalInput").ap()
    rufp = nc.dram_tensor("rufp", [128, 64 * 128], F32,
                          kind="ExternalInput").ap()
    ruT = nc.dram_tensor("ruT", [4, 64, 16, 64], F32,
                         kind="ExternalInput").ap()
    wcT = nc.dram_tensor("wcT", [16 * D, D], F32, kind="ExternalInput").ap()

    y = nc.dram_tensor("y", [T, D], F32, kind="ExternalOutput").ap()
    dbg_t = {}
    for name in dbg:
        shp = {"x1": [D, T], "x2": [D, T], "x3": [D, T], "xl": [D, T],
               "u": [DI, T], "z": [DI, T], "dtv": [DI, T],
               "xdbl": [128, T]}[name]
        dbg_t[name] = nc.dram_tensor("dbg_" + name, shp, F32,
                                     kind="ExternalOutput").ap()

    xf_sc = nc.dram_tensor("xf_sc", [D, T], F32, kind="Internal").ap()
    xl_sc = nc.dram_tensor("xl_sc", [D, T], F32, kind="Internal").ap()
    u_sc = nc.dram_tensor("u_sc", [DI, T], F32, kind="Internal").ap()
    z_sc = nc.dram_tensor("z_sc", [DI, T], F32, kind="Internal").ap()
    wcb = nc.dram_tensor("wcb", [16 * D, D], BF16, kind="Internal").ap()
    wob_sc = nc.dram_tensor("wob_sc", [DI, D], BF16, kind="Internal").ap()
    zs_sc = nc.dram_tensor("zs_sc", [DI, W], F32, kind="Internal").ap()
    xdbl_sc = nc.dram_tensor("xdbl_sc", [96, T], F32, kind="Internal").ap()

    with contextlib.ExitStack() as ST:
        C.init_sems(ST)
        sb = lambda *a: ST.enter_context(nc.sbuf_tensor(*a))

        ones1 = sb("ones1", [128, 1], F32)
        onesr = sb("onesr", [1, 128], F32)
        cst = sb("cst", [128, 2], F32)
        C.ones1, C.onesr = ones1, onesr
        C.eps1 = cst[0:1, 0:1]
        C.mhalf1 = cst[0:1, 1:2]
        identf = sb("identf", [128, 128], F32)
        cvi_sb = sb("cvi_sb", [128, NT * 4], F32)
        cvf_sb = sb("cvf_sb", [128, NT * 4], F32)
        gnw_sb = sb("gnw_sb", [128, NT], F32)
        lng_sb = sb("lng_sb", [128, NT], F32)
        lnb_sb = sb("lnb_sb", [128, NT], F32)
        cw_sb = sb("cw_sb", [128, NTI * 4], F32)
        cb_sb = sb("cb_sb", [128, NTI], F32)
        dtb_sb = sb("dtb_sb", [128, NTI], F32)
        dd_sb = sb("dd_sb", [128, NTI], F32)
        a_sb = sb("a_sb", [128, NTI * NS], F32)
        hst_g = sb("hst_g", [128, NT], F32)
        hst_m = sb("hst_m", [128, NTI * NS], F32)
        itail = sb("itail", [128, NT * 3], F32)
        ftail = sb("ftail", [128, NT * 3], F32)
        utail = sb("utail", [128, NTI * 3], F32)
        xsq = sb("xsq", [128, NT * W], F32)

        with nc.Block(no_gpsimd_drain=True) as blk:
            C.block = blk
            ld = C.dma("sy", [], [
                (cvi_sb[:, :], cvi), (cvf_sb[:, :], cvf),
                (gnw_sb[:, :], gnw), (lng_sb[:, :], lng),
                (lnb_sb[:, :], lnb), (cw_sb[:, :], cw),
                (cb_sb[:, :], cb), (dtb_sb[:, :], dtb),
                (dd_sb[:, :], dd), (a_sb[:, :], alog),
            ])
            s_a1 = C.st("ac", ld, lambda e: e.activation(a_sb[:, :],
                                                         a_sb[:, :], AF.Exp))
            C.st("dv", s_a1, lambda e: e.tensor_scalar_mul(
                a_sb[:, :], a_sb[:, :], -1.0))

            def zeros(e):
                e.memset(hst_g[:, :], 0.0)
                e.memset(hst_m[:, :], 0.0)
                e.memset(itail[:, :], 0.0)
                e.memset(ftail[:, :], 0.0)
                e.memset(utail[:, :], 0.0)
                e.memset(ones1[:, :], 1.0)
                e.memset(cst[:, 0:1], LN_EPS)
                e.memset(cst[:, 1:2], -0.5)
                return e.memset(onesr[:, :], 1.0)
            C.st("dv", [], zeros)

            def gpinit(gpsimd):
                sem = C.sems["gp"][0]
                cnt = C.sems["gp"][1]
                gpsimd.memset(identf[:, :], 0.0).then_inc(sem, 1)
                cnt += 1
                gpsimd.wait_ge(sem, cnt)
                gpsimd.affine_select(
                    out=identf[:, :], in_=identf[:, :],
                    compare_op=ALU.not_equal, fill=1.0, base=0,
                    pattern=[[-1, 128]], channel_multiplier=1).then_inc(
                        sem, 1)
                cnt += 1
                C.sems["gp"][1] = cnt
            blk.gpsimd(gpinit)

        need_wcb = ("p2" in phases) or ("p5" in phases) or ("p4" in phases)
        wcb_done = []
        if need_wcb:
            with nc.Block(no_gpsimd_drain=True) as blk:
                C.block = blk
                for r in range(16):
                    wcb_done += C.dma("gp", [], [
                        (wcb[r * 1024:(r + 1) * 1024, :],
                         wcT[r * 1024:(r + 1) * 1024, :])])
                wcb_done += C.dma("gp", [], [(wob_sc, woutT)])

        if "p1" in phases:
            with contextlib.ExitStack() as PS:
                psb = lambda *a: PS.enter_context(nc.sbuf_tensor(*a))
                ppt = lambda *a: PS.enter_context(nc.psum_tensor(*a))
                wib = psb("wib", [128, NT * D], BF16)
                wfb = psb("wfb", [128, NT * D], BF16)
                wgb = psb("wgb", [128, NT * D], BF16)
                wob = psb("wob", [128, NT * D], BF16)
                rows = psb("rows", [1, 4 * W], F32)
                tls = psb("tls", [128, W], F32)
                xg = psb("xg", [128, NT * W], F32)
                xnb = psb("xnb", [128, NT * W], BF16)
                iraw = psb("iraw", [128, NT * W], F32)
                fraw = psb("fraw", [128, NT * W], F32)
                icv = psb("icv", [128, NT * W], F32)
                fcv = psb("fcv", [128, NT * W], F32)
                ob = psb("ob", [128, NT * W], BF16)
                big = [ppt(f"p1ps{i}", [128, 1024], F32).ap()
                       for i in range(2)]
                st_sum = big[0][0:1, 0:W]
                st_sq = big[0][0:1, W:2 * W]
                rep_a = big[1][:, 0:W]
                rep_b = big[1][:, W:2 * W]
                mm_ps = [ppt("p1mm", [128, 1024], F32).ap(),
                         ppt("p1mm2", [128, 1024], F32).ap()]
                mm = [mm_ps[0][:, 0:W], mm_ps[0][:, W:2 * W],
                      mm_ps[1][:, 0:W], mm_ps[1][:, W:2 * W]]

                with nc.Block(no_gpsimd_drain=True) as blk:
                    C.block = blk
                    C.dma("gp", [], [
                        (wib[:, :].rearrange("p (k e) -> p k e", e=D),
                         _r3(wi)),
                        (wfb[:, :].rearrange("p (k e) -> p k e", e=D),
                         _r3(wf_)),
                        (wgb[:, :].rearrange("p (k e) -> p k e", e=D),
                         _r3(wg)),
                        (wob[:, :].rearrange("p (k e) -> p k e", e=D),
                         _r3(wo)),
                    ])
                wdep = C.snap("gd")

                p1tail = []
                prev_ev = [[], [], [], []]
                prev_o = [[], [], [], []]
                with nc.Block(no_gpsimd_drain=True) as blk:
                    C.block = blk
                    for tc in range(NCH):
                        tsl = slice(tc * W, (tc + 1) * W)
                        ksl = (lambda buf:
                               (lambda k: buf[:, k * W:(k + 1) * W]))
                        s_x = C.dma("sy", p1tail, [
                            (xg[:, :].rearrange("p (k t) -> p k t", t=W),
                             _r3(xf)[:, :, tsl])])
                        s_ln, _ = emit_ln(
                            C, ksl(xg), xsq, st_sum, st_sq, rep_a, rep_b,
                            rows, tls, s_x + p1tail, dst_bf_sl=ksl(xnb))
                        evs = []
                        gi = 0
                        for wsb, dst in ((wib, iraw), (wfb, fraw)):
                            for m in range(NT):
                                par = gi % 4

                                def mmg(e, wsb=wsb, m=m, par=par):
                                    last = None
                                    for k in range(NT):
                                        last = e.matmul(
                                            mm[par],
                                            wsb[:, k * D + m * 128:
                                                k * D + (m + 1) * 128],
                                            xnb[:, k * W:(k + 1) * W],
                                            start=(k == 0),
                                            stop=(k == NT - 1))
                                    return last
                                s_mm = C.st("pe",
                                            s_ln + wdep + prev_ev[par], mmg)
                                s_ev = C.st("dv", s_mm,
                                            lambda e, dst=dst, m=m, par=par:
                                            e.tensor_copy(
                                                dst[:, m * W:(m + 1) * W],
                                                mm[par]))
                                prev_ev[par] = s_ev
                                evs += s_ev
                                gi += 1
                        s_ic, s_it = emit_conv(
                            C, ksl(iraw), ksl(icv), itail,
                            lambda k, j: cvi_sb[:, k * 4 + j:k * 4 + j + 1],
                            evs, NT)
                        s_fc, s_ft = emit_conv(
                            C, ksl(fraw), ksl(fcv), ftail,
                            lambda k, j: cvf_sb[:, k * 4 + j:k * 4 + j + 1],
                            evs, NT)
                        s_fg = C.st("ac", s_fc + s_ft, lambda e:
                                    e.activation(fraw[:, :], fcv[:, :],
                                                 AF.Sigmoid))
                        s_si = C.st("ac", s_ic + s_it + s_fg, lambda e:
                                    e.activation(iraw[:, :], icv[:, :],
                                                 AF.Silu))
                        s_g1 = C.st("dv", s_fg + s_si, lambda e:
                                    e.tensor_mul(icv[:, :], iraw[:, :],
                                                 fraw[:, :]))
                        s_g2 = C.st("dv", s_g1, lambda e:
                                    e.tensor_sub(icv[:, :], iraw[:, :],
                                                 icv[:, :]))

                        def scans(e):
                            last = None
                            for k in range(NT):
                                sl = slice(k * W, (k + 1) * W)
                                last = e.tensor_tensor_scan(
                                    fcv[:, sl], fraw[:, sl], icv[:, sl],
                                    hst_g[:, k:k + 1], ALU.mult, ALU.add)
                            return last
                        s_sc = C.st("dv", s_g2, scans)

                        def savest(e):
                            last = None
                            for k in range(NT):
                                last = e.tensor_copy(
                                    hst_g[:, k:k + 1],
                                    fcv[:, k * W + W - 1:k * W + W])
                            return last
                        s_hs = C.st("dv", s_sc, savest)
                        prev_g4 = [s_sc, s_sc, s_sc, s_sc]
                        for m in range(NT):
                            par = m % 4

                            def mmg2(e, m=m, par=par):
                                last = None
                                for k in range(NT):
                                    last = e.matmul(
                                        mm[par],
                                        wgb[:, k * D + m * 128:
                                            k * D + (m + 1) * 128],
                                        xnb[:, k * W:(k + 1) * W],
                                        start=(k == 0), stop=(k == NT - 1))
                                return last
                            s_mm = C.st("pe", prev_g4[par], mmg2)
                            s_ev = C.st("dv", s_mm, lambda e, m=m, par=par:
                                        e.tensor_copy(
                                            icv[:, m * W:(m + 1) * W],
                                            mm[par]))
                            prev_g4[par] = s_ev
                        prev_g = prev_g4[0] + prev_g4[1] + prev_g4[2] \
                            + prev_g4[3]
                        s_sg = C.st("ac", prev_g, lambda e: e.activation(
                            fraw[:, :], icv[:, :], AF.Silu))
                        s_rms, _ = emit_ln(
                            C, ksl(fcv), xsq, st_sum, st_sq, rep_a, rep_b,
                            rows, tls, s_sc + s_hs, rms=True,
                            dst_f32_sl=ksl(iraw))

                        def gate(e):
                            last = None
                            for k in range(NT):
                                sl = slice(k * W, (k + 1) * W)
                                last = e.scalar_tensor_tensor(
                                    ob[:, sl], iraw[:, sl],
                                    gnw_sb[:, k:k + 1], fraw[:, sl],
                                    ALU.mult, ALU.mult)
                            return last
                        s_ob = C.st("dv", s_rms + s_sg, gate)
                        prev_o = [x + s_ob for x in prev_o]
                        s_all = []
                        for m in range(NT):
                            par = m % 4

                            def mmo(e, m=m, par=par):
                                last = None
                                for k in range(NT):
                                    last = e.matmul(
                                        mm[par],
                                        wob[:, k * D + m * 128:
                                            k * D + (m + 1) * 128],
                                        ob[:, k * W:(k + 1) * W],
                                        start=(k == 0), stop=(k == NT - 1))
                                return last
                            s_mm = C.st("pe", s_ob + prev_o[par], mmo)
                            s_e = C.st("dv", s_mm, lambda e, m=m, par=par:
                                       e.tensor_add(
                                           xg[:, m * W:(m + 1) * W],
                                           xg[:, m * W:(m + 1) * W],
                                           mm[par]))
                            prev_o[par] = s_e
                            s_all += s_e
                        p1tail = C.dma("sy", s_all, [
                            (_r3(xf_sc)[:, :, tsl],
                             xg[:, :].rearrange("p (k t) -> p k t",
                                                t=W))]) + s_all
                        prev_ev = list(prev_o)
                with nc.Block(no_gpsimd_drain=True) as blk:
                    C.block = blk
                    C.st("sy", C.snap("sd"), lambda e: None)

            if "x1" in dbg_t:
                _dump(C, nc, xf_sc, dbg_t["x1"], D, "dx1")

        if "p2" in phases:
            _emit_ffn(C, nc, ST, xsq, mtfp, rufp, wcb, xf_sc, None,
                      identf, dbg_dst=dbg_t.get("x2"), tag="a",
                      wdep0=wcb_done)

        if "p3" in phases:
            _emit_mamba_a(C, nc, xsq, xf_sc, xl_sc, u_sc, z_sc,
                          winT, xpT, cw_sb, cb_sb, lng_sb, lnb_sb, utail,
                          xdbl_sc)
            if "xl" in dbg_t:
                _dump(C, nc, xl_sc, dbg_t["xl"], D, "dxl")
            if "u" in dbg_t:
                _dump(C, nc, u_sc, dbg_t["u"], DI, "du")
            if "z" in dbg_t:
                _dump(C, nc, z_sc, dbg_t["z"], DI, "dz")

        if "p4" in phases:
            _emit_mamba_b(C, nc, xsq, xf_sc, xl_sc, u_sc, z_sc, wob_sc,
                          dtpT, xdbl_sc, a_sb, dtb_sb, dd_sb, hst_m, dbg_t,
                          zs_sc, wdep0=wcb_done)
            if "x3" in dbg_t:
                _dump(C, nc, xf_sc, dbg_t["x3"], D, "dx3")

        if "p5" in phases:
            _emit_ffn(C, nc, ST, xsq, mtfp, rufp, wcb, xf_sc, y,
                      identf, tag="b")

        with nc.Block(no_gpsimd_drain=True) as blk:
            C.block = blk
            C.st("sy", C.snap("sd") + C.snap("gd") + C.snap("ad"),
                 lambda e: None)

    return nc


def _emit_mamba_a(C, nc, xsq, xf_sc, xl_sc, u_sc, z_sc, winT,
                  xpT, cw_sb, cb_sb, lng_sb, lnb_sb, utail, xdbl_sc):
    with contextlib.ExitStack() as PS:
        sb = lambda *a: PS.enter_context(nc.sbuf_tensor(*a))
        pst = lambda *a: PS.enter_context(nc.psum_tensor(*a))
        rows = sb("rows_m", [1, 4 * W], F32)
        tls = sb("tls_m", [128, W], F32)
        winb = sb("winb", [128, NT * 2 * DI], BF16)
        xpb = sb("xpb", [128, NTI * 96], BF16)
        xg = sb("xg3", [128, NT * W], F32)
        xl_t = sb("xl3", [128, NT * W], F32)
        xnb = sb("xnb3", [128, NT * W], BF16)
        ub = sb("ub3", [128, NTI * W], BF16)
        xinr = sb("xinr", [128, W], F32)
        cacc = sb("cacc", [128, W], F32)
        uu = sb("uu", [128, W], F32)
        zz_t = sb("zz_t", [128, W], F32)
        xdb_s = sb("xdb_s", [96, W], F32)
        xdb_c = sb("xdb_c", [16, W], F32)
        big = [pst(f"p3ps{i}", [128, 1024], F32).ap() for i in range(3)]
        st_sum = big[0][0:1, 0:W]
        st_sq = big[0][0:1, W:2 * W]
        rep_a = big[1][:, 0:W]
        rep_b = big[1][:, W:2 * W]
        mm = [big[2][:, 0:W], big[2][:, W:2 * W],
              big[1][:, 0:W], big[1][:, W:2 * W]]
        ps_x = pst("p3psx", [128, 1024], F32).ap()
        ps_dt = ps_x[0:64, 0:W]
        ps_bb = ps_x[0:16, W:2 * W]
        ps_cc = ps_x[32:48, W:2 * W]

        with nc.Block(no_gpsimd_drain=True) as blk:
            C.block = blk
            C.dma("gp", [], [
                (winb[:, :].rearrange("p (k e) -> p k e", e=2 * DI),
                 _r3(winT)),
                (xpb[:, :].rearrange("p (k e) -> p k e", e=96),
                 xpT.rearrange("(k p) e -> p k e", p=128)),
            ])
        wdep = C.snap("gd")

        p3tail = []
        prev_ev = [[], [], [], []]
        prev_ust = []
        prev_zst = []
        prev_xst = []
        prev_xlst = []
        with nc.Block(no_gpsimd_drain=True) as blk:
            C.block = blk
            for tc in range(NCH):
                tsl = slice(tc * W, (tc + 1) * W)
                s_x = C.dma("sy", p3tail, [
                    (xg[:, :].rearrange("p (k t) -> p k t", t=W),
                     _r3(xf_sc)[:, :, tsl])])
                s_ln, _ = emit_ln(
                    C, lambda k: xg[:, k * W:(k + 1) * W], xsq, st_sum,
                    st_sq, rep_a, rep_b, rows, tls,
                    s_x + p3tail + prev_xlst,
                    dst_bf_sl=lambda k: xnb[:, k * W:(k + 1) * W],
                    dst_f32_sl=lambda k: xl_t[:, k * W:(k + 1) * W],
                    gb=(lambda k: lng_sb[:, k:k + 1],
                        lambda k: lnb_sb[:, k:k + 1]))
                s_xs = C.st("dv", s_ln, lambda e:
                            e.tensor_add(xl_t[:, :], xl_t[:, :],
                                         xg[:, :]))
                prev_xlst = C.dma("sy", s_xs, [
                    (_r3(xl_sc)[:, :, tsl],
                     xl_t[:, :].rearrange("p (k t) -> p k t", t=W))])
                u_done = []
                for m in range(2 * NTI):
                    par = m % 4

                    def mmi(e, m=m, par=par):
                        last = None
                        for k in range(NT):
                            last = e.matmul(
                                mm[par],
                                winb[:, k * 2 * DI + m * 128:
                                     k * 2 * DI + (m + 1) * 128],
                                xnb[:, k * W:(k + 1) * W],
                                start=(k == 0), stop=(k == NT - 1))
                        return last
                    s_mm = C.st("pe", s_ln + wdep + prev_ev[par], mmi)
                    if m < NTI:
                        s_e = C.st("dv", s_mm, lambda e, par=par:
                                   e.tensor_copy(xinr[:, :], mm[par]))
                        s_cv, s_ut = emit_conv(
                            C, lambda k: xinr[:, :], lambda k: cacc[:, :],
                            utail[:, m * 3:(m + 1) * 3],
                            lambda kk, j, m=m: cw_sb[:, m * 4 + j:
                                                     m * 4 + j + 1],
                            s_e, 1)
                        s_u = C.st("ac", s_cv + s_ut + prev_ust,
                                   lambda e, m=m:
                                   e.activation(uu[:, :], cacc[:, :],
                                                AF.Silu,
                                                bias=cb_sb[:, m:m + 1]))
                        s_ub = C.st("dv", s_u, lambda e, m=m:
                                    e.tensor_copy(ub[:, m * W:(m + 1) * W],
                                                  uu[:, :]))
                        s_st = C.dma("sy", s_u, [
                            (u_sc[m * 128:(m + 1) * 128, tsl], uu[:, :])])
                        prev_ust = s_st
                        u_done += s_ub
                        prev_ev[par] = s_e + s_st + s_ub
                    else:
                        s_e = C.st("dv", s_mm + prev_zst, lambda e, par=par:
                                   e.tensor_copy(zz_t[:, :], mm[par]))
                        s_st = C.dma("sy", s_e, [
                            (z_sc[(m - NTI) * 128:(m - NTI + 1) * 128, tsl],
                             zz_t[:, :])])
                        prev_zst = s_st
                        prev_ev[par] = s_e + s_st

                def mk_mm(dst, lo, hi):
                    def f(e):
                        last = None
                        for k in range(NTI):
                            last = e.matmul(
                                dst, xpb[:, k * 96 + lo:k * 96 + hi],
                                ub[:, k * W:(k + 1) * W],
                                start=(k == 0), stop=(k == NTI - 1))
                        return last
                    return f
                s_m1 = C.st("pe", u_done + prev_ev[0] + prev_ev[1],
                            mk_mm(ps_dt, 0, 64))
                s_m2 = C.st("pe", [], mk_mm(ps_bb, 64, 80))
                s_m3 = C.st("pe", [], mk_mm(ps_cc, 80, 96))

                def evx(e):
                    e.tensor_copy(xdb_s[0:64, :], ps_dt)
                    e.tensor_copy(xdb_s[64:80, :], ps_bb)
                    return e.tensor_copy(xdb_c[:, :], ps_cc)
                s_ev = C.st("dv", s_m1 + s_m2 + s_m3 + prev_xst, evx)
                prev_xst = C.dma("sy", s_ev, [
                    (xdbl_sc[0:80, tsl], xdb_s[0:80, :]),
                    (xdbl_sc[80:96, tsl], xdb_c[:, :])])
                p3tail = prev_xst + s_ev
        with nc.Block(no_gpsimd_drain=True) as blk:
            C.block = blk
            C.st("sy", C.snap("sd"), lambda e: None)


def _emit_mamba_b(C, nc, xsq, xf_sc, xl_sc, u_sc, z_sc, wob_sc, dtpT,
                  xdbl_sc, a_sb, dtb_sb, dd_sb, hst_m, dbg_t, zs_sc,
                  wdep0=()):
    with contextlib.ExitStack() as PS:
        sb = lambda *a: PS.enter_context(nc.sbuf_tensor(*a))
        pst = lambda *a: PS.enter_context(nc.psum_tensor(*a))
        dtpb = sb("dtpb", [64, DI], BF16)
        dtrb = sb("dtrb", [64, W], BF16)
        brep = sb("brep", [128, NS * W], F32)
        crep = sb("crep", [128, NS * W], F32)
        bbuf = [sb("bb0", [128, 4 * W], F32), sb("bb1", [128, 4 * W], F32),
                sb("bb2", [128, 4 * W], F32)]
        ab2 = sb("ab2", [128, 4 * W], F32)
        hcb = sb("hcb0", [128, 4 * W], F32)
        dt2 = [sb("dt20", [128, W], F32), sb("dt21", [128, W], F32)]
        sp1 = sb("sp1", [128, W], F32)
        yb = [sb("yb4a", [128, NTI * W], BF16),
              sb("yb4b", [128, NTI * W], BF16)]
        u_t = [sb("u_t0", [128, W], F32), sb("u_t1", [128, W], F32)]
        z_t = [sb("z_t0", [128, W], F32), sb("z_t1", [128, W], F32)]
        dtu = sb("dtu", [128, W], F32)
        yt2 = sb("yt2", [128, W], F32)
        fixt = sb("fixt", [128, 8], F32)
        wou = [sb("wou0", [128, NTI * 128], BF16),
               sb("wou1", [128, NTI * 128], BF16)]
        xg_t = sb("xg4", [128, W], F32)
        xl_t = sb("xl4", [128, W], F32)
        spsc = sb("spsc", [128, 2 * W], F32)
        big = pst("p4ps", [128, 1024], F32).ap()
        ps_r = [big[:, 0:W], big[:, W:2 * W]]
        pso_t = [pst("p4pso0", [128, 1024], F32).ap(),
                 pst("p4pso1", [128, 1024], F32).ap()]
        ps_o = [pso_t[0][:, 0:W], pso_t[0][:, W:2 * W],
                pso_t[1][:, 0:W], pso_t[1][:, W:2 * W]]
        abuf = [xsq[:, 0:4 * W], xsq[:, 4 * W:8 * W], ab2[:, :]]

        with nc.Block(no_gpsimd_drain=True) as blk:
            C.block = blk
            C.dma("gp", [], [(dtpb[:, :], dtpT)])
        wdep = C.snap("gd")

        scan_done = [[], [], []]
        gp_done = [[], [], []]
        uld = [[], []]
        zld = [[], []]
        hs_done = [[], [], []]
        ybfree2 = [[], []]
        pend_out = [None]
        brep_rd = [[], [], [], []]
        crep_rd = [[], [], [], []]
        prev_dbg = []
        prev_dtmm = [[], []]
        s_wprev = [list(wdep0), list(wdep0)]
        prev_out = [[], [], [], []]
        ps_free = [[], []]
        ps_free4 = [[], [], [], []]
        ybd = [[]]


        with nc.Block(no_gpsimd_drain=True) as blk:
            C.block = blk
            xgp = [xg_t, spsc[:, 0:W], xl_t, spsc[:, W:2 * W]]

            def _emit_out_m(po, m):
                ptc, pybd = po
                ptsl = slice(ptc * W, (ptc + 1) * W)
                ybuf = yb[ptc % 2]
                par = m % 2
                p4r = m % 4
                s_w = C.dma("sy", s_wprev[par], [
                    (wou[par][:, :].rearrange("p (k e) -> p k e", e=128),
                     _r3(wob_sc)[:, :, m * 128:(m + 1) * 128])])

                def mmo(e, m=m, par=par, p4r=p4r, ybuf=ybuf):
                    last = None
                    for kk in range(NTI):
                        last = e.matmul(
                            ps_o[p4r],
                            wou[par][:, kk * 128:(kk + 1) * 128],
                            ybuf[:, kk * W:(kk + 1) * W],
                            start=(kk == 0), stop=(kk == NTI - 1))
                    return last
                s_mm = C.st("pe", pybd + s_w + ps_free4[p4r], mmo)
                s_wprev[par] = s_mm
                ybfree2[ptc % 2] = ybfree2[ptc % 2] + s_mm
                s_x = C.dma("sy", prev_out[p4r], [
                    (xgp[p4r][:, :],
                     xl_sc[m * 128:(m + 1) * 128, ptsl])])
                s_r2 = C.st("dv", s_mm + s_x, lambda e, p4r=p4r:
                            e.tensor_add(xgp[p4r][:, :], xgp[p4r][:, :],
                                         ps_o[p4r]))
                s_st = C.dma("ac", s_r2, [
                    (xf_sc[m * 128:(m + 1) * 128, ptsl],
                     xgp[p4r][:, :])])
                prev_out[p4r] = s_st + s_r2
                ps_free4[p4r] = s_r2

            for tc in range(NCH):
                tsl = slice(tc * W, (tc + 1) * W)
                ydep_cur = ybfree2[tc % 2]
                ybfree2[tc % 2] = []
                s_db = C.dma("gp", prev_dbg + prev_dtmm[0] + prev_dtmm[1],
                             [(dtrb[:, :], xdbl_sc[0:64, tsl])])
                def _bc(r0, n0, n1):
                    return xdbl_sc[r0 + n0:r0 + n1, tsl].rearrange(
                        "n (o t) -> o n t", o=1).broadcast_to(
                        [128, n1 - n0, W])
                s_repb = []
                s_repc = []
                for qq in range(4):
                    nsl4 = slice(qq * 4 * W, (qq + 1) * 4 * W)
                    s_repb.append(C.dma("sy", brep_rd[qq], [
                        (brep[:, nsl4].rearrange("p (n t) -> p n t", n=4),
                         _bc(64, qq * 4, qq * 4 + 4))]))
                for qq in range(4):
                    nsl4 = slice(qq * 4 * W, (qq + 1) * 4 * W)
                    s_repc.append(C.dma("sy", crep_rd[qq], [
                        (crep[:, nsl4].rearrange("p (n t) -> p n t", n=4),
                         _bc(80, qq * 4, qq * 4 + 4))]))

                def dt_build(k):
                    p2 = k % 2
                    par = k % 2

                    def mmdt(e, k=k, par=par):
                        return e.matmul(
                            ps_r[par], dtpb[:, k * 128:(k + 1) * 128],
                            dtrb[:, :], start=True, stop=True)
                    s_mm = C.st("pe", s_db + wdep + ps_free[par]
                                + prev_dtmm[par], mmdt)
                    prev_dtmm[par] = s_mm

                    def evd(e, k=k, par=par, p2=p2):
                        return e.tensor_scalar_add(
                            dt2[p2][:, :], ps_r[par], dtb_sb[:, k:k + 1])
                    s_ev = C.st("dv", s_mm + uld[p2], evd)
                    ps_free[par] = s_ev
                    s_ab = C.st("ac", s_ev + sp_free[0], lambda e, p2=p2:
                                e.activation(sp1[:, :], dt2[p2][:, :],
                                             AF.Abs))
                    s_e1 = C.st("ac", s_ab, lambda e: e.activation(
                        sp1[:, :], sp1[:, :], AF.Exp, scale=-1.0))
                    s_rl = C.st("ac", s_ab, lambda e, p2=p2:
                                e.activation(dt2[p2][:, :], dt2[p2][:, :],
                                             AF.Relu))
                    s_e2 = C.st("ac", s_e1, lambda e: e.activation(
                        sp1[:, :], sp1[:, :], AF.Ln, bias=1.0))
                    def tail(p2=p2, s_e2=s_e2, s_rl=s_rl):
                        s_ad = C.st("dv", s_e2 + s_rl, lambda e, p2=p2:
                                    e.tensor_add(dt2[p2][:, :],
                                                 dt2[p2][:, :],
                                                 sp1[:, :]))
                        sp_free[0] = s_ad
                        return s_ad
                    return tail

                sp_free = [[]]
                s_dt = [None, None]
                s_dt[0] = dt_build(0)()
                for k in range(NTI):
                    p2 = k % 2
                    s_u = C.dma("sy", uld[p2], [
                        (u_t[p2][:, :], u_sc[k * 128:(k + 1) * 128, tsl])])
                    s_zt = C.dma("sy", zld[p2], [
                        (z_t[p2][:, :], z_sc[k * 128:(k + 1) * 128, tsl])])
                    s_sz = C.st("ac", s_zt + zld[p2], lambda e, p2=p2:
                                e.activation(z_t[p2][:, :], z_t[p2][:, :],
                                             AF.Silu))
                    s_du = C.st("dv", s_u + s_dt[p2], lambda e, p2=p2:
                                e.tensor_mul(dtu[:, :], dt2[p2][:, :],
                                             u_t[p2][:, :]))

                    def emit_prep(q):
                        p = (4 * k + q) % 3
                        nsl = slice(q * 4 * W, (q + 1) * 4 * W)
                        hsl = slice(k * NS + q * 4, k * NS + q * 4 + 4)

                        def a_exp(e, k=k, q=q, p=p, p2=p2):
                            last = None
                            for j in range(4):
                                n = q * 4 + j
                                last = e.activation(
                                    abuf[p][:, j * W:(j + 1) * W],
                                    dt2[p2][:, :], AF.Exp,
                                    scale=a_sb[:, k * NS + n:
                                               k * NS + n + 1])
                            return last
                        s_a = C.st("ac", s_dt[p2] + scan_done[p], a_exp)

                        def bbody(e, p=p, q=q):
                            last = None
                            for nn in range(4):
                                last = e.tensor_mul(
                                    bbuf[p][:, nn * W:(nn + 1) * W],
                                    brep[:, (q * 4 + nn) * W:
                                         (q * 4 + nn + 1) * W],
                                    dtu[:, :])
                            return last
                        s_b = C.st("dv", s_repb[q] + s_du + hs_done[p],
                                   bbody)
                        brep_rd[q] = s_b
                        acols = abuf[p][:, :].rearrange(
                            "p (n t) -> p n t", n=4)[:, :, 0:1]
                        bcols = bbuf[p][:, :].rearrange(
                            "p (n t) -> p n t", n=4)[:, :, 0:1]
                        ecols = bbuf[p][:, :].rearrange(
                            "p (n t) -> p n t", n=4)[:, :, W - 1:W]
                        hcols = hst_m[:, hsl].rearrange(
                            "p (a o) -> p a o", o=1)
                        fo = 4 * (q % 2)
                        ytmp = fixt[:, fo:fo + 4].rearrange(
                            "p (a o) -> p a o", o=1)
                        s_f1 = C.st("gp", s_a + s_b, lambda e, acols=acols,
                                    hcols=hcols, ytmp=ytmp:
                                    e.tensor_mul(ytmp, acols, hcols))
                        s_f2 = C.st("gp", s_f1, lambda e, bcols=bcols,
                                    ytmp=ytmp:
                                    e.tensor_add(bcols, bcols, ytmp))
                        s_f3 = C.st("gp", s_f2, lambda e, acols=acols:
                                    e.memset(acols, 0.0))
                        return (p, nsl, s_f3, ecols, hcols)

                    prep = [emit_prep(0), emit_prep(1), emit_prep(2),
                            None]
                    dt_tail = dt_build(k + 1) if k + 1 < NTI else None
                    for q in range(4):
                        if q == 2 and dt_tail is not None:
                            s_dt[(k + 1) % 2] = dt_tail()
                        if q == 3:
                            prep[3] = emit_prep(3)
                        p, nsl, s_f3, ecols, hcols = prep[q]
                        s_scan = C.st("dv", s_f3,
                                      lambda e, p=p:
                                      e.tensor_tensor_scan(
                                          bbuf[p][:, :], abuf[p][:, :],
                                          bbuf[p][:, :], 0.0, ALU.mult,
                                          ALU.add))
                        scan_done[p] = s_scan
                        hs_done[p] = C.st("gp", s_scan,
                                          lambda e, hcols=hcols,
                                          ecols=ecols:
                                          e.tensor_copy(hcols, ecols))
                        s_hc = C.st("dv", s_repc[q],
                                    lambda e, p=p, nsl=nsl:
                                    e.tensor_mul(hcb[:, :], bbuf[p][:, :],
                                                 crep[:, nsl]))
                        gp_done[p] = s_hc
                        crep_rd[q] = s_hc
                        C.st("dv", [], lambda e:
                             e.tensor_add(hcb[:, 0:2 * W], hcb[:, 0:2 * W],
                                          hcb[:, 2 * W:4 * W]))
                        if q == 0:
                            C.st("dv", [], lambda e:
                                 e.tensor_add(yt2[:, :], hcb[:, 0:W],
                                              hcb[:, W:2 * W]))
                        else:
                            C.st("dv", [], lambda e:
                                 e.tensor_add(yt2[:, :], yt2[:, :],
                                              hcb[:, 0:W]))
                            C.st("dv", [], lambda e:
                                 e.tensor_add(yt2[:, :], yt2[:, :],
                                              hcb[:, W:2 * W]))
                    s_y3 = C.st("dv", [], lambda e, k=k, p2=p2:
                                e.scalar_tensor_tensor(
                                    yt2[:, :], u_t[p2][:, :],
                                    dd_sb[:, k:k + 1], yt2[:, :],
                                    ALU.mult, ALU.add))
                    s_y4 = C.st("dv", s_y3 + s_sz + ydep_cur,
                                lambda e, k=k, p2=p2, tc=tc:
                                e.tensor_mul(
                                    yb[tc % 2][:, k * W:(k + 1) * W],
                                    yt2[:, :], z_t[p2][:, :]))
                    uld[p2] = s_y3
                    zld[p2] = s_y4
                    ybd[0] = s_y4
                    if pend_out[0] is not None and k < NT:
                        _emit_out_m(pend_out[0], k)
                pend_out[0] = (tc, ybd[0])
                if "dtv" in dbg_t:
                    prev_dbg = []
            for m in range(NT):
                _emit_out_m(pend_out[0], m)
        with nc.Block(no_gpsimd_drain=True) as blk:
            C.block = blk
            C.st("sy", C.snap("sd"), lambda e: None)


def _dump(C, nc, src, dst, rows_n, tag):
    with contextlib.ExitStack() as ES:
        tmp = ES.enter_context(nc.sbuf_tensor(f"dump_{tag}", [128, T], F32))
        with nc.Block(no_gpsimd_drain=True) as blk:
            C.block = blk
            prev = []
            for r in range(rows_n // 128):
                s = C.dma("sy", prev, [(tmp[:, :],
                                        src[r * 128:(r + 1) * 128, :])])
                prev = C.dma("sy", s, [(dst[r * 128:(r + 1) * 128, :],
                                        tmp[:, :])])
            C.st("sy", C.snap("sd"), lambda e: None)


def _vec(v, nk):
    v = np.asarray(v)
    if v.ndim == 1:
        return np.ascontiguousarray(v.reshape(nk, 128).T)
    c = v.shape[1]
    return np.ascontiguousarray(
        v.reshape(nk, 128, c).transpose(1, 0, 2).reshape(128, nk * c))


_PACK_CACHE = {}


def _pack_ffn(inputs):
    key = "ffn"
    if key in _PACK_CACHE:
        return _PACK_CACHE[key]
    mkTh = np.asarray(inputs["L0_ffn_rail_maker"]).transpose(0, 1, 3, 2)
    ruTh = np.asarray(inputs["L0_ffn_rail_up_proj"]).transpose(0, 1, 3, 2)
    mtfp = np.zeros((128, 64 * 128), np.float32)
    rufp = np.zeros((128, 64 * 128), np.float32)
    for hd in range(4):
        for m in range(32):
            slot = 2 * (hd * 8 + m // 4) + (m % 2)
            off = slot * 128
            pbase = 64 * ((m // 2) % 2)
            for c in range(2):
                rail = 4 * (m // 2) + 2 * (m % 2) + c
                rr = 2 * (m % 2) + c
                r0 = pbase + rr * 16
                mtfp[r0:r0 + 16, off + r0:off + r0 + 16] = mkTh[hd, rail]
                rufp[r0:r0 + 16,
                     off + c * 64:off + (c + 1) * 64] = ruTh[hd, rail]
    _PACK_CACHE[key] = (np.ascontiguousarray(mtfp),
                        np.ascontiguousarray(rufp))
    return _PACK_CACHE[key]


def prep_inputs(inputs):
    x = np.asarray(inputs["L0_attn_a_x"])
    base = {
        "wi": np.asarray(inputs["L0_attn_a_Wi"]),
        "wf": np.asarray(inputs["L0_attn_a_Wf"]),
        "wg": np.asarray(inputs["L0_attn_a_Wg"]),
        "wo": np.asarray(inputs["L0_attn_a_Wo"]),
        "cvi": _vec(inputs["L0_attn_a_conv_i"], NT),
        "cvf": _vec(inputs["L0_attn_a_conv_f"], NT),
        "gnw": _vec(inputs["L0_attn_a_g_norm_weight"], NT),
        "lng": _vec(inputs["L0_attn_b_ln_gamma"], NT),
        "lnb": _vec(inputs["L0_attn_b_ln_beta"], NT),
        "winT": np.ascontiguousarray(
            np.asarray(inputs["L0_attn_b_in_proj_w"]).T),
        "cw": _vec(inputs["L0_attn_b_conv_w"], NTI),
        "cb": _vec(inputs["L0_attn_b_conv_b"], NTI),
        "xpT": np.ascontiguousarray(
            np.asarray(inputs["L0_attn_b_x_proj_w"]).T),
        "dtpT": np.ascontiguousarray(
            np.asarray(inputs["L0_attn_b_dt_proj_w"]).T),
        "dtb": _vec(inputs["L0_attn_b_dt_proj_b"], NTI),
        "alog": _vec(inputs["L0_attn_b_A_log"], NTI),
        "dd": _vec(inputs["L0_attn_b_D"], NTI),
        "woutT": np.ascontiguousarray(
            np.asarray(inputs["L0_attn_b_out_proj_w"]).T),
        "mk": np.asarray(inputs["L0_ffn_rail_maker"]),
        "mkT": np.ascontiguousarray(
            np.asarray(inputs["L0_ffn_rail_maker"]).transpose(0, 1, 3, 2)),
        "mtfp": _pack_ffn(inputs)[0],
        "rufp": _pack_ffn(inputs)[1],
        "ruT": np.ascontiguousarray(
            np.asarray(inputs["L0_ffn_rail_up_proj"]).transpose(0, 1, 3, 2)),
        "wcT": np.ascontiguousarray(np.asarray(inputs["L0_ffn_w_cproj"]).T),
    }
    in_maps = []
    for b in range(8):
        m = dict(base)
        m["xf"] = np.ascontiguousarray(x[b].T)
        in_maps.append(m)
    return in_maps


def kernel(**inputs):
    in_maps = prep_inputs(inputs)
    nc = build_program()
    res = run_bass_kernel_spmd(nc, in_maps, core_ids=list(range(8)))
    return np.stack([r["y"] for r in res.results], axis=0)


# version 27 (final)
# speedup vs baseline: 1.0763x; 1.0013x over previous
import contextlib
import numpy as np
import concourse.bass as bass
import concourse.mybir as mybir
from concourse.bass_utils import run_bass_kernel_spmd

F32 = mybir.dt.float32
BF16 = mybir.dt.bfloat16
AF = mybir.ActivationFunctionType
ALU = mybir.AluOpType

T = 4096
D = 1024
W = 512
NCH = T // W
NT = D // 128
DI = 2048
NTI = DI // 128
NS = 16
GW = 1024
NG = T // GW
LN_EPS = 1e-5

_ENG = {"pe": "tensor", "dv": "vector", "ac": "scalar", "gp": "gpsimd",
        "sy": "sync"}


class Ctx:

    def __init__(self, nc):
        self.nc = nc
        self.block = None
        self.sems = {}

    def init_sems(self, stack):
        for name in ("pe", "dv", "ac", "gp", "sy"):
            h = stack.enter_context(self.nc.semaphore(f"sem_{name}"))
            self.sems[name] = [h, 0]
        self.lanes = {}
        self.lane_rr = {}
        for q, nl in (("sd", 8), ("gd", 6), ("ad", 4)):
            self.lanes[q] = []
            self.lane_rr[q] = 0
            for i in range(nl):
                h = stack.enter_context(self.nc.semaphore(f"lane_{q}{i}"))
                self.lanes[q].append([h, 0])

    def snap(self, *names):
        out = []
        for n in names:
            if n in self.sems:
                out.append((self.sems[n][0], self.sems[n][1]))
            else:
                out += [(h, v) for h, v in self.lanes[n]]
        return out

    def _waits(self, eng, deps):
        merged = {}
        for s, v in deps:
            if v > 0 and merged.get(s, 0) < v:
                merged[s] = v
        for s, v in merged.items():
            eng.wait_ge(s, v)

    def st(self, sel, deps, emit):
        sem, _ = self.sems[sel]

        def f(eng):
            self._waits(eng, deps)
            last = emit(eng)
            if last is not None:
                last.then_inc(sem, 1)
        getattr(self.block, _ENG[sel])(f)
        self.sems[sel][1] += 1
        return self.snap(sel)

    def dma(self, sel, deps, pairs):
        q = {"sy": "sd", "gp": "gd", "ac": "ad"}[sel]
        lanes = self.lanes[q]
        plan = []
        for _ in pairs:
            li = self.lane_rr[q] % len(lanes)
            self.lane_rr[q] += 1
            lane = lanes[li]
            plan.append((lane, lane[1], lane[1] + 16))
            lane[1] += 16
        out_snap = [(lane[0], after) for lane, _, after in plan]

        def f(eng):
            self._waits(eng, deps)
            for (lane, before, _), (out, in_) in zip(plan, pairs):
                if before > 0:
                    eng.wait_ge(lane[0], before)
                eng.dma_start(out=out, in_=in_).then_inc(lane[0], 16)
        getattr(self.block, _ENG[sel])(f)
        return out_snap


def emit_ln(C, src_sl, xsq, st_sum, st_sq, rep_a, rep_b, rows, tls, dep, *,
            rms=False, dst_bf_sl=None, dst_f32_sl=None, gb=None, n_k=NT):
    inv_n = 1.0 / (n_k * 128)
    ones1, onesr = C.ones1, C.onesr
    sum_row = rows[0:1, 0:W]
    sq_row = rows[0:1, W:2 * W]
    a_row = rows[0:1, 2 * W:3 * W]
    b_row = rows[0:1, 3 * W:4 * W]

    def sq(e):
        last = None
        for k in range(n_k):
            last = e.activation(xsq[:, k * W:(k + 1) * W], src_sl(k),
                                AF.Square)
        return last
    s_sq = C.st("ac", dep, sq)

    def mm_stats(e):
        last = None
        if not rms:
            for k in range(n_k):
                last = e.matmul(st_sum, ones1[:, 0:1], src_sl(k),
                                start=(k == 0), stop=(k == n_k - 1))
        for k in range(n_k):
            last = e.matmul(st_sq, ones1[:, 0:1],
                            xsq[:, k * W:(k + 1) * W],
                            start=(k == 0), stop=(k == n_k - 1))
        return last
    s_mm = C.st("pe", dep + s_sq, mm_stats)

    if rms:
        s = C.st("dv", s_mm, lambda e: e.tensor_copy(sq_row, st_sq))
        s = C.st("dv", s, lambda e: e.tensor_scalar_mul(sq_row, sq_row,
                                                        inv_n))
        s = C.st("ac", s, lambda e: e.activation(a_row, sq_row, AF.Ln,
                                                 bias=C.eps1))
        s = C.st("ac", s, lambda e: e.activation(a_row, a_row, AF.Exp,
                                                 scale=C.mhalf1))
        s_rep = C.st("pe", s, lambda e: e.matmul(rep_a, onesr[0:1, :], a_row,
                                                 start=True, stop=True))
        done = []
        for k in range(n_k):
            done += C.st("dv", s_rep, lambda e, k=k: e.tensor_mul(
                dst_f32_sl(k), src_sl(k), rep_a))
        return done, s

    def cps(e):
        e.tensor_copy(sum_row, st_sum)
        return e.tensor_copy(sq_row, st_sq)
    s = C.st("dv", s_mm, cps)

    def mumath(e):
        e.tensor_scalar_mul(sum_row, sum_row, inv_n)
        return e.tensor_scalar_mul(sq_row, sq_row, inv_n)
    s = C.st("dv", s, mumath)
    s = C.st("dv", s, lambda e: e.tensor_mul(b_row, sum_row, sum_row))
    s = C.st("dv", s, lambda e: e.tensor_sub(sq_row, sq_row, b_row))
    s = C.st("ac", s, lambda e: e.activation(a_row, sq_row, AF.Ln,
                                             bias=C.eps1))
    s = C.st("ac", s, lambda e: e.activation(a_row, a_row, AF.Exp,
                                             scale=C.mhalf1))
    s = C.st("dv", s, lambda e: e.tensor_mul(b_row, sum_row, a_row))
    s = C.st("dv", s, lambda e: e.tensor_scalar_mul(b_row, b_row, -1.0))

    def mmrep(e):
        e.matmul(rep_a, onesr[0:1, :], a_row, start=True, stop=True)
        return e.matmul(rep_b, onesr[0:1, :], b_row, start=True, stop=True)
    s_rep = C.st("pe", s, mmrep)

    done = []
    for k in range(n_k):
        s_t = C.st("dv", s_rep, lambda e, k=k: e.tensor_mul(
            tls[:, 0:W], src_sl(k), rep_a))
        if dst_f32_sl is not None:
            s_x = C.st("dv", s_t, lambda e, k=k: e.tensor_add(
                dst_f32_sl(k), tls[:, 0:W], rep_b))
            if dst_bf_sl is not None:
                g, b = gb
                s_x = C.st("dv", s_x, lambda e, k=k: e.tensor_scalar(
                    dst_bf_sl(k), dst_f32_sl(k), g(k), b(k), ALU.mult,
                    ALU.add))
        else:
            s_x = C.st("dv", s_t, lambda e, k=k: e.tensor_add(
                dst_bf_sl(k), tls[:, 0:W], rep_b))
        done += s_x
    return done, s_rep


def emit_conv(C, raw_sl, acc_sl, tail, wcol, dep, n_k):
    def tap3(e):
        last = None
        for k in range(n_k):
            last = e.tensor_scalar_mul(acc_sl(k), raw_sl(k), wcol(k, 3))
        return last
    s = C.st("dv", dep, tap3)
    for j in (2, 1, 0):
        sft = 3 - j

        def tapj(e, j=j, sft=sft):
            last = None
            for k in range(n_k):
                acc = acc_sl(k)
                raw = raw_sl(k)
                last = e.scalar_tensor_tensor(
                    acc[:, sft:W], raw[:, 0:W - sft], wcol(k, j),
                    acc[:, sft:W], ALU.mult, ALU.add)
                last = e.scalar_tensor_tensor(
                    acc[:, 0:sft], tail[:, k * 3 + 3 - sft:k * 3 + 3],
                    wcol(k, j), acc[:, 0:sft], ALU.mult, ALU.add)
            return last
        s = C.st("dv", s, tapj)

    def upd(e):
        last = None
        for k in range(n_k):
            last = e.tensor_copy(tail[:, k * 3:(k + 1) * 3],
                                 raw_sl(k)[:, W - 3:W])
        return last
    s_tail = C.st("dv", s, upd)
    return s, s_tail


def _r3(ap, p=128):
    return ap.rearrange("(k p) t -> p k t", p=p)


def _emit_ffn(C, nc, ST, xsq, mtfp, rufp, wcb, xf_sc, y, identf,
              dbg_dst=None, tag="", wdep0=()):
    with contextlib.ExitStack() as PS:
        sb = lambda *a: PS.enter_context(nc.sbuf_tensor(*a))
        pst = lambda *a: PS.enter_context(nc.psum_tensor(*a))
        rub = sb("rub" + tag, [128, 64 * 128], BF16)

        with contextlib.ExitStack() as QS:
            mtf = QS.enter_context(nc.sbuf_tensor("mtf" + tag,
                                                  [128, 64 * 128], F32))
            ruf = QS.enter_context(nc.sbuf_tensor("ruf" + tag, [128, 64 * 128],
                                                  F32))
            psW = QS.enter_context(nc.psum_tensor("psW" + tag,
                                                  [128, 512], F32))
            psWa = psW.ap()
            with nc.Block(no_gpsimd_drain=True) as blk:
                C.block = blk

                def zz(e):
                    e.memset(mtf[:, :], 0.0)
                    return e.memset(ruf[:, :], 0.0)
                s_z = C.st("gp", [], zz)
                qw = 16 * 128
                pre_pairs = []
                for qq in range(4):
                    sl = slice(qq * qw, (qq + 1) * qw)
                    pre_pairs.append((mtf[:, sl], mtfp[:, sl]))
                    pre_pairs.append((ruf[:, sl], rufp[:, sl]))
                s_d1 = C.dma("sy", s_z, pre_pairs)
                s_d2 = s_d1
                ev_prev = [[], [], [], []]
                for slot in range(64):
                    off = slot * 128
                    par = slot % 4
                    psl = psWa[:, par * 128:(par + 1) * 128]

                    def mmw(e, off=off, psl=psl):
                        return e.matmul(psl, mtf[:, off:off + 128],
                                        ruf[:, off:off + 128],
                                        start=True, stop=True)
                    s_mw = C.st("pe", s_d1 + s_d2 + ev_prev[par], mmw)
                    s_ev = C.st("dv", s_mw, lambda e, off=off, psl=psl:
                                e.tensor_copy(rub[:, off:off + 128], psl))
                    ev_prev[par] = s_ev
                s_wb = ev_prev[0] + ev_prev[1]

        rows = sb("rows_f" + tag, [1, 4 * W], F32)
        tls = sb("tls_f" + tag, [128, W], F32)
        xnb = sb("xnb_f" + tag, [128, NT * GW], BF16)
        hh = sb("hh" + tag, [128, 32 * GW], BF16)
        wcf = [sb("wcf0" + tag, [128, 32 * 128], BF16),
               sb("wcf1" + tag, [128, 32 * 128], BF16)]
        oac = sb("oac" + tag, [128, NT * GW], F32)
        xgs = sb("xgs" + tag, [128, NT * W], F32)
        zev = sb("zev" + tag, [128, GW], F32)
        rsx = zev
        ytok = zev

        big = [pst(f"fps{i}" + tag, [128, 1024], F32).ap()
               for i in range(4)]
        st_sum = big[0][0:1, 0:W]
        st_sq = big[0][0:1, W:2 * W]
        rep_a = big[1][:, 0:W]
        rep_b = big[1][:, W:2 * W]
        ps_y = big[2]
        ps_zs2 = [big[0], big[1], big[3], big[2]]
        ps_cs = [big[3], big[2]]
        ps_tr = big[1]

        ffn_tail = []
        last_maker = []
        s_ye_prev = [[], []]
        ec_prev = [[], []]
        wc_prev = [list(wdep0), list(wdep0)]
        s_gel_prev2 = [[], []]
        prev_half = []
        with nc.Block(no_gpsimd_drain=True) as blk:
            C.block = blk
            for g in range(NG):
                s_gel_prev = s_gel_prev + ffn_tail if g else []
                s_ln_all = []
                for hf in range(2):
                    c0 = g * GW + hf * W
                    s_x = C.dma("sy", prev_half + last_maker, [
                        (xgs[:, :].rearrange("p (k t) -> p k t", t=W),
                         _r3(xf_sc)[:, :, c0:c0 + W])])
                    psdep = (s_gel_prev + ec_prev[0] + ec_prev[1]
                             + ffn_tail) if g else []
                    s_ln, s_lrep = emit_ln(
                        C, lambda k: xgs[:, k * W:(k + 1) * W], xsq,
                        st_sum, st_sq, rep_a, rep_b, rows, tls,
                        s_x + psdep,
                        dst_bf_sl=lambda k: xnb[:, k * GW + hf * W:
                                                k * GW + hf * W + W])
                    s_ln_all += s_ln
                    prev_half = s_ln + s_lrep
                gel_last = []
                gelq = []
                s_gel_prev = s_gel_prev if g else []
                s_gel_prev2 = [s_ln_all + s_gel_prev2[0] + s_wb,
                               s_ln_all + s_gel_prev2[1] + s_wb,
                               s_ln_all + (s_gel_prev2[2]
                                           if len(s_gel_prev2) > 2 else
                                           ec_prev[0]) + s_wb,
                               s_ln_all + (s_gel_prev2[3]
                                           if len(s_gel_prev2) > 3 else
                                           ec_prev[1]) + s_wb]
                for hd in range(4):
                    for rg in range(8):
                        for mq in range(4):
                            m = rg * 4 + mq
                            q = m // 2
                            rbase = (q % 2) * 64
                            zpar = (hd * 32 + m) % 4

                            def mmz(e, m=m, hd=hd, rg=rg, rbase=rbase,
                                    zpar=zpar):
                                slot = 2 * (hd * 8 + m // 4) + (m % 2)
                                off = slot * 128
                                last = None
                                for h2 in range(2):
                                    last = e.matmul(
                                        ps_zs2[zpar][:, h2 * W:(h2 + 1) * W],
                                        rub[rbase:rbase + 64,
                                            off:off + 128],
                                        xnb[rbase:rbase + 64,
                                            rg * GW + h2 * W:
                                            rg * GW + (h2 + 1) * W],
                                        start=True, stop=True)
                                return last
                            s_mz = C.st("pe", s_ln_all + s_gel_prev2[zpar],
                                        mmz)
                            last_maker = s_mz
                            gelq.append((m, zpar, s_mz))
                            if len(gelq) == 3 or m == 31:
                                batch = list(gelq)
                                gelq.clear()

                                def gtri(e, batch=batch):
                                    last = None
                                    for (gm, gz, _) in batch:
                                        last = e.activation(
                                            hh[:, gm * GW:(gm + 1) * GW],
                                            ps_zs2[gz], AF.Gelu)
                                    return last
                                gdep = []
                                for (_, _, d) in batch:
                                    gdep += d
                                s_gel = C.st("ac", gdep, gtri)
                                s_gel_prev = s_gel
                                for (_, gz, _) in batch:
                                    s_gel_prev2[gz] = (s_gel_prev2[gz]
                                                       + s_gel)
                    gel_last = s_gel_prev
                    for et in range(NT):
                        par = et % 2
                        s_w = C.dma("sy", wc_prev[par], [
                            (wcf[par][:, :].rearrange(
                                "p (m e) -> p m e", e=128),
                             wcb[hd * 4096:(hd + 1) * 4096,
                                 et * 128:(et + 1) * 128].rearrange(
                                     "(m p) e -> p m e", p=128))])

                        def mmc(e, par=par):
                            last = None
                            for m in range(32):
                                for h2 in range(2):
                                    last = e.matmul(
                                        ps_cs[par][:, h2 * W:(h2 + 1) * W],
                                        wcf[par][:, m * 128:(m + 1) * 128],
                                        hh[:, m * GW + h2 * W:
                                           m * GW + (h2 + 1) * W],
                                        start=(m == 0), stop=(m == 31))
                            return last
                        dep_mc = gel_last + s_w + ec_prev[par]
                        if par == 0:
                            dep_mc = dep_mc + s_gel_prev2[2]
                        else:
                            dep_mc = dep_mc + s_gel_prev2[3]
                        s_mc = C.st("pe", dep_mc, mmc)
                        wc_prev[par] = s_mc
                        if hd == 0:
                            s_ec = C.st("dv", s_mc, lambda e, et=et, par=par:
                                        e.tensor_copy(
                                            oac[:, et * GW:(et + 1) * GW],
                                            ps_cs[par]))
                        else:
                            s_ec = C.st("dv", s_mc, lambda e, et=et, par=par:
                                        e.tensor_add(
                                            oac[:, et * GW:(et + 1) * GW],
                                            oac[:, et * GW:(et + 1) * GW],
                                            ps_cs[par]))
                        ec_prev[par] = s_ec
                    s_gel_prev2[2] = s_gel_prev2[2] + ec_prev[0]
                    s_gel_prev2[3] = s_gel_prev2[3] + ec_prev[1]
                s_tr_prev = [[], []]
                for et in range(NT):
                    s_xe = C.dma("sy", ec_prev[0] + ec_prev[1] + ffn_tail, [
                        (rsx[:, :],
                         xf_sc[et * 128:(et + 1) * 128,
                               g * GW:(g + 1) * GW])])
                    s_r = C.st("dv", s_xe + ec_prev[et % 2], lambda e, et=et:
                               e.tensor_add(oac[:, et * GW:(et + 1) * GW],
                                            oac[:, et * GW:(et + 1) * GW],
                                            rsx[:, :]))
                    if y is None:
                        s_so = C.dma("sy", s_r, [
                            (xf_sc[et * 128:(et + 1) * 128,
                                   g * GW:(g + 1) * GW],
                             oac[:, et * GW:(et + 1) * GW])])
                        if dbg_dst is not None:
                            s_so = s_so + C.dma("sy", s_r, [
                                (dbg_dst[et * 128:(et + 1) * 128,
                                         g * GW:(g + 1) * GW],
                                 oac[:, et * GW:(et + 1) * GW])])
                        ec_prev[et % 2] = s_r + s_so
                        ffn_tail = s_r + s_so
                    else:
                        ec_prev[et % 2] = s_r
                if y is not None:
                    res_done = ec_prev[0] + ec_prev[1]
                    s_yst = []
                    for sv in range(8):
                        s_cp_all = []
                        for et in range(NT):
                            par = et % 2
                            tsl2 = slice(par * 512, par * 512 + 128)

                            def tr(e, et=et, sv=sv, tsl2=tsl2):
                                return e.transpose(
                                    ps_tr[:, tsl2],
                                    oac[:, et * GW + sv * 128:
                                        et * GW + (sv + 1) * 128],
                                    identf[:, :])
                            s_t = C.st("pe", res_done + s_tr_prev[par], tr)
                            s_cp = C.st("dv", s_t + s_yst,
                                        lambda e, et=et, tsl2=tsl2:
                                        e.tensor_copy(
                                            ytok[:, et * 128:(et + 1) * 128],
                                            ps_tr[:, tsl2]))
                            s_tr_prev[par] = s_cp
                            s_cp_all += s_cp
                        s_yst = C.dma("sy", s_cp_all, [
                            (y[g * GW + sv * 128:g * GW + (sv + 1) * 128, :],
                             ytok[:, :])])
                    ffn_tail = s_yst + s_tr_prev[0] + s_tr_prev[1]
        with nc.Block(no_gpsimd_drain=True) as blk:
            C.block = blk
            C.st("sy", C.snap("sd") + C.snap("ad"), lambda e: None)


def build_program(dbg=(), phases=("p1", "p2", "p3", "p4", "p5")):
    nc = bass.Bass("TRN2", target_bir_lowering=False)
    C = Ctx(nc)

    xf = nc.dram_tensor("xf", [D, T], F32, kind="ExternalInput").ap()
    wi = nc.dram_tensor("wi", [D, D], F32, kind="ExternalInput").ap()
    wf_ = nc.dram_tensor("wf", [D, D], F32, kind="ExternalInput").ap()
    wg = nc.dram_tensor("wg", [D, D], F32, kind="ExternalInput").ap()
    wo = nc.dram_tensor("wo", [D, D], F32, kind="ExternalInput").ap()
    cvi = nc.dram_tensor("cvi", [128, NT * 4], F32,
                         kind="ExternalInput").ap()
    cvf = nc.dram_tensor("cvf", [128, NT * 4], F32,
                         kind="ExternalInput").ap()
    gnw = nc.dram_tensor("gnw", [128, NT], F32, kind="ExternalInput").ap()
    lng = nc.dram_tensor("lng", [128, NT], F32, kind="ExternalInput").ap()
    lnb = nc.dram_tensor("lnb", [128, NT], F32, kind="ExternalInput").ap()
    winT = nc.dram_tensor("winT", [D, 2 * DI], F32,
                          kind="ExternalInput").ap()
    cw = nc.dram_tensor("cw", [128, NTI * 4], F32,
                        kind="ExternalInput").ap()
    cb = nc.dram_tensor("cb", [128, NTI], F32, kind="ExternalInput").ap()
    xpT = nc.dram_tensor("xpT", [DI, 96], F32, kind="ExternalInput").ap()
    dtpT = nc.dram_tensor("dtpT", [64, DI], F32, kind="ExternalInput").ap()
    dtb = nc.dram_tensor("dtb", [128, NTI], F32, kind="ExternalInput").ap()
    alog = nc.dram_tensor("alog", [128, NTI * NS], F32,
                          kind="ExternalInput").ap()
    dd = nc.dram_tensor("dd", [128, NTI], F32, kind="ExternalInput").ap()
    woutT = nc.dram_tensor("woutT", [DI, D], F32, kind="ExternalInput").ap()
    mk = nc.dram_tensor("mk", [4, 64, 16, 16], F32,
                        kind="ExternalInput").ap()
    mkT = nc.dram_tensor("mkT", [4, 64, 16, 16], F32,
                         kind="ExternalInput").ap()
    mtfp = nc.dram_tensor("mtfp", [128, 64 * 128], F32,
                          kind="ExternalInput").ap()
    rufp = nc.dram_tensor("rufp", [128, 64 * 128], F32,
                          kind="ExternalInput").ap()
    ruT = nc.dram_tensor("ruT", [4, 64, 16, 64], F32,
                         kind="ExternalInput").ap()
    wcT = nc.dram_tensor("wcT", [16 * D, D], F32, kind="ExternalInput").ap()

    y = nc.dram_tensor("y", [T, D], F32, kind="ExternalOutput").ap()
    dbg_t = {}
    for name in dbg:
        shp = {"x1": [D, T], "x2": [D, T], "x3": [D, T], "xl": [D, T],
               "u": [DI, T], "z": [DI, T], "dtv": [DI, T],
               "xdbl": [128, T]}[name]
        dbg_t[name] = nc.dram_tensor("dbg_" + name, shp, F32,
                                     kind="ExternalOutput").ap()

    xf_sc = nc.dram_tensor("xf_sc", [D, T], F32, kind="Internal").ap()
    xl_sc = nc.dram_tensor("xl_sc", [D, T], F32, kind="Internal").ap()
    u_sc = nc.dram_tensor("u_sc", [DI, T], F32, kind="Internal").ap()
    z_sc = nc.dram_tensor("z_sc", [DI, T], F32, kind="Internal").ap()
    wcb = nc.dram_tensor("wcb", [16 * D, D], BF16, kind="Internal").ap()
    wob_sc = nc.dram_tensor("wob_sc", [DI, D], BF16, kind="Internal").ap()
    zs_sc = nc.dram_tensor("zs_sc", [DI, W], F32, kind="Internal").ap()
    xdbl_sc = nc.dram_tensor("xdbl_sc", [96, T], F32, kind="Internal").ap()

    with contextlib.ExitStack() as ST:
        C.init_sems(ST)
        sb = lambda *a: ST.enter_context(nc.sbuf_tensor(*a))

        ones1 = sb("ones1", [128, 1], F32)
        onesr = sb("onesr", [1, 128], F32)
        cst = sb("cst", [128, 2], F32)
        C.ones1, C.onesr = ones1, onesr
        C.eps1 = cst[0:1, 0:1]
        C.mhalf1 = cst[0:1, 1:2]
        identf = sb("identf", [128, 128], F32)
        cvi_sb = sb("cvi_sb", [128, NT * 4], F32)
        cvf_sb = sb("cvf_sb", [128, NT * 4], F32)
        gnw_sb = sb("gnw_sb", [128, NT], F32)
        lng_sb = sb("lng_sb", [128, NT], F32)
        lnb_sb = sb("lnb_sb", [128, NT], F32)
        cw_sb = sb("cw_sb", [128, NTI * 4], F32)
        cb_sb = sb("cb_sb", [128, NTI], F32)
        dtb_sb = sb("dtb_sb", [128, NTI], F32)
        dd_sb = sb("dd_sb", [128, NTI], F32)
        a_sb = sb("a_sb", [128, NTI * NS], F32)
        hst_g = sb("hst_g", [128, NT], F32)
        hst_m = sb("hst_m", [128, NTI * NS], F32)
        itail = sb("itail", [128, NT * 3], F32)
        ftail = sb("ftail", [128, NT * 3], F32)
        utail = sb("utail", [128, NTI * 3], F32)
        xsq = sb("xsq", [128, NT * W], F32)

        with nc.Block(no_gpsimd_drain=True) as blk:
            C.block = blk
            ld = C.dma("sy", [], [
                (cvi_sb[:, :], cvi), (cvf_sb[:, :], cvf),
                (gnw_sb[:, :], gnw), (lng_sb[:, :], lng),
                (lnb_sb[:, :], lnb), (cw_sb[:, :], cw),
                (cb_sb[:, :], cb), (dtb_sb[:, :], dtb),
                (dd_sb[:, :], dd), (a_sb[:, :], alog),
            ])
            s_a1 = C.st("ac", ld, lambda e: e.activation(a_sb[:, :],
                                                         a_sb[:, :], AF.Exp))
            C.st("dv", s_a1, lambda e: e.tensor_scalar_mul(
                a_sb[:, :], a_sb[:, :], -1.0))

            def zeros(e):
                e.memset(hst_g[:, :], 0.0)
                e.memset(hst_m[:, :], 0.0)
                e.memset(itail[:, :], 0.0)
                e.memset(ftail[:, :], 0.0)
                e.memset(utail[:, :], 0.0)
                e.memset(ones1[:, :], 1.0)
                e.memset(cst[:, 0:1], LN_EPS)
                e.memset(cst[:, 1:2], -0.5)
                return e.memset(onesr[:, :], 1.0)
            C.st("dv", [], zeros)

            def gpinit(gpsimd):
                sem = C.sems["gp"][0]
                cnt = C.sems["gp"][1]
                gpsimd.memset(identf[:, :], 0.0).then_inc(sem, 1)
                cnt += 1
                gpsimd.wait_ge(sem, cnt)
                gpsimd.affine_select(
                    out=identf[:, :], in_=identf[:, :],
                    compare_op=ALU.not_equal, fill=1.0, base=0,
                    pattern=[[-1, 128]], channel_multiplier=1).then_inc(
                        sem, 1)
                cnt += 1
                C.sems["gp"][1] = cnt
            blk.gpsimd(gpinit)

        need_wcb = ("p2" in phases) or ("p5" in phases) or ("p4" in phases)
        wcb_done = []
        if need_wcb:
            with nc.Block(no_gpsimd_drain=True) as blk:
                C.block = blk
                for r in range(16):
                    wcb_done += C.dma("gp", [], [
                        (wcb[r * 1024:(r + 1) * 1024, :],
                         wcT[r * 1024:(r + 1) * 1024, :])])
                wcb_done += C.dma("gp", [], [(wob_sc, woutT)])

        if "p1" in phases:
            with contextlib.ExitStack() as PS:
                psb = lambda *a: PS.enter_context(nc.sbuf_tensor(*a))
                ppt = lambda *a: PS.enter_context(nc.psum_tensor(*a))
                wib = psb("wib", [128, NT * D], BF16)
                wfb = psb("wfb", [128, NT * D], BF16)
                wgb = psb("wgb", [128, NT * D], BF16)
                wob = psb("wob", [128, NT * D], BF16)
                rows = psb("rows", [1, 4 * W], F32)
                tls = psb("tls", [128, W], F32)
                xg = psb("xg", [128, NT * W], F32)
                xnb = psb("xnb", [128, NT * W], BF16)
                iraw = psb("iraw", [128, NT * W], F32)
                fraw = psb("fraw", [128, NT * W], F32)
                icv = psb("icv", [128, NT * W], F32)
                fcv = psb("fcv", [128, NT * W], F32)
                ob = psb("ob", [128, NT * W], BF16)
                big = [ppt(f"p1ps{i}", [128, 1024], F32).ap()
                       for i in range(2)]
                st_sum = big[0][0:1, 0:W]
                st_sq = big[0][0:1, W:2 * W]
                rep_a = big[1][:, 0:W]
                rep_b = big[1][:, W:2 * W]
                mm_ps = [ppt("p1mm", [128, 1024], F32).ap(),
                         ppt("p1mm2", [128, 1024], F32).ap()]
                mm = [mm_ps[0][:, 0:W], mm_ps[0][:, W:2 * W],
                      mm_ps[1][:, 0:W], mm_ps[1][:, W:2 * W]]

                with nc.Block(no_gpsimd_drain=True) as blk:
                    C.block = blk
                    C.dma("gp", [], [
                        (wib[:, :].rearrange("p (k e) -> p k e", e=D),
                         _r3(wi)),
                        (wfb[:, :].rearrange("p (k e) -> p k e", e=D),
                         _r3(wf_)),
                        (wgb[:, :].rearrange("p (k e) -> p k e", e=D),
                         _r3(wg)),
                        (wob[:, :].rearrange("p (k e) -> p k e", e=D),
                         _r3(wo)),
                    ])
                wdep = C.snap("gd")

                p1tail = []
                prev_ev = [[], [], [], []]
                prev_o = [[], [], [], []]
                with nc.Block(no_gpsimd_drain=True) as blk:
                    C.block = blk
                    for tc in range(NCH):
                        tsl = slice(tc * W, (tc + 1) * W)
                        ksl = (lambda buf:
                               (lambda k: buf[:, k * W:(k + 1) * W]))
                        s_x = C.dma("sy", p1tail, [
                            (xg[:, :].rearrange("p (k t) -> p k t", t=W),
                             _r3(xf)[:, :, tsl])])
                        s_ln, _ = emit_ln(
                            C, ksl(xg), xsq, st_sum, st_sq, rep_a, rep_b,
                            rows, tls, s_x + p1tail, dst_bf_sl=ksl(xnb))
                        evs = []
                        gi = 0
                        for wsb, dst in ((wib, iraw), (wfb, fraw)):
                            for m in range(NT):
                                par = gi % 4

                                def mmg(e, wsb=wsb, m=m, par=par):
                                    last = None
                                    for k in range(NT):
                                        last = e.matmul(
                                            mm[par],
                                            wsb[:, k * D + m * 128:
                                                k * D + (m + 1) * 128],
                                            xnb[:, k * W:(k + 1) * W],
                                            start=(k == 0),
                                            stop=(k == NT - 1))
                                    return last
                                s_mm = C.st("pe",
                                            s_ln + wdep + prev_ev[par], mmg)
                                s_ev = C.st("dv", s_mm,
                                            lambda e, dst=dst, m=m, par=par:
                                            e.tensor_copy(
                                                dst[:, m * W:(m + 1) * W],
                                                mm[par]))
                                prev_ev[par] = s_ev
                                evs += s_ev
                                gi += 1
                        s_ic, s_it = emit_conv(
                            C, ksl(iraw), ksl(icv), itail,
                            lambda k, j: cvi_sb[:, k * 4 + j:k * 4 + j + 1],
                            evs, NT)
                        s_fc, s_ft = emit_conv(
                            C, ksl(fraw), ksl(fcv), ftail,
                            lambda k, j: cvf_sb[:, k * 4 + j:k * 4 + j + 1],
                            evs, NT)
                        s_fg = C.st("ac", s_fc + s_ft, lambda e:
                                    e.activation(fraw[:, :], fcv[:, :],
                                                 AF.Sigmoid))
                        s_si = C.st("ac", s_ic + s_it + s_fg, lambda e:
                                    e.activation(iraw[:, :], icv[:, :],
                                                 AF.Silu))
                        s_g1 = C.st("dv", s_fg + s_si, lambda e:
                                    e.tensor_mul(icv[:, :], iraw[:, :],
                                                 fraw[:, :]))
                        s_g2 = C.st("dv", s_g1, lambda e:
                                    e.tensor_sub(icv[:, :], iraw[:, :],
                                                 icv[:, :]))

                        def scans(e):
                            last = None
                            for k in range(NT):
                                sl = slice(k * W, (k + 1) * W)
                                last = e.tensor_tensor_scan(
                                    fcv[:, sl], fraw[:, sl], icv[:, sl],
                                    hst_g[:, k:k + 1], ALU.mult, ALU.add)
                            return last
                        s_sc = C.st("dv", s_g2, scans)

                        def savest(e):
                            last = None
                            for k in range(NT):
                                last = e.tensor_copy(
                                    hst_g[:, k:k + 1],
                                    fcv[:, k * W + W - 1:k * W + W])
                            return last
                        s_hs = C.st("dv", s_sc, savest)
                        prev_g4 = [s_sc, s_sc, s_sc, s_sc]
                        for m in range(NT):
                            par = m % 4

                            def mmg2(e, m=m, par=par):
                                last = None
                                for k in range(NT):
                                    last = e.matmul(
                                        mm[par],
                                        wgb[:, k * D + m * 128:
                                            k * D + (m + 1) * 128],
                                        xnb[:, k * W:(k + 1) * W],
                                        start=(k == 0), stop=(k == NT - 1))
                                return last
                            s_mm = C.st("pe", prev_g4[par], mmg2)
                            s_ev = C.st("dv", s_mm, lambda e, m=m, par=par:
                                        e.tensor_copy(
                                            icv[:, m * W:(m + 1) * W],
                                            mm[par]))
                            prev_g4[par] = s_ev
                        prev_g = prev_g4[0] + prev_g4[1] + prev_g4[2] \
                            + prev_g4[3]
                        s_sg = C.st("ac", prev_g, lambda e: e.activation(
                            fraw[:, :], icv[:, :], AF.Silu))
                        s_rms, _ = emit_ln(
                            C, ksl(fcv), xsq, st_sum, st_sq, rep_a, rep_b,
                            rows, tls, s_sc + s_hs, rms=True,
                            dst_f32_sl=ksl(iraw))

                        def gate(e):
                            last = None
                            for k in range(NT):
                                sl = slice(k * W, (k + 1) * W)
                                last = e.scalar_tensor_tensor(
                                    ob[:, sl], iraw[:, sl],
                                    gnw_sb[:, k:k + 1], fraw[:, sl],
                                    ALU.mult, ALU.mult)
                            return last
                        s_ob = C.st("dv", s_rms + s_sg, gate)
                        prev_o = [x + s_ob for x in prev_o]
                        s_all = []
                        for m in range(NT):
                            par = m % 4

                            def mmo(e, m=m, par=par):
                                last = None
                                for k in range(NT):
                                    last = e.matmul(
                                        mm[par],
                                        wob[:, k * D + m * 128:
                                            k * D + (m + 1) * 128],
                                        ob[:, k * W:(k + 1) * W],
                                        start=(k == 0), stop=(k == NT - 1))
                                return last
                            s_mm = C.st("pe", s_ob + prev_o[par], mmo)
                            s_e = C.st("dv", s_mm, lambda e, m=m, par=par:
                                       e.tensor_add(
                                           xg[:, m * W:(m + 1) * W],
                                           xg[:, m * W:(m + 1) * W],
                                           mm[par]))
                            prev_o[par] = s_e
                            s_all += s_e
                        p1tail = C.dma("sy", s_all, [
                            (_r3(xf_sc)[:, :, tsl],
                             xg[:, :].rearrange("p (k t) -> p k t",
                                                t=W))]) + s_all
                        prev_ev = list(prev_o)
                with nc.Block(no_gpsimd_drain=True) as blk:
                    C.block = blk
                    C.st("sy", C.snap("sd"), lambda e: None)

            if "x1" in dbg_t:
                _dump(C, nc, xf_sc, dbg_t["x1"], D, "dx1")

        if "p2" in phases:
            _emit_ffn(C, nc, ST, xsq, mtfp, rufp, wcb, xf_sc, None,
                      identf, dbg_dst=dbg_t.get("x2"), tag="a",
                      wdep0=wcb_done)

        if "p3" in phases:
            _emit_mamba_a(C, nc, xsq, xf_sc, xl_sc, u_sc, z_sc,
                          winT, xpT, cw_sb, cb_sb, lng_sb, lnb_sb, utail,
                          xdbl_sc)
            if "xl" in dbg_t:
                _dump(C, nc, xl_sc, dbg_t["xl"], D, "dxl")
            if "u" in dbg_t:
                _dump(C, nc, u_sc, dbg_t["u"], DI, "du")
            if "z" in dbg_t:
                _dump(C, nc, z_sc, dbg_t["z"], DI, "dz")

        if "p4" in phases:
            _emit_mamba_b(C, nc, xsq, xf_sc, xl_sc, u_sc, z_sc, wob_sc,
                          dtpT, xdbl_sc, a_sb, dtb_sb, dd_sb, hst_m, dbg_t,
                          zs_sc, wdep0=wcb_done)
            if "x3" in dbg_t:
                _dump(C, nc, xf_sc, dbg_t["x3"], D, "dx3")

        if "p5" in phases:
            _emit_ffn(C, nc, ST, xsq, mtfp, rufp, wcb, xf_sc, y,
                      identf, tag="b")

        with nc.Block(no_gpsimd_drain=True) as blk:
            C.block = blk
            C.st("sy", C.snap("sd") + C.snap("gd") + C.snap("ad"),
                 lambda e: None)

    return nc


def _emit_mamba_a(C, nc, xsq, xf_sc, xl_sc, u_sc, z_sc, winT,
                  xpT, cw_sb, cb_sb, lng_sb, lnb_sb, utail, xdbl_sc):
    with contextlib.ExitStack() as PS:
        sb = lambda *a: PS.enter_context(nc.sbuf_tensor(*a))
        pst = lambda *a: PS.enter_context(nc.psum_tensor(*a))
        rows = sb("rows_m", [1, 4 * W], F32)
        tls = sb("tls_m", [128, W], F32)
        winb = sb("winb", [128, NT * 2 * DI], BF16)
        xpb = sb("xpb", [128, NTI * 96], BF16)
        xg = sb("xg3", [128, NT * W], F32)
        xl_t = sb("xl3", [128, NT * W], F32)
        xnb = sb("xnb3", [128, NT * W], BF16)
        ub = sb("ub3", [128, NTI * W], BF16)
        xinr = sb("xinr", [128, W], F32)
        cacc = sb("cacc", [128, W], F32)
        uu = sb("uu", [128, W], F32)
        zz_t = sb("zz_t", [128, W], F32)
        xdb_s = sb("xdb_s", [96, W], F32)
        xdb_c = sb("xdb_c", [16, W], F32)
        big = [pst(f"p3ps{i}", [128, 1024], F32).ap() for i in range(3)]
        st_sum = big[0][0:1, 0:W]
        st_sq = big[0][0:1, W:2 * W]
        rep_a = big[1][:, 0:W]
        rep_b = big[1][:, W:2 * W]
        mm = [big[2][:, 0:W], big[2][:, W:2 * W],
              big[1][:, 0:W], big[1][:, W:2 * W]]
        ps_x = pst("p3psx", [128, 1024], F32).ap()
        ps_dt = ps_x[0:64, 0:W]
        ps_bb = ps_x[0:16, W:2 * W]
        ps_cc = ps_x[32:48, W:2 * W]

        with nc.Block(no_gpsimd_drain=True) as blk:
            C.block = blk
            C.dma("gp", [], [
                (winb[:, :].rearrange("p (k e) -> p k e", e=2 * DI),
                 _r3(winT)),
                (xpb[:, :].rearrange("p (k e) -> p k e", e=96),
                 xpT.rearrange("(k p) e -> p k e", p=128)),
            ])
        wdep = C.snap("gd")

        p3tail = []
        prev_ev = [[], [], [], []]
        prev_ust = []
        prev_zst = []
        prev_xst = []
        prev_xlst = []
        with nc.Block(no_gpsimd_drain=True) as blk:
            C.block = blk
            for tc in range(NCH):
                tsl = slice(tc * W, (tc + 1) * W)
                s_x = C.dma("sy", p3tail, [
                    (xg[:, :].rearrange("p (k t) -> p k t", t=W),
                     _r3(xf_sc)[:, :, tsl])])
                s_ln, _ = emit_ln(
                    C, lambda k: xg[:, k * W:(k + 1) * W], xsq, st_sum,
                    st_sq, rep_a, rep_b, rows, tls,
                    s_x + p3tail + prev_xlst,
                    dst_bf_sl=lambda k: xnb[:, k * W:(k + 1) * W],
                    dst_f32_sl=lambda k: xl_t[:, k * W:(k + 1) * W],
                    gb=(lambda k: lng_sb[:, k:k + 1],
                        lambda k: lnb_sb[:, k:k + 1]))
                s_xs = C.st("dv", s_ln, lambda e:
                            e.tensor_add(xl_t[:, :], xl_t[:, :],
                                         xg[:, :]))
                prev_xlst = C.dma("sy", s_xs, [
                    (_r3(xl_sc)[:, :, tsl],
                     xl_t[:, :].rearrange("p (k t) -> p k t", t=W))])
                u_done = []
                for m in range(2 * NTI):
                    par = m % 4

                    def mmi(e, m=m, par=par):
                        last = None
                        for k in range(NT):
                            last = e.matmul(
                                mm[par],
                                winb[:, k * 2 * DI + m * 128:
                                     k * 2 * DI + (m + 1) * 128],
                                xnb[:, k * W:(k + 1) * W],
                                start=(k == 0), stop=(k == NT - 1))
                        return last
                    s_mm = C.st("pe", s_ln + wdep + prev_ev[par], mmi)
                    if m < NTI:
                        s_e = C.st("dv", s_mm, lambda e, par=par:
                                   e.tensor_copy(xinr[:, :], mm[par]))
                        s_cv, s_ut = emit_conv(
                            C, lambda k: xinr[:, :], lambda k: cacc[:, :],
                            utail[:, m * 3:(m + 1) * 3],
                            lambda kk, j, m=m: cw_sb[:, m * 4 + j:
                                                     m * 4 + j + 1],
                            s_e, 1)
                        s_u = C.st("ac", s_cv + s_ut + prev_ust,
                                   lambda e, m=m:
                                   e.activation(uu[:, :], cacc[:, :],
                                                AF.Silu,
                                                bias=cb_sb[:, m:m + 1]))
                        s_ub = C.st("dv", s_u, lambda e, m=m:
                                    e.tensor_copy(ub[:, m * W:(m + 1) * W],
                                                  uu[:, :]))
                        s_st = C.dma("sy", s_u, [
                            (u_sc[m * 128:(m + 1) * 128, tsl], uu[:, :])])
                        prev_ust = s_st
                        u_done += s_ub
                        prev_ev[par] = s_e + s_st + s_ub
                    else:
                        s_e = C.st("dv", s_mm + prev_zst, lambda e, par=par:
                                   e.tensor_copy(zz_t[:, :], mm[par]))
                        s_st = C.dma("sy", s_e, [
                            (z_sc[(m - NTI) * 128:(m - NTI + 1) * 128, tsl],
                             zz_t[:, :])])
                        prev_zst = s_st
                        prev_ev[par] = s_e + s_st

                def mk_mm(dst, lo, hi):
                    def f(e):
                        last = None
                        for k in range(NTI):
                            last = e.matmul(
                                dst, xpb[:, k * 96 + lo:k * 96 + hi],
                                ub[:, k * W:(k + 1) * W],
                                start=(k == 0), stop=(k == NTI - 1))
                        return last
                    return f
                s_m1 = C.st("pe", u_done + prev_ev[0] + prev_ev[1],
                            mk_mm(ps_dt, 0, 64))
                s_m2 = C.st("pe", [], mk_mm(ps_bb, 64, 80))
                s_m3 = C.st("pe", [], mk_mm(ps_cc, 80, 96))

                def evx(e):
                    e.tensor_copy(xdb_s[0:64, :], ps_dt)
                    e.tensor_copy(xdb_s[64:80, :], ps_bb)
                    return e.tensor_copy(xdb_c[:, :], ps_cc)
                s_ev = C.st("dv", s_m1 + s_m2 + s_m3 + prev_xst, evx)
                prev_xst = C.dma("sy", s_ev, [
                    (xdbl_sc[0:80, tsl], xdb_s[0:80, :]),
                    (xdbl_sc[80:96, tsl], xdb_c[:, :])])
                p3tail = prev_xst + s_ev
        with nc.Block(no_gpsimd_drain=True) as blk:
            C.block = blk
            C.st("sy", C.snap("sd"), lambda e: None)


def _emit_mamba_b(C, nc, xsq, xf_sc, xl_sc, u_sc, z_sc, wob_sc, dtpT,
                  xdbl_sc, a_sb, dtb_sb, dd_sb, hst_m, dbg_t, zs_sc,
                  wdep0=()):
    with contextlib.ExitStack() as PS:
        sb = lambda *a: PS.enter_context(nc.sbuf_tensor(*a))
        pst = lambda *a: PS.enter_context(nc.psum_tensor(*a))
        dtpb = sb("dtpb", [64, DI], BF16)
        dtrb = sb("dtrb", [64, W], BF16)
        brep = sb("brep", [128, NS * W], F32)
        crep = sb("crep", [128, NS * W], F32)
        bbuf = [sb("bb0", [128, 4 * W], F32), sb("bb1", [128, 4 * W], F32),
                sb("bb2", [128, 4 * W], F32)]
        ab2 = sb("ab2", [128, 4 * W], F32)
        hcb = sb("hcb0", [128, 4 * W], F32)
        dt2 = [sb("dt20", [128, W], F32), sb("dt21", [128, W], F32)]
        sp1 = sb("sp1", [128, W], F32)
        yb = [sb("yb4a", [128, NTI * W], BF16),
              sb("yb4b", [128, NTI * W], BF16)]
        u_t = [sb("u_t0", [128, W], F32), sb("u_t1", [128, W], F32)]
        z_t = [sb("z_t0", [128, W], F32), sb("z_t1", [128, W], F32)]
        dtu = sb("dtu", [128, W], F32)
        yt2 = sb("yt2", [128, W], F32)
        fixt = sb("fixt", [128, 8], F32)
        wou = [sb("wou0", [128, NTI * 128], BF16),
               sb("wou1", [128, NTI * 128], BF16)]
        xg_t = sb("xg4", [128, W], F32)
        xl_t = sb("xl4", [128, W], F32)
        spsc = sb("spsc", [128, 2 * W], F32)
        big = pst("p4ps", [128, 1024], F32).ap()
        ps_r = [big[:, 0:W], big[:, W:2 * W]]
        pso_t = [pst("p4pso0", [128, 1024], F32).ap(),
                 pst("p4pso1", [128, 1024], F32).ap()]
        ps_o = [pso_t[0][:, 0:W], pso_t[0][:, W:2 * W],
                pso_t[1][:, 0:W], pso_t[1][:, W:2 * W]]
        abuf = [xsq[:, 0:4 * W], xsq[:, 4 * W:8 * W], ab2[:, :]]

        with nc.Block(no_gpsimd_drain=True) as blk:
            C.block = blk
            C.dma("gp", [], [(dtpb[:, :], dtpT)])
        wdep = C.snap("gd")

        scan_done = [[], [], []]
        gp_done = [[], [], []]
        uld = [[], []]
        zld = [[], []]
        hs_done = [[], [], []]
        ybfree2 = [[], []]
        pend_out = [None]
        brep_rd = [[], [], [], []]
        crep_rd = [[], [], [], []]
        prev_dbg = []
        prev_dtmm = [[], []]
        s_wprev = [list(wdep0), list(wdep0)]
        prev_out = [[], [], [], []]
        ps_free = [[], []]
        ps_free4 = [[], [], [], []]
        ybd = [[]]


        with nc.Block(no_gpsimd_drain=True) as blk:
            C.block = blk
            xgp = [xg_t, spsc[:, 0:W], xl_t, spsc[:, W:2 * W]]

            def _emit_out_m(po, m):
                ptc, pybd = po
                ptsl = slice(ptc * W, (ptc + 1) * W)
                ybuf = yb[ptc % 2]
                par = m % 2
                p4r = m % 4
                s_w = C.dma("sy", s_wprev[par], [
                    (wou[par][:, :].rearrange("p (k e) -> p k e", e=128),
                     _r3(wob_sc)[:, :, m * 128:(m + 1) * 128])])

                def mmo(e, m=m, par=par, p4r=p4r, ybuf=ybuf):
                    last = None
                    for kk in range(NTI):
                        last = e.matmul(
                            ps_o[p4r],
                            wou[par][:, kk * 128:(kk + 1) * 128],
                            ybuf[:, kk * W:(kk + 1) * W],
                            start=(kk == 0), stop=(kk == NTI - 1))
                    return last
                s_mm = C.st("pe", pybd + s_w + ps_free4[p4r], mmo)
                s_wprev[par] = s_mm
                ybfree2[ptc % 2] = ybfree2[ptc % 2] + s_mm
                s_x = C.dma("sy", prev_out[p4r], [
                    (xgp[p4r][:, :],
                     xl_sc[m * 128:(m + 1) * 128, ptsl])])
                s_r2 = C.st("dv", s_mm + s_x, lambda e, p4r=p4r:
                            e.tensor_add(xgp[p4r][:, :], xgp[p4r][:, :],
                                         ps_o[p4r]))
                s_st = C.dma("ac", s_r2, [
                    (xf_sc[m * 128:(m + 1) * 128, ptsl],
                     xgp[p4r][:, :])])
                prev_out[p4r] = s_st + s_r2
                ps_free4[p4r] = s_r2

            for tc in range(NCH):
                tsl = slice(tc * W, (tc + 1) * W)
                ydep_cur = ybfree2[tc % 2]
                ybfree2[tc % 2] = []
                s_db = C.dma("gp", prev_dbg + prev_dtmm[0] + prev_dtmm[1],
                             [(dtrb[:, :], xdbl_sc[0:64, tsl])])
                def _bc(r0, n0, n1):
                    return xdbl_sc[r0 + n0:r0 + n1, tsl].rearrange(
                        "n (o t) -> o n t", o=1).broadcast_to(
                        [128, n1 - n0, W])
                s_repb = []
                s_repc = []
                for qq in range(4):
                    nsl4 = slice(qq * 4 * W, (qq + 1) * 4 * W)
                    s_repb.append(C.dma("sy", brep_rd[qq], [
                        (brep[:, nsl4].rearrange("p (n t) -> p n t", n=4),
                         _bc(64, qq * 4, qq * 4 + 4))]))
                for qq in range(4):
                    nsl4 = slice(qq * 4 * W, (qq + 1) * 4 * W)
                    s_repc.append(C.dma("sy", crep_rd[qq], [
                        (crep[:, nsl4].rearrange("p (n t) -> p n t", n=4),
                         _bc(80, qq * 4, qq * 4 + 4))]))

                def dt_build(k):
                    p2 = k % 2
                    par = k % 2

                    def mmdt(e, k=k, par=par):
                        return e.matmul(
                            ps_r[par], dtpb[:, k * 128:(k + 1) * 128],
                            dtrb[:, :], start=True, stop=True)
                    s_mm = C.st("pe", s_db + wdep + ps_free[par]
                                + prev_dtmm[par], mmdt)
                    prev_dtmm[par] = s_mm

                    def evd(e, k=k, par=par, p2=p2):
                        return e.tensor_scalar_add(
                            dt2[p2][:, :], ps_r[par], dtb_sb[:, k:k + 1])
                    s_ev = C.st("dv", s_mm + uld[p2], evd)
                    ps_free[par] = s_ev
                    s_ab = C.st("ac", s_ev + sp_free[0], lambda e, p2=p2:
                                e.activation(sp1[:, :], dt2[p2][:, :],
                                             AF.Abs))
                    s_e1 = C.st("ac", s_ab, lambda e: e.activation(
                        sp1[:, :], sp1[:, :], AF.Exp, scale=-1.0))
                    s_rl = C.st("ac", s_ab, lambda e, p2=p2:
                                e.activation(dt2[p2][:, :], dt2[p2][:, :],
                                             AF.Relu))
                    s_e2 = C.st("ac", s_e1, lambda e: e.activation(
                        sp1[:, :], sp1[:, :], AF.Ln, bias=1.0))
                    def tail(p2=p2, s_e2=s_e2, s_rl=s_rl):
                        s_ad = C.st("dv", s_e2 + s_rl, lambda e, p2=p2:
                                    e.tensor_add(dt2[p2][:, :],
                                                 dt2[p2][:, :],
                                                 sp1[:, :]))
                        sp_free[0] = s_ad
                        return s_ad
                    return tail

                sp_free = [[]]
                s_dt = [None, None]
                s_dt[0] = dt_build(0)()
                for k in range(NTI):
                    p2 = k % 2
                    s_u = C.dma("sy", uld[p2], [
                        (u_t[p2][:, :], u_sc[k * 128:(k + 1) * 128, tsl])])
                    s_zt = C.dma("sy", zld[p2], [
                        (z_t[p2][:, :], z_sc[k * 128:(k + 1) * 128, tsl])])
                    s_sz = C.st("ac", s_zt + zld[p2], lambda e, p2=p2:
                                e.activation(z_t[p2][:, :], z_t[p2][:, :],
                                             AF.Silu))
                    s_du = C.st("dv", s_u + s_dt[p2], lambda e, p2=p2:
                                e.tensor_mul(dtu[:, :], dt2[p2][:, :],
                                             u_t[p2][:, :]))

                    def emit_prep(q):
                        p = (4 * k + q) % 3
                        nsl = slice(q * 4 * W, (q + 1) * 4 * W)
                        hsl = slice(k * NS + q * 4, k * NS + q * 4 + 4)

                        def a_exp(e, k=k, q=q, p=p, p2=p2):
                            last = None
                            for j in range(4):
                                n = q * 4 + j
                                last = e.activation(
                                    abuf[p][:, j * W:(j + 1) * W],
                                    dt2[p2][:, :], AF.Exp,
                                    scale=a_sb[:, k * NS + n:
                                               k * NS + n + 1])
                            return last
                        s_a = C.st("ac", s_dt[p2] + scan_done[p], a_exp)

                        def bbody(e, p=p, q=q):
                            last = None
                            for nn in range(4):
                                last = e.tensor_mul(
                                    bbuf[p][:, nn * W:(nn + 1) * W],
                                    brep[:, (q * 4 + nn) * W:
                                         (q * 4 + nn + 1) * W],
                                    dtu[:, :])
                            return last
                        s_b = C.st("dv", s_repb[q] + s_du + hs_done[p],
                                   bbody)
                        brep_rd[q] = s_b
                        acols = abuf[p][:, :].rearrange(
                            "p (n t) -> p n t", n=4)[:, :, 0:1]
                        bcols = bbuf[p][:, :].rearrange(
                            "p (n t) -> p n t", n=4)[:, :, 0:1]
                        ecols = bbuf[p][:, :].rearrange(
                            "p (n t) -> p n t", n=4)[:, :, W - 1:W]
                        hcols = hst_m[:, hsl].rearrange(
                            "p (a o) -> p a o", o=1)
                        fo = 4 * (q % 2)
                        ytmp = fixt[:, fo:fo + 4].rearrange(
                            "p (a o) -> p a o", o=1)
                        s_f1 = C.st("gp", s_a + s_b, lambda e, acols=acols,
                                    hcols=hcols, ytmp=ytmp:
                                    e.tensor_mul(ytmp, acols, hcols))
                        s_f2 = C.st("gp", s_f1, lambda e, bcols=bcols,
                                    ytmp=ytmp:
                                    e.tensor_add(bcols, bcols, ytmp))
                        s_f3 = C.st("gp", s_f2, lambda e, acols=acols:
                                    e.memset(acols, 0.0))
                        return (p, nsl, s_f3, ecols, hcols)

                    prep = [emit_prep(0), emit_prep(1), emit_prep(2),
                            None]
                    dt_tail = dt_build(k + 1) if k + 1 < NTI else None
                    for q in range(4):
                        if q == 2 and dt_tail is not None:
                            s_dt[(k + 1) % 2] = dt_tail()
                        if q == 3:
                            prep[3] = emit_prep(3)
                        p, nsl, s_f3, ecols, hcols = prep[q]
                        s_scan = C.st("dv", s_f3,
                                      lambda e, p=p:
                                      e.tensor_tensor_scan(
                                          bbuf[p][:, :], abuf[p][:, :],
                                          bbuf[p][:, :], 0.0, ALU.mult,
                                          ALU.add))
                        scan_done[p] = s_scan
                        hs_done[p] = C.st("gp", s_scan,
                                          lambda e, hcols=hcols,
                                          ecols=ecols:
                                          e.tensor_copy(hcols, ecols))
                        s_hc = C.st("dv", s_repc[q],
                                    lambda e, p=p, nsl=nsl:
                                    e.tensor_mul(hcb[:, :], bbuf[p][:, :],
                                                 crep[:, nsl]))
                        gp_done[p] = s_hc
                        crep_rd[q] = s_hc
                        C.st("dv", [], lambda e:
                             e.tensor_add(hcb[:, 0:2 * W], hcb[:, 0:2 * W],
                                          hcb[:, 2 * W:4 * W]))
                        if q == 0:
                            C.st("dv", [], lambda e:
                                 e.tensor_add(yt2[:, :], hcb[:, 0:W],
                                              hcb[:, W:2 * W]))
                        else:
                            C.st("dv", [], lambda e:
                                 e.tensor_add(yt2[:, :], yt2[:, :],
                                              hcb[:, 0:W]))
                            C.st("dv", [], lambda e:
                                 e.tensor_add(yt2[:, :], yt2[:, :],
                                              hcb[:, W:2 * W]))
                    s_y3 = C.st("dv", [], lambda e, k=k, p2=p2:
                                e.scalar_tensor_tensor(
                                    yt2[:, :], u_t[p2][:, :],
                                    dd_sb[:, k:k + 1], yt2[:, :],
                                    ALU.mult, ALU.add))
                    s_y4 = C.st("dv", s_y3 + s_sz + ydep_cur,
                                lambda e, k=k, p2=p2, tc=tc:
                                e.tensor_mul(
                                    yb[tc % 2][:, k * W:(k + 1) * W],
                                    yt2[:, :], z_t[p2][:, :]))
                    uld[p2] = s_y3
                    zld[p2] = s_y4
                    ybd[0] = s_y4
                    if pend_out[0] is not None and k < NT:
                        _emit_out_m(pend_out[0], k)
                pend_out[0] = (tc, ybd[0])
                if "dtv" in dbg_t:
                    prev_dbg = []
            for m in range(NT):
                _emit_out_m(pend_out[0], m)
        with nc.Block(no_gpsimd_drain=True) as blk:
            C.block = blk
            C.st("sy", C.snap("sd"), lambda e: None)


def _dump(C, nc, src, dst, rows_n, tag):
    with contextlib.ExitStack() as ES:
        tmp = ES.enter_context(nc.sbuf_tensor(f"dump_{tag}", [128, T], F32))
        with nc.Block(no_gpsimd_drain=True) as blk:
            C.block = blk
            prev = []
            for r in range(rows_n // 128):
                s = C.dma("sy", prev, [(tmp[:, :],
                                        src[r * 128:(r + 1) * 128, :])])
                prev = C.dma("sy", s, [(dst[r * 128:(r + 1) * 128, :],
                                        tmp[:, :])])
            C.st("sy", C.snap("sd"), lambda e: None)


def _vec(v, nk):
    v = np.asarray(v)
    if v.ndim == 1:
        return np.ascontiguousarray(v.reshape(nk, 128).T)
    c = v.shape[1]
    return np.ascontiguousarray(
        v.reshape(nk, 128, c).transpose(1, 0, 2).reshape(128, nk * c))


_PACK_CACHE = {}


def _pack_ffn(inputs):
    key = "ffn"
    if key in _PACK_CACHE:
        return _PACK_CACHE[key]
    mkTh = np.asarray(inputs["L0_ffn_rail_maker"]).transpose(0, 1, 3, 2)
    ruTh = np.asarray(inputs["L0_ffn_rail_up_proj"]).transpose(0, 1, 3, 2)
    mtfp = np.zeros((128, 64 * 128), np.float32)
    rufp = np.zeros((128, 64 * 128), np.float32)
    for hd in range(4):
        for m in range(32):
            slot = 2 * (hd * 8 + m // 4) + (m % 2)
            off = slot * 128
            pbase = 64 * ((m // 2) % 2)
            for c in range(2):
                rail = 4 * (m // 2) + 2 * (m % 2) + c
                rr = 2 * (m % 2) + c
                r0 = pbase + rr * 16
                mtfp[r0:r0 + 16, off + r0:off + r0 + 16] = mkTh[hd, rail]
                rufp[r0:r0 + 16,
                     off + c * 64:off + (c + 1) * 64] = ruTh[hd, rail]
    _PACK_CACHE[key] = (np.ascontiguousarray(mtfp),
                        np.ascontiguousarray(rufp))
    return _PACK_CACHE[key]


def prep_inputs(inputs):
    x = np.asarray(inputs["L0_attn_a_x"])
    base = {
        "wi": np.asarray(inputs["L0_attn_a_Wi"]),
        "wf": np.asarray(inputs["L0_attn_a_Wf"]),
        "wg": np.asarray(inputs["L0_attn_a_Wg"]),
        "wo": np.asarray(inputs["L0_attn_a_Wo"]),
        "cvi": _vec(inputs["L0_attn_a_conv_i"], NT),
        "cvf": _vec(inputs["L0_attn_a_conv_f"], NT),
        "gnw": _vec(inputs["L0_attn_a_g_norm_weight"], NT),
        "lng": _vec(inputs["L0_attn_b_ln_gamma"], NT),
        "lnb": _vec(inputs["L0_attn_b_ln_beta"], NT),
        "winT": np.ascontiguousarray(
            np.asarray(inputs["L0_attn_b_in_proj_w"]).T),
        "cw": _vec(inputs["L0_attn_b_conv_w"], NTI),
        "cb": _vec(inputs["L0_attn_b_conv_b"], NTI),
        "xpT": np.ascontiguousarray(
            np.asarray(inputs["L0_attn_b_x_proj_w"]).T),
        "dtpT": np.ascontiguousarray(
            np.asarray(inputs["L0_attn_b_dt_proj_w"]).T),
        "dtb": _vec(inputs["L0_attn_b_dt_proj_b"], NTI),
        "alog": _vec(inputs["L0_attn_b_A_log"], NTI),
        "dd": _vec(inputs["L0_attn_b_D"], NTI),
        "woutT": np.ascontiguousarray(
            np.asarray(inputs["L0_attn_b_out_proj_w"]).T),
        "mk": np.asarray(inputs["L0_ffn_rail_maker"]),
        "mkT": np.ascontiguousarray(
            np.asarray(inputs["L0_ffn_rail_maker"]).transpose(0, 1, 3, 2)),
        "mtfp": _pack_ffn(inputs)[0],
        "rufp": _pack_ffn(inputs)[1],
        "ruT": np.ascontiguousarray(
            np.asarray(inputs["L0_ffn_rail_up_proj"]).transpose(0, 1, 3, 2)),
        "wcT": np.ascontiguousarray(np.asarray(inputs["L0_ffn_w_cproj"]).T),
    }
    in_maps = []
    for b in range(8):
        m = dict(base)
        m["xf"] = np.ascontiguousarray(x[b].T)
        in_maps.append(m)
    return in_maps


def kernel(**inputs):
    in_maps = prep_inputs(inputs)
    nc = build_program()
    res = run_bass_kernel_spmd(nc, in_maps, core_ids=list(range(8)))
    return np.stack([r["y"] for r in res.results], axis=0)
